# version 10
# baseline (speedup 1.0000x reference)
import numpy as np
import ml_dtypes
import concourse.bass as bass
import concourse.mybir as mybir
from concourse.bass_utils import run_bass_kernel_spmd

F32 = mybir.dt.float32
BF16 = mybir.dt.bfloat16
I32 = mybir.dt.int32
AF = mybir.ActivationFunctionType
ALU = mybir.AluOpType
AXX = mybir.AxisListType.X

B, T, C = 2, 4096, 4096
H = 32
NCH = 32
MYS = 32
NSLOT = 64
NOWN = 8
EPS = 1e-5
TAU_INV = 10.0
THRESH = 0.01
INV_SQRT_HD = float(1.0 / np.sqrt(128.0))
INV_64 = float(1.0 / 64.0)
INV_C = float(1.0 / 4096.0)
G2 = 11

ENGINES = ("sync", "tensor", "vector", "scalar", "gpsimd")


class _CountStub:

    def __init__(self):
        self.n = 0

    def __getattr__(self, name):
        def f(*a, **kw):
            self.n += 1

            class _D:
                def then_inc(self, *a2, **k2):
                    return self

                def __getitem__(self, *a2):
                    return self
            return _D()
        return f


CHAINED = ("vector", "scalar")


class Plan:
    def __init__(self):
        self.items = {e: [] for e in ENGINES}
        self.counts = {}
        self.chain = {e: 0 for e in CHAINED}
        self.hist = {}

    def now(self, s):
        return self.counts.get(s, 0)

    def _xlate(self, waits):
        out = []
        for s, v in waits:
            if v <= 0:
                continue
            if s in self.hist:
                eng, marks = self.hist[s]
                out.append(("__chain_" + eng, marks[min(v, len(marks)) - 1]))
            else:
                out.append((s, v))
        return out

    def emit(self, eng, fn, waits=(), inc=None):
        waits = self._xlate(waits)
        if eng in CHAINED and fn is not None:
            is_dma = inc is not None and inc[1] % 16 == 0
            if not is_dma:
                stub = _CountStub()
                fn(stub)
                assert stub.n > 0, "chained emit with no ops"
                self.chain[eng] += stub.n
                if inc is not None:
                    name = inc[0]
                    self.counts[name] = self.counts.get(name, 0) + 1
                    self.hist.setdefault(name, (eng, []))
                    h = self.hist[name]
                    assert h[0] == eng, f"sem {name} produced by two engines"
                    h[1].append(self.chain[eng])
                self.items[eng].append((fn, waits, None))
                return
        if inc is not None and inc[1] % 16 == 0 and inc[1] > 0:
            prior = self.counts.get(inc[0], 0)
            if prior > 0:
                waits.append((inc[0], prior))
        self.items[eng].append((fn, waits, inc))
        if inc is not None:
            self.counts[inc[0]] = self.counts.get(inc[0], 0) + inc[1]

    def barrier(self, name="bar"):
        base = self.now(name)
        marks = dict(self.chain)
        for e in ENGINES:
            self.emit(e, None, inc=(name, 1))
            self.counts[name] = self.counts.get(name, 0)
        for e in ENGINES:
            w = [(name, base + 5)]
            for ce in CHAINED:
                if marks[ce] > 0 and ce != e:
                    w.append(("__chain_" + ce, marks[ce]))
            self.items[e].append((None, w, None))


def build_program(upto=99):
    import os as _os
    dbg = bool(_os.environ.get("KMK_DEBUG"))
    nc = bass.Bass("TRN2", target_bir_lowering=False)

    xp = nc.dram_tensor("xp", [NSLOT * 128, C], F32, kind="ExternalInput").ap()
    w_attn = nc.dram_tensor("w_attn", [3 * C, C], F32, kind="ExternalInput").ap()
    w_proj = nc.dram_tensor("w_proj", [C, C], F32, kind="ExternalInput").ap()
    w_ffn = nc.dram_tensor("w_ffn", [C, C], F32, kind="ExternalInput").ap()
    ffn_bias = nc.dram_tensor("ffn_bias", [1, C], F32, kind="ExternalInput").ap()
    cents = nc.dram_tensor("cents", [64, C], F32, kind="ExternalInput").ap()
    asg = nc.dram_tensor("asg", [1, C], I32, kind="ExternalInput").ap()
    adapter = nc.dram_tensor("adapter", [10, C], F32, kind="ExternalInput").ap()
    gating = nc.dram_tensor("gating", [1, 1], F32, kind="ExternalInput").ap()
    cos_tab = nc.dram_tensor("cos_tab", [MYS * 128, 64], F32, kind="ExternalInput").ap()
    sin_tab = nc.dram_tensor("sin_tab", [MYS * 128, 64], F32, kind="ExternalInput").ap()
    amask = nc.dram_tensor("amask", [MYS * 128, 128], BF16, kind="ExternalInput").ap()
    ident_bf = nc.dram_tensor("ident_bf", [128, 128], BF16, kind="ExternalInput").ap()
    ident_f = nc.dram_tensor("ident_f", [128, 128], F32, kind="ExternalInput").ap()
    iota64 = nc.dram_tensor("iota64", [64, 1], F32, kind="ExternalInput").ap()
    onescol_b = nc.dram_tensor("onescol_b", [128, 1], BF16, kind="ExternalInput").ap()
    onescol_f = nc.dram_tensor("onescol_f", [128, 1], F32, kind="ExternalInput").ap()
    onesrow_f = nc.dram_tensor("onesrow_f", [1, 128], F32, kind="ExternalInput").ap()

    out = nc.dram_tensor("out", [NOWN * 128, C], F32, kind="ExternalOutput").ap()
    if dbg:
        dbg_rstd = nc.dram_tensor("dbg_rstd", [128, NSLOT], F32, kind="ExternalOutput").ap()
        dbg_qsel = nc.dram_tensor("dbg_qsel", [128, NSLOT], F32, kind="ExternalOutput").ap()
        dbg_mrow = nc.dram_tensor("dbg_mrow", [1, MYS * 128], BF16, kind="ExternalOutput").ap()
        dbg_sqv = nc.dram_tensor("dbg_sqv", [1, MYS * 128], BF16, kind="ExternalOutput").ap()
        dbg_rm = nc.dram_tensor("dbg_rm", [128, C], BF16, kind="ExternalOutput").ap()
        dbg_xfm = nc.dram_tensor("dbg_xfm", [128, C], BF16, kind="ExternalOutput").ap()
        dbg_k = nc.dram_tensor("dbg_k", [MYS * 128, C], BF16, kind="ExternalOutput").ap()
        dbg_v = nc.dram_tensor("dbg_v", [MYS * 128, C], BF16, kind="ExternalOutput").ap()
        dbg_q = nc.dram_tensor("dbg_q", [NOWN * 128, C], BF16, kind="ExternalOutput").ap()
        dbg_g = nc.dram_tensor("dbg_g", [NOWN * 128, C], BF16, kind="ExternalOutput").ap()
        dbg_y = nc.dram_tensor("dbg_y", [NOWN * 128, C], BF16, kind="ExternalOutput").ap()
        dbg_ak = nc.dram_tensor("dbg_ak", [128, H * 10], BF16, kind="ExternalOutput").ap()
        dbg_av = nc.dram_tensor("dbg_av", [10, H * 128], BF16, kind="ExternalOutput").ap()
        dbg_srow = nc.dram_tensor("dbg_srow", [128, 64], F32, kind="ExternalOutput").ap()
        dbg_xh = nc.dram_tensor("dbg_xh", [256, C], BF16, kind="ExternalOutput").ap()
        dbg_xl = nc.dram_tensor("dbg_xl", [256, C], BF16, kind="ExternalOutput").ap()

    wattn_bf = nc.dram_tensor("wattn_bf", [3 * C, C], BF16, kind="Internal").ap()
    wproj_bf = nc.dram_tensor("wproj_bf", [C, C], BF16, kind="Internal").ap()
    wffn_bf = nc.dram_tensor("wffn_bf", [C, C], BF16, kind="Internal").ap()
    adapt_bf = nc.dram_tensor("adapt_bf", [10, C], BF16, kind="Internal").ap()
    bias_d = nc.dram_tensor("bias_d", [1, C], BF16, kind="Internal").ap()
    xhat_d = nc.dram_tensor("xhat_d", [NSLOT * 128, C], BF16, kind="Internal").ap()
    xlo_d = nc.dram_tensor("xlo_d", [NSLOT * 128, C], BF16, kind="Internal").ap()
    xfm_d = nc.dram_tensor("xfm_d", [NSLOT, 128, C], BF16, kind="Internal").ap()
    wbar_d = nc.dram_tensor("wbar_d", [16, 512], BF16, kind="Internal").ap()
    mrow_d = nc.dram_tensor("mrow_d", [1, MYS * 128], BF16, kind="Internal").ap()
    sqv_d = nc.dram_tensor("sqv_d", [1, MYS * 128], BF16, kind="Internal").ap()
    rm_d = nc.dram_tensor("rm_d", [128, C], BF16, kind="Internal").ap()
    k_d = nc.dram_tensor("k_d", [MYS * 128, C], BF16, kind="Internal").ap()
    v_d = nc.dram_tensor("v_d", [MYS * 128, C], BF16, kind="Internal").ap()
    q_d = nc.dram_tensor("q_d", [NOWN * 128, C], BF16, kind="Internal").ap()
    g_d = nc.dram_tensor("g_d", [NOWN * 128, C], BF16, kind="Internal").ap()
    y_d = nc.dram_tensor("y_d", [NOWN * 128, C], BF16, kind="Internal").ap()

    p = Plan()

    from contextlib import ExitStack
    _stack = ExitStack()
    idb = _stack.enter_context(nc.sbuf_tensor("idb", [128, 128], BF16))
    idf = _stack.enter_context(nc.sbuf_tensor("idf", [128, 128], F32))
    iota_sb = _stack.enter_context(nc.sbuf_tensor("iota_sb", [64, 1], F32))
    o_cb = _stack.enter_context(nc.sbuf_tensor("o_cb", [128, 1], BF16))
    o_cf = _stack.enter_context(nc.sbuf_tensor("o_cf", [128, 1], F32))
    o_rf = _stack.enter_context(nc.sbuf_tensor("o_rf", [1, 128], F32))
    gat1 = _stack.enter_context(nc.sbuf_tensor("gat1", [1, 1], F32))
    gat_bc = _stack.enter_context(nc.sbuf_tensor("gat_bc", [128, 1], F32))
    srow_bc = _stack.enter_context(nc.sbuf_tensor("srow_bc", [128, 64], F32))
    rstd_all = _stack.enter_context(nc.sbuf_tensor("rstd_all", [128, NSLOT], F32))
    qsel_all = _stack.enter_context(nc.sbuf_tensor("qsel_all", [128, NSLOT], F32))
    adapt_fm = _stack.enter_context(nc.sbuf_tensor("adapt_fm", [128, NCH * 10], BF16))
    ak_all = _stack.enter_context(nc.sbuf_tensor("ak_all", [128, H * 10], BF16))
    av_all = _stack.enter_context(nc.sbuf_tensor("av_all", [10, H * 128], BF16))
    stat = _stack.enter_context(nc.sbuf_tensor("stat", [128, 16], F32))
    rowsc = _stack.enter_context(nc.sbuf_tensor("rowsc", [1, 256], F32))
    mrow_pc = _stack.enter_context(nc.sbuf_tensor("mrow_pc", [1, 256], F32))
    mrow_sw = _stack.enter_context(nc.sbuf_tensor("mrow_sw", [1, G2 * 128 + 128], BF16))
    sqv_sw = _stack.enter_context(nc.sbuf_tensor("sqv_sw", [1, G2 * 128 + 128], BF16))
    bias_blk = _stack.enter_context(nc.sbuf_tensor("bias_blk", [1, 512], BF16))
    rm_blk = _stack.enter_context(nc.sbuf_tensor("rm_blk", [128, 512], BF16))
    nwbar = _stack.enter_context(nc.sbuf_tensor("nwbar", [1, 512], BF16))
    csel = _stack.enter_context(nc.sbuf_tensor("csel", [64, 1], F32))
    adsc = _stack.enter_context(nc.sbuf_tensor("adsc", [128, 40], BF16))
    asg_ic = _stack.enter_context(nc.sbuf_tensor("asg_ic", [1, 512], I32))
    arena_bf = _stack.enter_context(nc.sbuf_tensor("arena_bf", [128, 51712], BF16))
    arena_w = _stack.enter_context(nc.sbuf_tensor("arena_w", [128, 32768], BF16))
    arena_f = _stack.enter_context(nc.sbuf_tensor("arena_f", [128, 4096], F32))
    ps0 = _stack.enter_context(nc.psum_tensor("ps0", [128, 512], F32))
    ps1 = _stack.enter_context(nc.psum_tensor("ps1", [128, 512], F32))
    ps2 = _stack.enter_context(nc.psum_tensor("ps2", [128, 512], F32))
    ps3 = _stack.enter_context(nc.psum_tensor("ps3", [128, 512], F32))
    ps4 = _stack.enter_context(nc.psum_tensor("ps4", [128, 512], F32))
    ps5 = _stack.enter_context(nc.psum_tensor("ps5", [128, 512], F32))
    ps6 = _stack.enter_context(nc.psum_tensor("ps6", [128, 512], F32))
    ps7 = _stack.enter_context(nc.psum_tensor("ps7", [128, 1024], BF16))
    block = _stack.enter_context(nc.Block())
    if True:
        centhi = arena_bf[:, 36864:38976]
        centlo = arena_bf[:, 38976:41088]
        ohc = arena_f[0:64, 2818:3330]
        asg_fc = arena_f[0:1, 3330:3842]

        def ph0():
            loads = [(idb[:], ident_bf), (idf[:], ident_f), (iota_sb[:], iota64),
                     (o_cb[:], onescol_b), (o_cf[:], onescol_f), (o_rf[:], onesrow_f),
                     (gat1[:], gating)]
            for dst, src in loads:
                p.emit("sync", (lambda d=dst, s_=src: lambda e: e.dma_start(out=d, in_=s_))(),
                       inc=("c_ld", 16))
            nl = 16 * len(loads)
            p.emit("gpsimd", lambda e: e.dma_start(out=adapt_bf, in_=adapter), inc=("wb_ad", 16))
            p.emit("gpsimd", lambda e: e.dma_start(out=bias_d, in_=ffn_bias), inc=("wb_b", 16))
            cent_nat = arena_f[0:64, 0:4096]
            p.emit("sync", lambda e: e.dma_start(out=cent_nat, in_=cents), inc=("c_ld", 16))
            ad_nat = arena_bf[0:10, 0:4096]
            p.emit("sync", lambda e: e.dma_start(out=ad_nat, in_=adapt_bf),
                   waits=[("wb_ad", 16)], inc=("c_ld", 16))

            p.emit("tensor", lambda e: e.matmul(out=ps6[:, 0:1], lhsT=o_rf[:], rhs=gat1[:],
                                                start=True, stop=True),
                   waits=[("c_ld", nl)], inc=("s_t", 1))
            p.emit("vector", lambda e: e.tensor_copy(out=gat_bc[:], in_=ps6[:, 0:1]),
                   waits=[("s_t", p.now("s_t"))], inc=("s_v", 1))
            gv = p.now("s_v")

            p.emit("vector", lambda e: e.reduce_sum(out=csel[:], in_=cent_nat, axis=AXX),
                   waits=[("c_ld", nl + 16)], inc=("s_v", 1))
            p.emit("tensor", lambda e: e.transpose(out=ps5[0:1, 0:64], in_=csel[:],
                                                   identity=idf[0:64, 0:64]),
                   waits=[("s_v", p.now("s_v"))], inc=("s_t", 1))
            p.emit("vector", lambda e: e.tensor_copy(out=rowsc[:, 0:64], in_=ps5[0:1, 0:64]),
                   waits=[("s_t", p.now("s_t"))], inc=("s_v", 1))
            p.emit("tensor", lambda e: e.matmul(out=ps6[:, 0:64], lhsT=o_rf[:],
                                                rhs=rowsc[0:1, 0:64], start=True, stop=True),
                   waits=[("s_v", p.now("s_v"))], inc=("s_t", 1))
            p.emit("vector", lambda e: e.tensor_copy(out=srow_bc[:], in_=ps6[:, 0:64]),
                   waits=[("s_t", p.now("s_t"))], inc=("s_v", 1))

            for b4 in range(8):
                bank = ps4 if b4 % 2 == 0 else ps5
                p.emit("tensor", (lambda b=b4, bk=bank: lambda e: [
                    e.transpose(out=bk[:, jj * 64:jj * 64 + 64],
                                in_=cent_nat[0:64, (b * 4 + jj) * 128:(b * 4 + jj + 1) * 128],
                                identity=idf[0:64, 0:64]) for jj in range(4)][-1])(),
                    waits=[("c_ld", nl + 16), ("c_ev", max(0, b4 - 1)),
                           ("s_v", p.now("s_v"))],
                    inc=("c_tr", 1))

                def csplit(e, b=b4, bk=bank):
                    r = None
                    for jj in range(4):
                        j = b * 4 + jj
                        e.tensor_copy(out=centhi[:, j * 66:j * 66 + 64],
                                      in_=bk[:, jj * 64:jj * 64 + 64])
                        r = e.scalar_tensor_tensor(out=centlo[:, j * 66:j * 66 + 64],
                                                   in0=centhi[:, j * 66:j * 66 + 64],
                                                   scalar=-1.0,
                                                   in1=bk[:, jj * 64:jj * 64 + 64],
                                                   op0=ALU.mult, op1=ALU.add)
                    return r
                p.emit("vector", csplit, waits=[("c_tr", p.now("c_tr"))], inc=("c_ev", 1))
            p.emit("vector", (lambda: lambda e: [
                e.memset(centhi.rearrange("p (j w) -> p j w", w=66)[:, :, 64:66], 0.0),
                e.memset(centlo.rearrange("p (j w) -> p j w", w=66)[:, :, 64:66], 0.0),
                e.memset(centhi.rearrange("p (j w) -> p j w", w=66)[:, :, 64:65], 1.0)][-1])(),
                waits=[("c_ev", 8)], inc=("s_v", 1))

            for b4 in range(8):
                p.emit("tensor", (lambda b=b4: lambda e: [
                    e.transpose(out=ps7[:, jj * 10:jj * 10 + 10],
                                in_=ad_nat[0:10, (b * 4 + jj) * 128:(b * 4 + jj + 1) * 128],
                                identity=idb[0:10, 0:10]) for jj in range(4)][-1])(),
                    waits=[("c_ld", nl + 32), ("ad_ev", b4)], inc=("ad_tr", 1))
                p.emit("vector", (lambda b=b4: lambda e: e.tensor_copy(
                    out=adapt_fm[:, b * 40:(b + 1) * 40], in_=ps7[:, 0:40]))(),
                    waits=[("ad_tr", p.now("ad_tr"))], inc=("ad_ev", 1))

        xtm = [arena_f[:, i * 512:(i + 1) * 512] for i in range(4)]
        xhat_t = [arena_bf[:, i * 4096:(i + 1) * 4096] for i in range(4)]
        xlo_t = [arena_bf[:, 16384 + i * 4096:16384 + (i + 1) * 4096] for i in range(4)]

        ph1a_pend = [None]

        def ph1a_flush():
            if ph1a_pend[0] is not None:
                ph1a_pend[0]()
                ph1a_pend[0] = None

        def ph1a_block(tb):
            for s in range(4 * tb, 4 * tb + 4):
                spr = s % 4
                for qq in range(8):
                    idx = s * 8 + qq
                    p.emit("scalar", (lambda s=s, qq=qq, idx=idx: lambda e: e.dma_start(
                        out=xtm[idx % 4][:],
                        in_=xp[s * 128:(s + 1) * 128, qq * 512:(qq + 1) * 512]))(),
                        waits=[("p1cast", max(0, idx - 3))], inc=("p1xld", 16))
                    xldv = p.now("p1xld")
                    p.emit("vector", (lambda spr=spr, qq=qq, idx=idx: lambda e: e.tensor_copy(
                        out=xhat_t[spr][:, qq * 512:(qq + 1) * 512], in_=xtm[idx % 4][:]))(),
                        waits=[("p1xld", xldv), ("p1sp", 32 * max(0, s - 3))])
                    p.emit("vector", (lambda spr=spr, qq=qq, idx=idx: lambda e: e.tensor_tensor(
                        out=xlo_t[spr][:, qq * 512:(qq + 1) * 512], in0=xtm[idx % 4][:],
                        in1=xhat_t[spr][:, qq * 512:(qq + 1) * 512], op=ALU.subtract))(),
                        inc=("p1cast", 1))
                ph1a_flush()
                cv = p.now("p1cast")

                def spill(s=s, spr=spr, cv=cv):
                    p.emit("sync", (lambda s=s, spr=spr, cv=cv: lambda e: e.dma_start(
                        out=xhat_d[s * 128:(s + 1) * 128, :], in_=xhat_t[spr][:]))(),
                        waits=[("p1cast", cv)], inc=("p1sp", 16))
                    p.emit("sync", (lambda s=s, spr=spr: lambda e: e.dma_start(
                        out=xlo_d[s * 128:(s + 1) * 128, :], in_=xlo_t[spr][:]))(),
                        inc=("p1sp", 16))
                ph1a_pend[0] = spill

        thi = [arena_bf[:, 32768 + i * 512:32768 + (i + 1) * 512] for i in range(4)]
        tlo = [arena_bf[:, 34816 + i * 512:34816 + (i + 1) * 512] for i in range(4)]
        zt_sb = arena_f[:, 2048:2560]
        ztm = arena_f[:, 2560:2626]
        diag_sc = arena_f[:, 2626:2754]
        hits = arena_f[:, 2754:2818]

        def ph1b_block(tb):
            if True:
                zbank = ps0 if tb % 2 == 0 else ps5
                dbank = ps1 if tb % 2 == 0 else ps6
                for cc in range(NCH):
                    pr = (tb * NCH + cc) % 4
                    idx = tb * NCH + cc
                    p.emit("sync", (lambda tb=tb, cc=cc, pr=pr: lambda e: e.dma_start(
                        out=thi[pr][:],
                        in_=xhat_d[tb * 512:(tb + 1) * 512, cc * 128:(cc + 1) * 128],
                        transpose=True))(),
                        waits=[("p1mm", max(0, idx - 3)), ("p1fsp", 16 * max(0, idx - 3))] +
                              ([("p1sp", 32 * (4 * tb + 4))] if cc == 0 else []),
                        inc=("p1trd", 16))
                    p.emit("sync", (lambda tb=tb, cc=cc, pr=pr: lambda e: e.dma_start(
                        out=tlo[pr][:],
                        in_=xlo_d[tb * 512:(tb + 1) * 512, cc * 128:(cc + 1) * 128],
                        transpose=True))(),
                        waits=[("p1mm", max(0, idx - 3))] +
                              ([("p1sp", 32 * (4 * tb + 4))] if cc == 0 else []),
                        inc=("p1trd2", 16))
                    trd, trd2 = p.now("p1trd"), p.now("p1trd2")
                    mm_waits = [("p1trd", trd), ("p1trd2", trd2)]
                    if cc == 0:
                        mm_waits += [("p1zev", max(0, tb - 1)), ("p1v1", 4 * max(0, tb - 1))]

                    def mmz(e, cc=cc, pr=pr, zbank=zbank, dbank=dbank):
                        first = (cc == 0)
                        last_ = (cc == NCH - 1)
                        e.matmul(out=zbank[0:66, :], lhsT=centhi[:, cc * 66:(cc + 1) * 66],
                                 rhs=thi[pr][:], start=first, stop=False,
                                 skip_group_check=True)
                        e.matmul(out=zbank[0:66, :], lhsT=centlo[:, cc * 66:(cc + 1) * 66],
                                 rhs=thi[pr][:], start=False, stop=False,
                                 skip_group_check=True)
                        e.matmul(out=zbank[0:66, :], lhsT=centhi[:, cc * 66:(cc + 1) * 66],
                                 rhs=tlo[pr][:], start=False, stop=last_,
                                 skip_group_check=True)
                        r = None
                        for u in range(4):
                            r = e.matmul(out=dbank[:, u * 128:(u + 1) * 128],
                                         lhsT=thi[pr][:, u * 128:(u + 1) * 128],
                                         rhs=thi[pr][:, u * 128:(u + 1) * 128],
                                         start=(first and u == 0), stop=last_,
                                         skip_group_check=True)
                        return r
                    p.emit("tensor", mmz, waits=mm_waits, inc=("p1mm", 1))
                    p.emit("gpsimd", (lambda tb=tb, cc=cc, pr=pr: lambda e: e.dma_start(
                        out=xfm_d[tb * 4:(tb + 1) * 4, :, cc * 128:(cc + 1) * 128]
                            .rearrange("s p t -> p s t"),
                        in_=thi[pr][:].rearrange("p (s t) -> p s t", s=4)))(),
                        waits=[("p1trd", trd)], inc=("p1fsp", 16))

                mmv = p.now("p1mm")
                p.emit("vector", (lambda zbank=zbank: lambda e: e.tensor_copy(
                    out=zt_sb[0:66, :], in_=zbank[0:66, :]))(),
                       waits=[("p1mm", mmv), ("p1ztr", 4 * tb)], inc=("p1zev", 1))
                zev = p.now("p1zev")
                for u in range(4):
                    s = tb * 4 + u
                    p.emit("tensor", (lambda u=u: lambda e: e.transpose(
                        out=ps2[:, 0:66], in_=zt_sb[0:66, u * 128:(u + 1) * 128],
                        identity=idf[0:66, 0:66]))(),
                        waits=[("p1zev", zev), ("p1v1", max(0, s))], inc=("p1ztr", 1))
                    ztr = p.now("p1ztr")
                    prior = [("p1ztr", ztr), ("p1v3", max(0, s)), ("p1cnt", max(0, s)),
                             ("p1rtr", p.now("p1rtr"))]

                    def vst1(e, u=u, dbank=dbank):
                        e.tensor_tensor(out=diag_sc[:], in0=dbank[:, u * 128:(u + 1) * 128],
                                        in1=idf[:], op=ALU.mult)
                        e.reduce_sum(out=stat[:, 0:1], in_=diag_sc[:], axis=AXX)
                        e.tensor_copy(out=ztm[:, 0:66], in_=ps2[:, 0:66])
                        e.tensor_scalar(out=stat[:, 1:2], in0=ztm[:, 64:65], scalar1=INV_C,
                                        scalar2=None, op0=ALU.mult)
                        e.tensor_scalar(out=stat[:, 2:3], in0=stat[:, 0:1], scalar1=INV_C,
                                        scalar2=None, op0=ALU.mult)
                        e.tensor_tensor(out=stat[:, 3:4], in0=stat[:, 1:2], in1=stat[:, 1:2],
                                        op=ALU.mult)
                        e.tensor_tensor(out=stat[:, 4:5], in0=stat[:, 2:3], in1=stat[:, 3:4],
                                        op=ALU.subtract)
                        return e.tensor_scalar(out=stat[:, 5:6], in0=stat[:, 4:5], scalar1=EPS,
                                               scalar2=None, op0=ALU.add)
                    p.emit("vector", vst1, waits=prior, inc=("p1v1", 1))
                    v1 = p.now("p1v1")
                    p.emit("scalar", (lambda: lambda e: [
                        e.activation(out=stat[:, 13:14], in_=stat[:, 5:6], func=AF.Ln),
                        e.activation(out=stat[:, 6:7], in_=stat[:, 13:14], func=AF.Exp,
                                     scale=0.5)][-1])(),
                        waits=[("p1v1", v1), ("p1rtr", p.now("p1rtr"))], inc=("p1a1", 1))
                    a1 = p.now("p1a1")

                    def vst2(e, s=s):
                        e.reciprocal(out=rstd_all[:, s:s + 1], in_=stat[:, 6:7])
                        e.tensor_scalar(out=stat[:, 7:8], in0=stat[:, 1:2], scalar1=-1.0,
                                        scalar2=None, op0=ALU.mult)
                        e.scalar_tensor_tensor(out=ztm[:, 0:64], in0=srow_bc[:],
                                               scalar=stat[:, 7:8], in1=ztm[:, 0:64],
                                               op0=ALU.mult, op1=ALU.add)
                        e.reduce_max(out=stat[:, 8:9], in_=ztm[:, 0:64], axis=AXX, negate=True)
                        e.tensor_scalar(out=stat[:, 9:10], in0=rstd_all[:, s:s + 1],
                                        scalar1=TAU_INV, scalar2=None, op0=ALU.mult)
                        return e.tensor_tensor(out=stat[:, 10:11], in0=stat[:, 8:9],
                                               in1=stat[:, 9:10], op=ALU.mult)
                    p.emit("vector", vst2, waits=[("p1a1", a1)], inc=("p1v2", 1))
                    v2 = p.now("p1v2")
                    p.emit("scalar", lambda e: e.activation(
                        out=diag_sc[:, 0:64], in_=ztm[:, 0:64], func=AF.Exp,
                        bias=stat[:, 10:11], scale=stat[:, 9:10]),
                        waits=[("p1v2", v2)], inc=("p1a2", 1))
                    a2 = p.now("p1a2")

                    def vst3(e, s=s):
                        e.reduce_sum(out=stat[:, 11:12], in_=diag_sc[:, 0:64], axis=AXX)
                        e.tensor_scalar(out=stat[:, 12:13], in0=stat[:, 11:12], scalar1=THRESH,
                                        scalar2=None, op0=ALU.mult)
                        e.tensor_scalar(out=hits[:], in0=diag_sc[:, 0:64],
                                        scalar1=stat[:, 12:13], scalar2=None, op0=ALU.is_gt)
                        return e.reduce_max(out=qsel_all[:, s:s + 1], in_=hits[:], axis=AXX)
                    p.emit("vector", vst3, waits=[("p1a2", a2), ("p1cnt", max(0, s))],
                           inc=("p1v3", 1))
                    v3 = p.now("p1v3")
                    p.emit("tensor", (lambda s=s: lambda e: e.matmul(
                        out=ps3[0:1, 0:64], lhsT=o_cf[:], rhs=hits[:],
                        start=(s == 0), stop=(s == NSLOT - 1), skip_group_check=True))(),
                        waits=[("p1v3", v3)], inc=("p1cnt", 1))
                    if s < MYS:
                        p.emit("tensor", (lambda: lambda e: [
                            e.transpose(out=ps4[0:1, 0:128], in_=stat[:, 1:2], identity=idf[:]),
                            e.transpose(out=ps4[0:1, 128:256], in_=stat[:, 6:7],
                                        identity=idf[:])][-1])(),
                            waits=[("p1v2", v2), ("p1v4", p.now("p1rtr"))], inc=("p1rtr", 1))
                        rr = p.now("p1rtr")
                        p.emit("vector", (lambda: lambda e: [
                            e.tensor_copy(out=mrow_pc[:, 0:128], in_=ps4[0:1, 0:128]),
                            e.tensor_copy(out=mrow_pc[:, 128:256], in_=ps4[0:1, 128:256]),
                        ][-1])(), waits=[("p1rtr", rr), ("p1rsp", 32 * s)],
                            inc=("p1v4", 1))
                        v4 = p.now("p1v4")
                        p.emit("gpsimd", (lambda s=s: lambda e: e.dma_start(
                            out=mrow_d[:, s * 128:(s + 1) * 128], in_=mrow_pc[:, 0:128]))(),
                            waits=[("p1v4", v4)], inc=("p1rsp", 16))
                        p.emit("gpsimd", (lambda s=s: lambda e: e.dma_start(
                            out=sqv_d[:, s * 128:(s + 1) * 128], in_=mrow_pc[:, 128:256]))(),
                            inc=("p1rsp", 16))

        def ph1c():
            p.emit("vector", lambda e: e.tensor_copy(out=rowsc[:, 64:128], in_=ps3[0:1, 0:64]),
                   waits=[("p1cnt", NSLOT)], inc=("c1v", 1))
            p.emit("tensor", lambda e: e.transpose(out=ps4[0:64, 0:1],
                                                   in_=rowsc[0:1, 64:128][:, 0:64],
                                                   identity=idf[0:1, 0:1]),
                   waits=[("c1v", p.now("c1v")), ("p1v4", p.now("p1v4"))], inc=("c1t", 1))
            p.emit("vector", (lambda: lambda e: [
                e.tensor_copy(out=csel[:], in_=ps4[0:64, 0:1]),
                e.tensor_scalar(out=csel[:], in0=csel[:], scalar1=0.5, scalar2=None,
                                op0=ALU.is_gt)][-1])(),
                waits=[("c1t", p.now("c1t"))], inc=("c1v", 1))
            for j in range(8):
                p.emit("sync", (lambda j=j: lambda e: e.dma_start(
                    out=asg_ic[:], in_=asg[:, j * 512:(j + 1) * 512]))(),
                    waits=[("c1oh", 2 * j)], inc=("c1al", 16))
                p.emit("vector", (lambda: lambda e: e.tensor_copy(
                    out=asg_fc[:], in_=asg_ic[:]))(),
                    waits=[("c1al", p.now("c1al")), ("c1mm", p.now("c1mm"))], inc=("c1v", 1))
                p.emit("tensor", (lambda: lambda e: e.matmul(
                    out=ps5[0:64, :], lhsT=o_rf[:, 0:64], rhs=asg_fc[:],
                    start=True, stop=True))(),
                    waits=[("c1v", p.now("c1v")), ("c1oh", 2 * j), ("p1zev", 16)],
                    inc=("c1mm", 1))
                p.emit("vector", (lambda: lambda e: [
                    e.tensor_copy(out=ohc[:], in_=ps5[0:64, :]),
                    e.tensor_scalar(out=ohc[:], in0=ohc[:], scalar1=iota_sb[:, 0:1],
                                    scalar2=None, op0=ALU.is_equal)][-1])(),
                    waits=[("c1mm", p.now("c1mm"))], inc=("c1oh", 1))
                p.emit("tensor", (lambda: lambda e: e.matmul(
                    out=ps5[0:1, :], lhsT=csel[:], rhs=ohc[:], start=True, stop=True))(),
                    waits=[("c1oh", p.now("c1oh")), ("c1v", p.now("c1v"))], inc=("c1mm", 1))
                p.emit("vector", (lambda: lambda e: e.tensor_copy(
                    out=ohc[0:1, :], in_=ps5[0:1, :]))(),
                    waits=[("c1mm", p.now("c1mm"))], inc=("c1v", 1))
                p.emit("tensor", (lambda: lambda e: e.matmul(
                    out=ps6[:, :], lhsT=o_rf[:], rhs=ohc[0:1, :], start=True, stop=True))(),
                    waits=[("c1v", p.now("c1v")), ("p1v1", 4 * NSLOT // 4)], inc=("c1mm", 1))
                p.emit("vector", (lambda: lambda e: e.tensor_copy(
                    out=rm_blk[:], in_=ps6[:, :]))(),
                    waits=[("c1mm", p.now("c1mm")), ("c1rsp", 16 * j)], inc=("c1oh", 1))
                p.emit("gpsimd", (lambda j=j: lambda e: e.dma_start(
                    out=rm_d[:, j * 512:(j + 1) * 512], in_=rm_blk[:]))(),
                    waits=[("c1oh", p.now("c1oh"))], inc=("c1rsp", 16))

        xg = [arena_bf[:, i * 4096:(i + 1) * 4096] for i in range(G2)]
        ev_st = [arena_bf[:, 45056 + i * 512:45056 + (i + 1) * 512] for i in range(2)]
        wblk = [arena_w[:, 0:16384], arena_w[:, 16384:32768]]
        rp_t1 = arena_f[:, 0:256]
        rp_t2 = arena_f[:, 256:512]
        costl = arena_f[:, 512:576]
        sintl = arena_f[:, 576:640]
        cosP = [arena_bf[:, 46080 + i * 256:46080 + (i + 1) * 256] for i in range(G2)]
        sinP = [arena_bf[:, 48896 + i * 256:48896 + (i + 1) * 256] for i in range(G2)]

        state = {"evc": 0, "wlast": {0: 0, 1: 0}, "adlast": {0: 0, 1: 0}}

        CAST_CHUNKS = ([(wattn_bf, w_attn, 4096 + i * 512) for i in range(16)] +
                       [(wattn_bf, w_attn, i * 512) for i in range(8)] +
                       [(wffn_bf, w_ffn, i * 512) for i in range(8)] +
                       [(wproj_bf, w_proj, i * 512) for i in range(8)])

        def emit_cast_chunks(lo, hi):
            for ci in range(lo, min(hi, len(CAST_CHUNKS))):
                dst, srcw, r0 = CAST_CHUNKS[ci]
                p.emit("gpsimd", (lambda dst=dst, srcw=srcw, r0=r0: lambda e: e.dma_start(
                    out=dst[r0:r0 + 512, :], in_=srcw[r0:r0 + 512, :]))(),
                    inc=("wbc", 16))

        def w_src(kind, blk):
            if kind == "kv":
                return wattn_bf, 4096 + blk * 512, 16 * (blk + 1)
            if kind == "q":
                return wattn_bf, blk * 512, 16 * (17 + blk)
            if kind == "ffn":
                return wffn_bf, blk * 512, 16 * (25 + blk)
            return wproj_bf, blk * 512, 16 * (33 + blk)

        def load_wblk(kind, blk, par, mm_sem):
            src, ro, wneed = w_src(kind, blk)
            guard = [(mm_sem, state["wlast"][par]), ("p2ad", state["adlast"][par])]
            for cc in range(NCH):
                p.emit("sync", (lambda src=src, ro=ro, cc=cc, par=par: lambda e: e.dma_start(
                    out=wblk[par][:, cc * 512:(cc + 1) * 512],
                    in_=src[ro:ro + 512, cc * 128:(cc + 1) * 128], transpose=True))(),
                    waits=[("wbc", wneed)] + (guard if cc < 1 else []), inc=("w_ls", 16))
            return p.now("w_ls"), p.now("w_lc")

        def wbar_compute(par, wls, wlc, mm_sem, store_blk=None):
            p.emit("tensor", (lambda par=par: lambda e: [
                e.matmul(out=ps2[0:1, :], lhsT=o_cb[:],
                         rhs=wblk[par][:, cc * 512:(cc + 1) * 512],
                         start=(cc == 0), stop=(cc == NCH - 1)) for cc in range(NCH)][-1])(),
                waits=[("w_ls", wls), ("w_lc", wlc), ("p2wbv", p.now("p2wbv"))],
                inc=("p2wb", 1))
            p.emit("vector", lambda e: e.tensor_scalar(out=nwbar[:], in0=ps2[0:1, :],
                                                       scalar1=-1.0, scalar2=None,
                                                       op0=ALU.mult),
                   waits=[("p2wb", p.now("p2wb")), (mm_sem, p.now(mm_sem)),
                          ("p2wbsp", p.now("p2wbsp"))],
                   inc=("p2wbv", 1))
            if store_blk is not None:
                p.emit("gpsimd", (lambda b=store_blk: lambda e: e.dma_start(
                    out=wbar_d[b:b + 1, :], in_=nwbar[:]))(),
                    waits=[("p2wbv", p.now("p2wbv"))], inc=("p2wbsp", 16))

        def wbar_load(blk):
            p.emit("sync", (lambda b=blk: lambda e: e.dma_start(
                out=nwbar[:], in_=wbar_d[b:b + 1, :]))(),
                waits=[("p2wbsp", 16 * 16), ("p2mm", p.now("p2mm"))],
                inc=("p2wbl", 16))

        def slot_prologue(gslot, si):
            p.emit("sync", (lambda s=gslot: lambda e: e.dma_start(
                out=costl[:], in_=cos_tab[s * 128:(s + 1) * 128, :]))(),
                waits=[("p2rope", p.now("p2rope"))], inc=("p2cs", 16))
            p.emit("sync", (lambda s=gslot: lambda e: e.dma_start(
                out=sintl[:], in_=sin_tab[s * 128:(s + 1) * 128, :]))(),
                inc=("p2cs", 16))
            cs = p.now("p2cs")
            def mkrep(e, s=gslot, si=si):
                e.tensor_scalar(out=rp_t1[:, 0:64], in0=costl[:],
                                scalar1=rstd_all[:, s:s + 1], scalar2=None, op0=ALU.mult)
                e.tensor_scalar(out=rp_t2[:, 0:64], in0=sintl[:],
                                scalar1=rstd_all[:, s:s + 1], scalar2=None, op0=ALU.mult)
                for r in range(4):
                    e.tensor_copy(out=cosP[si][:, r * 64:(r + 1) * 64], in_=rp_t1[:, 0:64])
                r_ = None
                for r in range(4):
                    r_ = e.tensor_copy(out=sinP[si][:, r * 64:(r + 1) * 64],
                                       in_=rp_t2[:, 0:64])
                return r_
            p.emit("vector", mkrep, waits=[("p2cs", cs)], inc=("p2rope", 1))

        def slot_mms(kind, blk, par, xg_i, sw_i, psum, extra):
            def mm(e, kind=kind, blk=blk, par=par, xg_i=xg_i, sw_i=sw_i, psum=psum):
                for cc in range(NCH):
                    e.matmul(out=psum[:, :], lhsT=xg[xg_i][:, cc * 128:(cc + 1) * 128],
                             rhs=wblk[par][:, cc * 512:(cc + 1) * 512],
                             start=(cc == 0), stop=False)
                r = e.matmul(out=psum[:, :], lhsT=mrow_sw[:, sw_i * 128:(sw_i + 1) * 128],
                             rhs=nwbar[:], start=False, stop=(kind != "ffn"))
                if kind == "ffn":
                    r = e.matmul(out=psum[:, :],
                                 lhsT=sqv_sw[:, sw_i * 128:(sw_i + 1) * 128],
                                 rhs=bias_blk[:], start=False, stop=True)
                return r
            p.emit("tensor", mm, waits=extra, inc=("p2mm", 1))
            return p.now("p2mm")

        def spill_guard():
            return [("p2sp", 16 * max(0, state["evc"] - 1))]

        def evac_v(blk, gslot, psum, mmv):
            pr = state["evc"] % 2
            p.emit("vector", (lambda gslot=gslot, psum=psum, pr=pr: lambda e: e.tensor_scalar(
                out=ev_st[pr][:], in0=psum[:, :], scalar1=rstd_all[:, gslot:gslot + 1],
                scalar2=None, op0=ALU.mult))(),
                waits=[("p2mm", mmv)] + spill_guard(), inc=("p2ev", 1))
            p.emit("gpsimd", (lambda blk=blk, gslot=gslot, pr=pr: lambda e: e.dma_start(
                out=v_d[gslot * 128:(gslot + 1) * 128, (blk - 8) * 512:(blk - 7) * 512],
                in_=ev_st[pr][:]))(),
                waits=[("p2ev", p.now("p2ev"))], inc=("p2sp", 16))
            state["evc"] += 1

        def evac_rope(kind, blk, row, si, psum, mmv):
            pr = state["evc"] % 2
            dst = k_d if kind == "kv" else q_d

            def ev(e, psum=psum, pr=pr, si=si):
                pv = psum.rearrange("p (h two i) -> p h two i", two=2, i=64)
                ev_ = ev_st[pr].rearrange("p (h two i) -> p h two i", two=2, i=64)
                cp = cosP[si].rearrange("p (h i) -> p h i", i=64)
                sp_ = sinP[si].rearrange("p (h i) -> p h i", i=64)
                t1 = rp_t1.rearrange("p (h i) -> p h i", i=64)
                t2 = rp_t2.rearrange("p (h i) -> p h i", i=64)
                e.tensor_tensor(out=t1, in0=pv[:, :, 0, :], in1=cp, op=ALU.mult)
                e.tensor_tensor(out=t2, in0=pv[:, :, 1, :], in1=sp_, op=ALU.mult)
                e.tensor_tensor(out=ev_[:, :, 0, :], in0=t1, in1=t2, op=ALU.subtract)
                e.tensor_tensor(out=t1, in0=pv[:, :, 1, :], in1=cp, op=ALU.mult)
                e.tensor_tensor(out=t2, in0=pv[:, :, 0, :], in1=sp_, op=ALU.mult)
                return e.tensor_tensor(out=ev_[:, :, 1, :], in0=t1, in1=t2, op=ALU.add)
            p.emit("vector", ev,
                   waits=[("p2mm", mmv), ("p2rope", p.now("p2rope"))] + spill_guard(),
                   inc=("p2ev", 1))
            p.emit("gpsimd", (lambda dst=dst, row=row, blk=blk, pr=pr: lambda e: e.dma_start(
                out=dst[row * 128:(row + 1) * 128, blk * 512:(blk + 1) * 512],
                in_=ev_st[pr][:]))(),
                waits=[("p2ev", p.now("p2ev"))], inc=("p2sp", 16))
            state["evc"] += 1

        def evac_ffn(blk, gslot, jown, psum, mmv):
            pr = state["evc"] % 2
            p.emit("scalar", (lambda gslot=gslot, psum=psum, pr=pr: lambda e: e.activation(
                out=ev_st[pr][:], in_=psum[:, :], func=AF.Silu,
                scale=rstd_all[:, gslot:gslot + 1]))(),
                waits=[("p2mm", mmv)] + spill_guard(), inc=("p2ea", 1))
            p.emit("vector", (lambda gslot=gslot, pr=pr: lambda e: [
                e.tensor_tensor(out=ev_st[pr][:], in0=ev_st[pr][:], in1=rm_blk[:],
                                op=ALU.mult),
                e.tensor_scalar(out=ev_st[pr][:], in0=ev_st[pr][:],
                                scalar1=qsel_all[:, gslot:gslot + 1], scalar2=None,
                                op0=ALU.mult)][-1])(),
                waits=[("p2ea", p.now("p2ea")), ("p2rml", p.now("p2rml"))], inc=("p2ev", 1))
            p.emit("gpsimd", (lambda blk=blk, jown=jown, pr=pr: lambda e: e.dma_start(
                out=g_d[jown * 128:(jown + 1) * 128, blk * 512:(blk + 1) * 512],
                in_=ev_st[pr][:]))(),
                waits=[("p2ev", p.now("p2ev"))], inc=("p2sp", 16))
            state["evc"] += 1

        def adapter_mms(blk, par, wls, wlc):
            def amm(e, par=par):
                r = None
                for u in range(4):
                    for cc in range(NCH):
                        r = e.matmul(out=ps3[:, u * 10:(u + 1) * 10],
                                     lhsT=wblk[par][:, cc * 512 + u * 128:cc * 512 + (u + 1) * 128],
                                     rhs=adapt_fm[:, cc * 10:(cc + 1) * 10],
                                     start=(cc == 0), stop=(cc == NCH - 1),
                                     skip_group_check=True)
                return r
            p.emit("tensor", amm,
                   waits=[("ad_ev", 8), ("p2adv", p.now("p2adv")), ("w_ls", wls),
                          ("w_lc", wlc)],
                   inc=("p2ad", 1))
            state["adlast"][par] = p.now("p2ad")
            if blk < 8:
                p.emit("vector", (lambda blk=blk: lambda e: [
                    e.tensor_copy(out=ak_all[:, (blk * 4 + u) * 10:(blk * 4 + u + 1) * 10],
                                  in_=ps3[:, u * 10:(u + 1) * 10]) for u in range(4)][-1])(),
                    waits=[("p2ad", p.now("p2ad"))], inc=("p2adv", 1))
            else:
                p.emit("vector", lambda e: e.tensor_copy(out=adsc[:], in_=ps3[:, 0:40]),
                       waits=[("p2ad", p.now("p2ad"))], inc=("p2adv", 1))
                p.emit("tensor", (lambda: lambda e: [
                    e.transpose(out=ps7[0:10, u * 128:(u + 1) * 128],
                                in_=adsc[:, u * 10:(u + 1) * 10], identity=idb[:])
                    for u in range(4)][-1])(),
                    waits=[("p2adv", p.now("p2adv"))], inc=("p2ad", 1))
                p.emit("vector", (lambda blk=blk: lambda e: e.tensor_copy(
                    out=av_all[0:10, (blk - 8) * 512:(blk - 7) * 512],
                    in_=ps7[0:10, 0:512]))(),
                    waits=[("p2ad", p.now("p2ad"))], inc=("p2adv", 1))

        def load_sweep_rows(slots):
            runs = []
            st0 = slots[0]
            prev = st0
            for s in slots[1:] + [None]:
                if s is not None and s == prev + 1:
                    prev = s
                    continue
                runs.append((st0, prev))
                if s is not None:
                    st0 = prev = s
            ofs = 0
            for (a, b_) in runs:
                n = (b_ - a + 1) * 128
                p.emit("sync", (lambda a=a, n=n, ofs=ofs: lambda e: e.dma_start(
                    out=mrow_sw[:, ofs:ofs + n], in_=mrow_d[:, a * 128:a * 128 + n]))(),
                    waits=[("p2mm", p.now("p2mm")), ("p1rsp", 32 * MYS)],
                    inc=("p2mr", 16))
                p.emit("sync", (lambda a=a, n=n, ofs=ofs: lambda e: e.dma_start(
                    out=sqv_sw[:, ofs:ofs + n], in_=sqv_d[:, a * 128:a * 128 + n]))(),
                    inc=("p2mr", 16))
                ofs += n
            return p.now("p2mr")

        def ph2():
            sweeps = [list(range(0, G2)), list(range(G2, 2 * G2)),
                      list(range(2 * G2, MYS))]
            for si, slots in enumerate(sweeps):
                for i, s in enumerate(slots):
                    p.emit("sync", (lambda i=i, s=s: lambda e: e.dma_start(
                        out=xg[i][:], in_=xfm_d[s, :, :]))(),
                        waits=[("p2mm", p.now("p2mm")), ("p1fsp", 16 * 512)],
                        inc=("p2xg", 16))
                mrn = load_sweep_rows(slots)
                for i, s in enumerate(slots):
                    slot_prologue(s, i)
                xgn = p.now("p2xg")
                ropen = p.now("p2rope")
                for bi in range(16):
                    par = bi % 2
                    wls, wlc = load_wblk("kv", bi, par, "p2mm")
                    if si == 0:
                        wbar_compute(par, wls, wlc, "p2mm", store_blk=bi)
                        adapter_mms(bi, par, wls, wlc)
                    else:
                        wbar_load(bi)
                    wbv, wbl = p.now("p2wbv"), p.now("p2wbl")
                    for s_i, s in enumerate(slots):
                        psum = ps0 if state["evc"] % 2 == 0 else ps1
                        waits = [("w_ls", wls), ("w_lc", wlc), ("p2xg", xgn),
                                 ("p2wbv", wbv), ("p2wbl", wbl), ("p2rope", ropen),
                                 ("p2mr", mrn),
                                 ("p2ev", max(0, p.now("p2ev") - 1)),
                                 ("p2ea", max(0, p.now("p2ea") - 1))]
                        mmv = slot_mms("kv", bi, par, s_i, s_i, psum, waits)
                        if bi < 8:
                            evac_rope("kv", bi, s, s_i, psum, mmv)
                        else:
                            evac_v(bi, s, psum, mmv)
                    state["wlast"][par] = p.now("p2mm")
            own = [4 * j + 3 for j in range(8)]
            for i, s in enumerate(own):
                p.emit("sync", (lambda i=i, s=s: lambda e: e.dma_start(
                    out=xg[i][:], in_=xfm_d[s, :, :]))(),
                    waits=[("p2mm", p.now("p2mm"))], inc=("p2xg", 16))
            mrn = load_sweep_rows(own)
            for i, s in enumerate(own):
                slot_prologue(s, i)
            xgn = p.now("p2xg")
            ropen = p.now("p2rope")
            for kind in ("q", "ffn"):
                for bi in range(8):
                    par = bi % 2
                    wls, wlc = load_wblk(kind, bi, par, "p2mm")
                    wbar_compute(par, wls, wlc, "p2mm")
                    wbv = p.now("p2wbv")
                    if kind == "ffn":
                        p.emit("sync", (lambda bi=bi: lambda e: e.dma_start(
                            out=bias_blk[:], in_=bias_d[:, bi * 512:(bi + 1) * 512]))(),
                            waits=[("wb_b", 16), ("p2mm", p.now("p2mm"))], inc=("p2bl", 16))
                        p.emit("sync", (lambda bi=bi: lambda e: e.dma_start(
                            out=rm_blk[:], in_=rm_d[:, bi * 512:(bi + 1) * 512]))(),
                            waits=[("c1rsp", 128), ("p2ev", p.now("p2ev"))],
                            inc=("p2rml", 16))
                    bln = p.now("p2bl")
                    rmn = p.now("p2rml")
                    for j in range(8):
                        s = own[j]
                        psum = ps0 if state["evc"] % 2 == 0 else ps1
                        waits = [("w_ls", wls), ("w_lc", wlc), ("p2xg", xgn),
                                 ("p2wbv", wbv), ("p2rope", ropen), ("p2mr", mrn),
                                 ("p2bl", bln), ("p2rml", rmn),
                                 ("p2ev", max(0, p.now("p2ev") - 1)),
                                 ("p2ea", max(0, p.now("p2ea") - 1))]
                        mmv = slot_mms(kind, bi, par, j, j, psum, waits)
                        if kind == "q":
                            evac_rope("q", bi, j, j, psum, mmv)
                        else:
                            evac_ffn(bi, s, j, psum, mmv)
                    state["wlast"][par] = p.now("p2mm")

        SP2_TOTAL = 16 * (16 * MYS + 16 * NOWN)
        kfm = [arena_bf[:, 0:4096], arena_bf[:, 4096:8192]]
        qfm = [arena_bf[:, 8192:9216], arena_bf[:, 9216:10240]]
        vsb = [arena_bf[:, 10240:14368], arena_bf[:, 14368:18496]]
        expS = arena_bf[:, 18496:36928]
        mks = arena_bf[:, 36928:41024]
        y_sb = arena_bf[:, 41216:42240]
        pa_all = arena_bf[:, 42240:42320]
        paT_all = arena_bf[0:10, 42320:43344]
        ev8 = arena_bf[:, 43344:44368]
        easc_in = arena_f[:, 0:80]
        easc = arena_f[:, 128:208]
        fac = arena_f[:, 256:264]
        sofs = []
        off = 0
        for s in range(MYS):
            w = (8 - s // 4) * 128
            sofs.append((off, w))
            off += w

        def ph3():
            p.emit("gpsimd", lambda e: e.dma_start(
                out=mks.rearrange("p (s q) -> p s q", s=32),
                in_=amask.rearrange("(s k) q -> k s q", k=128)), inc=("p3mk", 16))
            for h in range(H):
                pr = h % 2
                guard = [("p3m", MYS * max(0, h - 1)), ("p3av0", 8 * max(0, h - 1)),
                         ("p3as", 8 * max(0, h - 1))]
                p.emit("sync", (lambda h=h, pr=pr: lambda e: e.dma_start(
                    out=kfm[pr][:], in_=k_d[:, h * 128:(h + 1) * 128], transpose=True))(),
                    waits=guard + [("p2sp", SP2_TOTAL)], inc=("p3kl", 16))
                p.emit("sync", (lambda h=h, pr=pr: lambda e: e.dma_start(
                    out=qfm[pr][:], in_=q_d[:, h * 128:(h + 1) * 128], transpose=True))(),
                    inc=("p3kl", 16))
                p.emit("vector", (lambda pr=pr: lambda e: e.memset(
                    vsb[pr].rearrange("p (s w) -> p s w", w=129)[:, :, 128:129], 1.0))(),
                    waits=[("p3av0", 8 * max(0, h - 1))], inc=("p3vm", 1))
                p.emit("gpsimd", (lambda h=h, pr=pr: lambda e: e.dma_start(
                    out=vsb[pr].rearrange("p (s w) -> p s w", w=129)[:, :, 0:128],
                    in_=v_d.rearrange("(s t) c -> t s c", t=128)[:, :, h * 128:(h + 1) * 128]))(),
                    waits=[("p3vm", p.now("p3vm")), ("p2sp", SP2_TOTAL)],
                    inc=("p3vl", 16))
                kln, vln = p.now("p3kl"), p.now("p3vl")
                for s in range(MYS):
                    o0, w = sofs[s]
                    nmm = (w + 511) // 512
                    for m_i in range(nmm):
                        n0 = m_i * 512
                        n1 = min(w, n0 + 512)
                        bank = ps0 if p.now("p3s") % 2 == 0 else ps1
                        qc0 = (s // 4) * 128 + n0
                        p.emit("tensor", (lambda s=s, pr=pr, bank=bank, n0=n0, n1=n1,
                                          qc0=qc0: lambda e: e.matmul(
                                              out=bank[:, 0:n1 - n0],
                                              lhsT=kfm[pr][:, s * 128:(s + 1) * 128],
                                              rhs=qfm[pr][:, qc0:qc0 + (n1 - n0)],
                                              start=True, stop=True))(),
                               waits=[("p3kl", kln), ("p3x", max(0, p.now("p3x") - 1))],
                               inc=("p3s", 1))
                        p.emit("scalar", (lambda bank=bank, o0=o0, n0=n0, n1=n1:
                                          lambda e: e.activation(
                                              out=expS[:, o0 + n0:o0 + n1],
                                              in_=bank[:, 0:n1 - n0], func=AF.Exp,
                                              scale=INV_SQRT_HD))(),
                               waits=[("p3s", p.now("p3s"))], inc=("p3x", 1))
                    p.emit("vector", (lambda s=s, o0=o0: lambda e: e.tensor_tensor(
                        out=expS[:, o0:o0 + 128], in0=expS[:, o0:o0 + 128],
                        in1=mks[:, s * 128:(s + 1) * 128], op=ALU.mult))(),
                        waits=[("p3x", p.now("p3x")), ("p3mk", 16)], inc=("p3m", 1))
                mn = p.now("p3m")
                for j in range(8):
                    ybank = ps2 if j % 2 == 0 else ps4
                    nkt = 4 * j + 4
                    p.emit("tensor", (lambda j=j, pr=pr, ybank=ybank, nkt=nkt: lambda e: [
                        e.matmul(out=ybank[:, 0:129],
                                 lhsT=expS[:, sofs[s][0] + (j - s // 4) * 128:
                                           sofs[s][0] + (j - s // 4) * 128 + 128],
                                 rhs=vsb[pr][:, s * 129:(s + 1) * 129],
                                 start=(s == 0), stop=(s == nkt - 1),
                                 skip_group_check=True)
                        for s in range(nkt)][-1])(),
                        waits=[("p3m", mn), ("p3vl", vln),
                               ("p3ye", max(0, p.now("p3ye") - 1))],
                        inc=("p3av0", 1))
                    ywaits = [("p3av0", p.now("p3av0"))]
                    if j == 0:
                        ywaits.append(("p3ya", max(0, h - 1)))
                    p.emit("vector", (lambda j=j, ybank=ybank: lambda e: [
                        e.reciprocal(out=stat[:, 7:8], in_=ybank[:, 128:129]),
                        e.tensor_scalar(out=y_sb[:, j * 128:(j + 1) * 128],
                                        in0=ybank[:, 0:128],
                                        scalar1=stat[:, 7:8], scalar2=None,
                                        op0=ALU.mult)][-1])(),
                        waits=ywaits, inc=("p3ye", 1))
                for j in range(8):
                    p.emit("tensor", (lambda j=j, pr=pr, h=h: lambda e: e.matmul(
                        out=ps5[:, j * 10:(j + 1) * 10],
                        lhsT=qfm[pr][:, j * 128:(j + 1) * 128],
                        rhs=ak_all[:, h * 10:(h + 1) * 10], start=True, stop=True,
                        skip_group_check=True))(),
                        waits=[("p3ax", p.now("p3ax"))], inc=("p3as", 1))
                p.emit("vector", (lambda: lambda e: ([
                    e.reduce_max(out=stat[:, j:j + 1], in_=ps5[:, j * 10:(j + 1) * 10],
                                 axis=AXX, negate=True) for j in range(8)] + [
                    e.tensor_scalar(out=easc_in[:, j * 10:(j + 1) * 10],
                                    in0=ps5[:, j * 10:(j + 1) * 10],
                                    scalar1=stat[:, j:j + 1], scalar2=INV_64,
                                    op0=ALU.add, op1=ALU.mult) for j in range(8)])[-1])(),
                    waits=[("p3as", p.now("p3as"))], inc=("p3adv", 1))
                p.emit("scalar", (lambda: lambda e: e.activation(
                    out=easc[:], in_=easc_in[:], func=AF.Exp))(),
                    waits=[("p3adv", p.now("p3adv"))], inc=("p3ax", 1))
                p.emit("vector", (lambda: lambda e: ([
                    e.reduce_sum(out=stat[:, 8 + j:9 + j], in_=easc[:, j * 10:(j + 1) * 10],
                                 axis=AXX) for j in range(8)] + [
                    e.reciprocal(out=fac[:], in_=stat[:, 8:16]),
                    e.tensor_scalar(out=fac[:], in0=fac[:], scalar1=gat_bc[:, 0:1],
                                    scalar2=None, op0=ALU.mult)] + [
                    e.tensor_scalar(out=pa_all[:, j * 10:(j + 1) * 10],
                                    in0=easc[:, j * 10:(j + 1) * 10],
                                    scalar1=fac[:, j:j + 1], scalar2=None,
                                    op0=ALU.mult) for j in range(8)])[-1])(),
                    waits=[("p3ax", p.now("p3ax"))], inc=("p3adv", 1))
                p.emit("tensor", (lambda: lambda e: [
                    e.transpose(out=ps7[0:10, j * 128:(j + 1) * 128],
                                in_=pa_all[:, j * 10:(j + 1) * 10], identity=idb[:])
                    for j in range(8)][-1])(),
                    waits=[("p3adv", p.now("p3adv"))], inc=("p3at", 1))
                p.emit("vector", (lambda: lambda e: e.tensor_copy(
                    out=paT_all, in_=ps7[0:10, 0:1024]))(),
                    waits=[("p3at", p.now("p3at"))], inc=("p3adv", 1))
                p.emit("tensor", (lambda h=h: lambda e: [
                    e.matmul(out=(ps6 if j < 4 else ps3)[:, (j % 4) * 128:(j % 4 + 1) * 128],
                             lhsT=paT_all[0:10, j * 128:(j + 1) * 128],
                             rhs=av_all[0:10, h * 128:(h + 1) * 128], start=True, stop=True,
                             skip_group_check=True) for j in range(8)][-1])(),
                    waits=[("p3adv", p.now("p3adv")), ("p3ya", max(0, h - 1))],
                    inc=("p3avm", 1))
                p.emit("vector", (lambda: lambda e: [
                    e.tensor_tensor(out=ev8[:, j * 128:(j + 1) * 128],
                                    in0=y_sb[:, j * 128:(j + 1) * 128],
                                    in1=(ps6 if j < 4 else ps3)[:, (j % 4) * 128:(j % 4 + 1) * 128],
                                    op=ALU.add) for j in range(8)][-1])(),
                    waits=[("p3avm", p.now("p3avm")), ("p3ye", p.now("p3ye")),
                           ("p3ysp", 16 * 8 * h)],
                    inc=("p3ya", 1))
                yan = p.now("p3ya")
                for j in range(8):
                    p.emit("gpsimd", (lambda j=j, h=h: lambda e: e.dma_start(
                        out=y_d[j * 128:(j + 1) * 128, h * 128:(h + 1) * 128],
                        in_=ev8[:, j * 128:(j + 1) * 128]))(),
                        waits=[("p3ya", yan)], inc=("p3ysp", 16))

        yfm = arena_bf[:, 0:32768]
        gsl = [arena_bf[:, 32768 + i * 512:32768 + (i + 1) * 512] for i in range(2)]
        xsl = [arena_f[:, 0:512], arena_f[:, 512:1024]]
        osl = [arena_f[:, 1024:1536], arena_f[:, 1536:2048]]

        def ph4():
            for cc in range(NCH):
                p.emit("sync", (lambda cc=cc: lambda e: e.dma_start(
                    out=yfm[:, cc * 1024:(cc + 1) * 1024],
                    in_=y_d[:, cc * 128:(cc + 1) * 128], transpose=True))(),
                    waits=[("p3ysp", 16 * 256)], inc=("p4ys", 16))
            yl = (p.now("p4ys"), p.now("p4yc"))
            state["wlast"] = {0: 0, 1: 0}
            state["adlast"] = {0: p.now("p2ad"), 1: p.now("p2ad")}
            for bi in range(8):
                par = bi % 2
                wls, wlc = load_wblk("proj", bi, par, "p4mm")
                for j in range(8):
                    psum = ps0 if j % 2 == 0 else ps1
                    pr = j % 2
                    p.emit("sync", (lambda j=j, bi=bi, pr=pr: lambda e: e.dma_start(
                        out=gsl[pr][:],
                        in_=g_d[j * 128:(j + 1) * 128, bi * 512:(bi + 1) * 512]))(),
                        waits=[("p4o", max(0, p.now("p4o") - 1)), ("p2sp", SP2_TOTAL)],
                        inc=("p4g", 16))
                    p.emit("scalar", (lambda j=j, bi=bi, pr=pr: lambda e: e.dma_start(
                        out=xsl[pr][:], in_=xp[(4 * j + 3) * 128:(4 * j + 4) * 128,
                                               bi * 512:(bi + 1) * 512]))(),
                        waits=[("p4o", max(0, p.now("p4o") - 1))], inc=("p4x", 16))
                    gn, xn_ = p.now("p4g"), p.now("p4x")
                    p.emit("tensor", (lambda j=j, par=par, psum=psum: lambda e: [
                        e.matmul(out=psum[:, :],
                                 lhsT=yfm[:, cc * 1024 + j * 128:cc * 1024 + (j + 1) * 128],
                                 rhs=wblk[par][:, cc * 512:(cc + 1) * 512],
                                 start=(cc == 0), stop=(cc == NCH - 1))
                        for cc in range(NCH)][-1])(),
                        waits=[("p4ys", yl[0]), ("p4yc", yl[1]), ("w_ls", wls),
                               ("w_lc", wlc), ("p4o", max(0, p.now("p4o") - 1))],
                        inc=("p4mm", 1))
                    p.emit("vector", (lambda psum=psum, pr=pr: lambda e: [
                        e.tensor_tensor(out=osl[pr][:], in0=psum[:, :], in1=gsl[pr][:],
                                        op=ALU.mult),
                        e.tensor_tensor(out=osl[pr][:], in0=osl[pr][:], in1=xsl[pr][:],
                                        op=ALU.add)][-1])(),
                        waits=[("p4mm", p.now("p4mm")), ("p4g", gn), ("p4x", xn_),
                               ("p4st", 16 * max(0, p.now("p4o") - 1))],
                        inc=("p4o", 1))
                    p.emit("gpsimd", (lambda j=j, bi=bi, pr=pr: lambda e: e.dma_start(
                        out=out[j * 128:(j + 1) * 128, bi * 512:(bi + 1) * 512],
                        in_=osl[pr][:]))(),
                        waits=[("p4o", p.now("p4o"))], inc=("p4st", 16))
                state["wlast"][par] = p.now("p4mm")
            p.emit("gpsimd", None, waits=[("p4st", p.now("p4st"))])

        def phdbg():
            def d(dst, src):
                p.emit("gpsimd", (lambda dst=dst, src=src: lambda e: e.dma_start(
                    out=dst, in_=src))(), inc=("dbgs", 16))
            d(dbg_rstd, rstd_all[:])
            d(dbg_qsel, qsel_all[:])
            d(dbg_mrow, mrow_d)
            d(dbg_sqv, sqv_d)
            d(dbg_rm, rm_d)
            d(dbg_xfm, xfm_d[3, :, :])
            d(dbg_k, k_d)
            d(dbg_v, v_d)
            d(dbg_q, q_d)
            d(dbg_g, g_d)
            d(dbg_y, y_d)
            d(dbg_ak, ak_all[:])
            d(dbg_av, av_all[:])
            d(dbg_srow, srow_bc[:])
            d(dbg_xh, xhat_d[3 * 128:5 * 128, :])
            d(dbg_xl, xlo_d[3 * 128:5 * 128, :])
            p.emit("gpsimd", None, waits=[("dbgs", p.now("dbgs"))])

        def ph1():
            for tb in range(16):
                ph1a_block(tb)
                if tb >= 1:
                    ph1b_block(tb - 1)
                emit_cast_chunks((40 * tb) // 16, (40 * (tb + 1)) // 16)
            ph1a_flush()
            ph1b_block(15)
            emit_cast_chunks(40, 40)
            ph1c()

        phases = [ph0, ph1, ph2, ph3, ph4]
        for i, phf in enumerate(phases):
            if i >= upto:
                break
            phf()
            if i < len(phases) - 1:
                p.barrier()
        if dbg:
            p.barrier()
            phdbg()

        handles = {}
        guards = []
        for name in sorted(p.counts.keys()):
            g = nc.semaphore(name)
            handles[name] = g.__enter__()
            guards.append(g)
        chain_guards = []
        chain_sems = {}
        for en in CHAINED:
            g = nc.semaphore("__chain_" + en)
            chain_sems[en] = g.__enter__()
            chain_guards.append(g)
        handles.update({"__chain_" + en: chain_sems[en] for en in CHAINED})

        _CHAIN_OPS = {"tensor_copy", "tensor_tensor", "tensor_scalar",
                      "scalar_tensor_tensor", "tensor_single_scalar",
                      "tensor_reduce", "reduce_max", "reduce_sum", "reciprocal",
                      "memset", "memzero", "activation", "sqrt", "copy", "mul",
                      "add", "sign", "square", "select", "tensor_add",
                      "tensor_sub", "tensor_mul", "tensor_max"}

        class Chained:
            def __init__(self, eng, sem):
                self._eng = eng
                self._sem = sem
                self._n = 0

            def __getattr__(self, name):
                f = getattr(self._eng, name)
                if name in _CHAIN_OPS:
                    def wrapped(*a, **kw):
                        if self._n > 0:
                            self._eng.wait_ge(self._sem, self._n)
                        ins = f(*a, **kw)
                        ins.then_inc(self._sem, 1)
                        self._n += 1
                        return ins
                    return wrapped
                return f

        try:
            def run_items(eng_name):
                def body(eng):
                    if eng_name in chain_sems:
                        eng_w = Chained(eng, chain_sems[eng_name])
                    else:
                        eng_w = eng
                    for fn, waits, inc in p.items[eng_name]:
                        for s, v in waits:
                            eng.wait_ge(handles[s], v)
                        if fn is not None:
                            last = fn(eng_w)
                            if isinstance(last, list):
                                last = last[-1]
                            if inc is not None:
                                last.then_inc(handles[inc[0]], inc[1])
                        elif inc is not None:
                            eng.sem_inc(handles[inc[0]], inc[1])
                return body
            block.sync(run_items("sync"))
            block.tensor(run_items("tensor"))
            block.vector(run_items("vector"))
            block.scalar(run_items("scalar"))
            block.gpsimd(run_items("gpsimd"))
        finally:
            for g in reversed(chain_guards):
                g.__exit__(None, None, None)
            for g in reversed(guards):
                g.__exit__(None, None, None)
    _stack.close()
    return nc



def _rope_tables():
    d = np.arange(0, 128, 2, dtype=np.float64) / 128.0
    theta = 1.0 / (10000.0 ** d)
    pos = np.arange(T, dtype=np.float64)
    idx = np.outer(pos, theta)
    return np.cos(idx).astype(np.float32), np.sin(idx).astype(np.float32)


def _core_order(lane):
    order = []
    for g in range(8):
        others = [t for t in range(4 * g, 4 * g + 4) if t != 4 * g + lane]
        order.extend(others + [4 * g + lane])
    return order


def kernel(L0_attn_x, L0_attn_w_attn, L0_attn_w_proj, L0_attn_adapter_wte,
           L0_attn_gating, L0_ffn_weight, L0_ffn_bias, L0_ffn_centroids,
           L0_ffn_assignments):
    x = np.ascontiguousarray(np.asarray(L0_attn_x, np.float32))
    cosf, sinf = _rope_tables()
    tri = np.triu(np.ones((128, 128), np.float32)).astype(ml_dtypes.bfloat16)
    ones_m = np.ones((128, 128), ml_dtypes.bfloat16)
    zero_m = np.zeros((128, 128), ml_dtypes.bfloat16)

    shared = {
        "w_attn": np.ascontiguousarray(np.asarray(L0_attn_w_attn, np.float32)),
        "w_proj": np.ascontiguousarray(np.asarray(L0_attn_w_proj, np.float32)),
        "w_ffn": np.ascontiguousarray(np.asarray(L0_ffn_weight, np.float32)),
        "ffn_bias": np.ascontiguousarray(np.asarray(L0_ffn_bias, np.float32))[None, :],
        "cents": np.ascontiguousarray(np.asarray(L0_ffn_centroids, np.float32)),
        "asg": np.ascontiguousarray(np.asarray(L0_ffn_assignments, np.int32))[None, :],
        "adapter": np.ascontiguousarray(np.asarray(L0_attn_adapter_wte, np.float32)),
        "gating": np.asarray(L0_attn_gating, np.float32).reshape(1, 1),
        "ident_bf": np.eye(128, dtype=ml_dtypes.bfloat16),
        "ident_f": np.eye(128, dtype=np.float32),
        "iota64": np.arange(64, dtype=np.float32)[:, None],
        "onescol_b": np.ones((128, 1), ml_dtypes.bfloat16),
        "onescol_f": np.ones((128, 1), np.float32),
        "onesrow_f": np.ones((1, 128), np.float32),
    }

    in_maps = []
    metas = []
    for c in range(8):
        lane, seq = c % 4, c // 4
        order = _core_order(lane)
        xt = x[seq].reshape(32, 128, C)
        xo = x[1 - seq].reshape(32, 128, C)
        xp_rows = np.concatenate([xt[order], xo], axis=0).reshape(NSLOT * 128, C)
        ctab = cosf.reshape(32, 128, 64)[order].reshape(MYS * 128, 64)
        stab = sinf.reshape(32, 128, 64)[order].reshape(MYS * 128, 64)
        masks = np.empty((32, 128, 128), ml_dtypes.bfloat16)
        for s in range(32):
            ktrue = order[s]
            town = 4 * (s // 4) + lane
            if ktrue < town:
                masks[s] = ones_m
            elif ktrue == town:
                masks[s] = tri
            else:
                masks[s] = zero_m
        m = dict(shared)
        m["xp"] = np.ascontiguousarray(xp_rows)
        m["cos_tab"] = np.ascontiguousarray(ctab)
        m["sin_tab"] = np.ascontiguousarray(stab)
        m["amask"] = np.ascontiguousarray(masks.reshape(MYS * 128, 128))
        in_maps.append(m)
        metas.append((seq, lane))

    nc = build_program()
    res = run_bass_kernel_spmd(nc, in_maps, core_ids=list(range(8)))

    out_full = np.empty((B, T, C), np.float32)
    for c in range(8):
        seq, lane = metas[c]
        shard = res.results[c]["out"]
        for j in range(8):
            t0 = (4 * j + lane) * 128
            out_full[seq, t0:t0 + 128, :] = shard[j * 128:(j + 1) * 128, :]
    return out_full


if __name__ == "__main__":
    build_program()
    print("build ok")


# version 12
# speedup vs baseline: 1.0037x; 1.0037x over previous
import numpy as np
import ml_dtypes
import concourse.bass as bass
import concourse.mybir as mybir
from concourse.bass_utils import run_bass_kernel_spmd

F32 = mybir.dt.float32
BF16 = mybir.dt.bfloat16
I32 = mybir.dt.int32
AF = mybir.ActivationFunctionType
ALU = mybir.AluOpType
AXX = mybir.AxisListType.X

B, T, C = 2, 4096, 4096
H = 32
NCH = 32
MYS = 32
NSLOT = 64
NOWN = 8
EPS = 1e-5
TAU_INV = 10.0
THRESH = 0.01
INV_SQRT_HD = float(1.0 / np.sqrt(128.0))
INV_64 = float(1.0 / 64.0)
INV_C = float(1.0 / 4096.0)
G2 = 11

ENGINES = ("sync", "tensor", "vector", "scalar", "gpsimd")


class _CountStub:

    def __init__(self):
        self.n = 0

    def __getattr__(self, name):
        def f(*a, **kw):
            self.n += 1

            class _D:
                def then_inc(self, *a2, **k2):
                    return self

                def __getitem__(self, *a2):
                    return self
            return _D()
        return f


CHAINED = ("vector", "scalar")


class Plan:
    def __init__(self):
        self.items = {e: [] for e in ENGINES}
        self.counts = {}
        self.chain = {e: 0 for e in CHAINED}
        self.hist = {}

    def now(self, s):
        return self.counts.get(s, 0)

    def _xlate(self, waits):
        out = []
        for s, v in waits:
            if v <= 0:
                continue
            if s in self.hist:
                eng, marks = self.hist[s]
                out.append(("__chain_" + eng, marks[min(v, len(marks)) - 1]))
            else:
                out.append((s, v))
        return out

    def emit(self, eng, fn, waits=(), inc=None):
        waits = self._xlate(waits)
        if eng in CHAINED and fn is not None:
            is_dma = inc is not None and inc[1] % 16 == 0
            if not is_dma:
                stub = _CountStub()
                fn(stub)
                assert stub.n > 0, "chained emit with no ops"
                self.chain[eng] += stub.n
                if inc is not None:
                    name = inc[0]
                    self.counts[name] = self.counts.get(name, 0) + 1
                    self.hist.setdefault(name, (eng, []))
                    h = self.hist[name]
                    assert h[0] == eng, f"sem {name} produced by two engines"
                    h[1].append(self.chain[eng])
                self.items[eng].append((fn, waits, None))
                return
        if inc is not None and inc[1] % 16 == 0 and inc[1] > 0:
            prior = self.counts.get(inc[0], 0)
            if prior > 0:
                waits.append((inc[0], prior))
        self.items[eng].append((fn, waits, inc))
        if inc is not None:
            self.counts[inc[0]] = self.counts.get(inc[0], 0) + inc[1]

    def barrier(self, name="bar"):
        base = self.now(name)
        marks = dict(self.chain)
        for e in ENGINES:
            self.emit(e, None, inc=(name, 1))
            self.counts[name] = self.counts.get(name, 0)
        for e in ENGINES:
            w = [(name, base + 5)]
            for ce in CHAINED:
                if marks[ce] > 0 and ce != e:
                    w.append(("__chain_" + ce, marks[ce]))
            self.items[e].append((None, w, None))


def build_program(upto=99):
    import os as _os
    dbg = bool(_os.environ.get("KMK_DEBUG"))
    nc = bass.Bass("TRN2", target_bir_lowering=False)

    xp = nc.dram_tensor("xp", [NSLOT * 128, C], F32, kind="ExternalInput").ap()
    w_attn = nc.dram_tensor("w_attn", [3 * C, C], F32, kind="ExternalInput").ap()
    w_proj = nc.dram_tensor("w_proj", [C, C], F32, kind="ExternalInput").ap()
    w_ffn = nc.dram_tensor("w_ffn", [C, C], F32, kind="ExternalInput").ap()
    ffn_bias = nc.dram_tensor("ffn_bias", [1, C], F32, kind="ExternalInput").ap()
    cents = nc.dram_tensor("cents", [64, C], F32, kind="ExternalInput").ap()
    asg = nc.dram_tensor("asg", [1, C], I32, kind="ExternalInput").ap()
    adapter = nc.dram_tensor("adapter", [10, C], F32, kind="ExternalInput").ap()
    gating = nc.dram_tensor("gating", [1, 1], F32, kind="ExternalInput").ap()
    cos_tab = nc.dram_tensor("cos_tab", [MYS * 128, 64], F32, kind="ExternalInput").ap()
    sin_tab = nc.dram_tensor("sin_tab", [MYS * 128, 64], F32, kind="ExternalInput").ap()
    amask = nc.dram_tensor("amask", [MYS * 128, 128], BF16, kind="ExternalInput").ap()
    ident_bf = nc.dram_tensor("ident_bf", [128, 128], BF16, kind="ExternalInput").ap()
    ident_f = nc.dram_tensor("ident_f", [128, 128], F32, kind="ExternalInput").ap()
    iota64 = nc.dram_tensor("iota64", [64, 1], F32, kind="ExternalInput").ap()
    onescol_b = nc.dram_tensor("onescol_b", [128, 1], BF16, kind="ExternalInput").ap()
    onescol_f = nc.dram_tensor("onescol_f", [128, 1], F32, kind="ExternalInput").ap()
    onesrow_f = nc.dram_tensor("onesrow_f", [1, 128], F32, kind="ExternalInput").ap()

    out = nc.dram_tensor("out", [NOWN * 128, C], F32, kind="ExternalOutput").ap()
    if dbg:
        dbg_rstd = nc.dram_tensor("dbg_rstd", [128, NSLOT], F32, kind="ExternalOutput").ap()
        dbg_qsel = nc.dram_tensor("dbg_qsel", [128, NSLOT], F32, kind="ExternalOutput").ap()
        dbg_mrow = nc.dram_tensor("dbg_mrow", [1, MYS * 128], BF16, kind="ExternalOutput").ap()
        dbg_sqv = nc.dram_tensor("dbg_sqv", [1, MYS * 128], BF16, kind="ExternalOutput").ap()
        dbg_rm = nc.dram_tensor("dbg_rm", [128, C], BF16, kind="ExternalOutput").ap()
        dbg_xfm = nc.dram_tensor("dbg_xfm", [128, C], BF16, kind="ExternalOutput").ap()
        dbg_k = nc.dram_tensor("dbg_k", [MYS * 128, C], BF16, kind="ExternalOutput").ap()
        dbg_v = nc.dram_tensor("dbg_v", [MYS * 128, C], BF16, kind="ExternalOutput").ap()
        dbg_q = nc.dram_tensor("dbg_q", [NOWN * 128, C], BF16, kind="ExternalOutput").ap()
        dbg_g = nc.dram_tensor("dbg_g", [NOWN * 128, C], BF16, kind="ExternalOutput").ap()
        dbg_y = nc.dram_tensor("dbg_y", [NOWN * 128, C], BF16, kind="ExternalOutput").ap()
        dbg_ak = nc.dram_tensor("dbg_ak", [128, H * 10], BF16, kind="ExternalOutput").ap()
        dbg_av = nc.dram_tensor("dbg_av", [10, H * 128], BF16, kind="ExternalOutput").ap()
        dbg_srow = nc.dram_tensor("dbg_srow", [128, 64], F32, kind="ExternalOutput").ap()
        dbg_xh = nc.dram_tensor("dbg_xh", [256, C], BF16, kind="ExternalOutput").ap()
        dbg_xl = nc.dram_tensor("dbg_xl", [256, C], BF16, kind="ExternalOutput").ap()

    wattn_bf = nc.dram_tensor("wattn_bf", [3 * C, C], BF16, kind="Internal").ap()
    wproj_bf = nc.dram_tensor("wproj_bf", [C, C], BF16, kind="Internal").ap()
    wffn_bf = nc.dram_tensor("wffn_bf", [C, C], BF16, kind="Internal").ap()
    adapt_bf = nc.dram_tensor("adapt_bf", [10, C], BF16, kind="Internal").ap()
    bias_d = nc.dram_tensor("bias_d", [1, C], BF16, kind="Internal").ap()
    xhat_d = nc.dram_tensor("xhat_d", [NSLOT * 128, C], BF16, kind="Internal").ap()
    xlo_d = nc.dram_tensor("xlo_d", [NSLOT * 128, C], BF16, kind="Internal").ap()
    xfm_d = nc.dram_tensor("xfm_d", [NSLOT, 128, C], BF16, kind="Internal").ap()
    wbar_d = nc.dram_tensor("wbar_d", [16, 512], BF16, kind="Internal").ap()
    mrow_d = nc.dram_tensor("mrow_d", [1, MYS * 128], BF16, kind="Internal").ap()
    sqv_d = nc.dram_tensor("sqv_d", [1, MYS * 128], BF16, kind="Internal").ap()
    rm_d = nc.dram_tensor("rm_d", [128, C], BF16, kind="Internal").ap()
    k_d = nc.dram_tensor("k_d", [MYS * 128, C], BF16, kind="Internal").ap()
    v_d = nc.dram_tensor("v_d", [MYS * 128, C], BF16, kind="Internal").ap()
    q_d = nc.dram_tensor("q_d", [NOWN * 128, C], BF16, kind="Internal").ap()
    g_d = nc.dram_tensor("g_d", [NOWN * 128, C], BF16, kind="Internal").ap()
    y_d = nc.dram_tensor("y_d", [NOWN * 128, C], BF16, kind="Internal").ap()

    p = Plan()

    from contextlib import ExitStack
    _stack = ExitStack()
    idb = _stack.enter_context(nc.sbuf_tensor("idb", [128, 128], BF16))
    idf = _stack.enter_context(nc.sbuf_tensor("idf", [128, 128], F32))
    iota_sb = _stack.enter_context(nc.sbuf_tensor("iota_sb", [64, 1], F32))
    o_cb = _stack.enter_context(nc.sbuf_tensor("o_cb", [128, 1], BF16))
    o_cf = _stack.enter_context(nc.sbuf_tensor("o_cf", [128, 1], F32))
    o_rf = _stack.enter_context(nc.sbuf_tensor("o_rf", [1, 128], F32))
    gat1 = _stack.enter_context(nc.sbuf_tensor("gat1", [1, 1], F32))
    gat_bc = _stack.enter_context(nc.sbuf_tensor("gat_bc", [128, 1], F32))
    srow_bc = _stack.enter_context(nc.sbuf_tensor("srow_bc", [128, 64], F32))
    rstd_all = _stack.enter_context(nc.sbuf_tensor("rstd_all", [128, NSLOT], F32))
    qsel_all = _stack.enter_context(nc.sbuf_tensor("qsel_all", [128, NSLOT], F32))
    adapt_fm = _stack.enter_context(nc.sbuf_tensor("adapt_fm", [128, NCH * 10], BF16))
    ak_all = _stack.enter_context(nc.sbuf_tensor("ak_all", [128, H * 10], BF16))
    av_all = _stack.enter_context(nc.sbuf_tensor("av_all", [10, H * 128], BF16))
    stat = _stack.enter_context(nc.sbuf_tensor("stat", [128, 48], F32))
    rowsc = _stack.enter_context(nc.sbuf_tensor("rowsc", [1, 256], F32))
    mrow_pc = _stack.enter_context(nc.sbuf_tensor("mrow_pc", [1, 256], F32))
    mrow_sw = _stack.enter_context(nc.sbuf_tensor("mrow_sw", [1, G2 * 128 + 128], BF16))
    sqv_sw = _stack.enter_context(nc.sbuf_tensor("sqv_sw", [1, G2 * 128 + 128], BF16))
    bias_blk = _stack.enter_context(nc.sbuf_tensor("bias_blk", [1, 512], BF16))
    rm_blk = _stack.enter_context(nc.sbuf_tensor("rm_blk", [128, 512], BF16))
    nwbar = _stack.enter_context(nc.sbuf_tensor("nwbar", [1, 512], BF16))
    csel = _stack.enter_context(nc.sbuf_tensor("csel", [64, 1], F32))
    adsc = _stack.enter_context(nc.sbuf_tensor("adsc", [128, 40], BF16))
    asg_ic = _stack.enter_context(nc.sbuf_tensor("asg_ic", [1, 512], I32))
    arena_bf = _stack.enter_context(nc.sbuf_tensor("arena_bf", [128, 51712], BF16))
    arena_w = _stack.enter_context(nc.sbuf_tensor("arena_w", [128, 32768], BF16))
    arena_f = _stack.enter_context(nc.sbuf_tensor("arena_f", [128, 4096], F32))
    ps0 = _stack.enter_context(nc.psum_tensor("ps0", [128, 512], F32))
    ps1 = _stack.enter_context(nc.psum_tensor("ps1", [128, 512], F32))
    ps2 = _stack.enter_context(nc.psum_tensor("ps2", [128, 512], F32))
    ps3 = _stack.enter_context(nc.psum_tensor("ps3", [128, 512], F32))
    ps4 = _stack.enter_context(nc.psum_tensor("ps4", [128, 512], F32))
    ps5 = _stack.enter_context(nc.psum_tensor("ps5", [128, 512], F32))
    ps6 = _stack.enter_context(nc.psum_tensor("ps6", [128, 512], F32))
    ps7 = _stack.enter_context(nc.psum_tensor("ps7", [128, 1024], BF16))
    block = _stack.enter_context(nc.Block())
    if True:
        centhi = arena_bf[:, 36864:38976]
        centlo = arena_bf[:, 38976:41088]
        ohc = arena_f[0:64, 3072:3584]
        asg_fc = arena_f[0:1, 2560:3072]

        def ph0():
            loads = [(idb[:], ident_bf), (idf[:], ident_f), (iota_sb[:], iota64),
                     (o_cb[:], onescol_b), (o_cf[:], onescol_f), (o_rf[:], onesrow_f),
                     (gat1[:], gating)]
            for dst, src in loads:
                p.emit("sync", (lambda d=dst, s_=src: lambda e: e.dma_start(out=d, in_=s_))(),
                       inc=("c_ld", 16))
            nl = 16 * len(loads)
            p.emit("gpsimd", lambda e: e.dma_start(out=adapt_bf, in_=adapter), inc=("wb_ad", 16))
            p.emit("gpsimd", lambda e: e.dma_start(out=bias_d, in_=ffn_bias), inc=("wb_b", 16))
            cent_nat = arena_f[0:64, 0:4096]
            p.emit("sync", lambda e: e.dma_start(out=cent_nat, in_=cents), inc=("c_ld", 16))
            ad_nat = arena_bf[0:10, 0:4096]
            p.emit("sync", lambda e: e.dma_start(out=ad_nat, in_=adapt_bf),
                   waits=[("wb_ad", 16)], inc=("c_ld", 16))

            p.emit("tensor", lambda e: e.matmul(out=ps6[:, 0:1], lhsT=o_rf[:], rhs=gat1[:],
                                                start=True, stop=True),
                   waits=[("c_ld", nl)], inc=("s_t", 1))
            p.emit("vector", lambda e: e.tensor_copy(out=gat_bc[:], in_=ps6[:, 0:1]),
                   waits=[("s_t", p.now("s_t"))], inc=("s_v", 1))
            gv = p.now("s_v")

            p.emit("vector", lambda e: e.reduce_sum(out=csel[:], in_=cent_nat, axis=AXX),
                   waits=[("c_ld", nl + 16)], inc=("s_v", 1))
            p.emit("tensor", lambda e: e.transpose(out=ps5[0:1, 0:64], in_=csel[:],
                                                   identity=idf[0:64, 0:64]),
                   waits=[("s_v", p.now("s_v"))], inc=("s_t", 1))
            p.emit("vector", lambda e: e.tensor_copy(out=rowsc[:, 0:64], in_=ps5[0:1, 0:64]),
                   waits=[("s_t", p.now("s_t"))], inc=("s_v", 1))
            p.emit("tensor", lambda e: e.matmul(out=ps6[:, 0:64], lhsT=o_rf[:],
                                                rhs=rowsc[0:1, 0:64], start=True, stop=True),
                   waits=[("s_v", p.now("s_v"))], inc=("s_t", 1))
            p.emit("vector", lambda e: e.tensor_copy(out=srow_bc[:], in_=ps6[:, 0:64]),
                   waits=[("s_t", p.now("s_t"))], inc=("s_v", 1))

            for b4 in range(8):
                bank = ps4 if b4 % 2 == 0 else ps5
                p.emit("tensor", (lambda b=b4, bk=bank: lambda e: [
                    e.transpose(out=bk[:, jj * 64:jj * 64 + 64],
                                in_=cent_nat[0:64, (b * 4 + jj) * 128:(b * 4 + jj + 1) * 128],
                                identity=idf[0:64, 0:64]) for jj in range(4)][-1])(),
                    waits=[("c_ld", nl + 16), ("c_ev", max(0, b4 - 1)),
                           ("s_v", p.now("s_v"))],
                    inc=("c_tr", 1))

                def csplit(e, b=b4, bk=bank):
                    r = None
                    for jj in range(4):
                        j = b * 4 + jj
                        e.tensor_copy(out=centhi[:, j * 66:j * 66 + 64],
                                      in_=bk[:, jj * 64:jj * 64 + 64])
                        r = e.scalar_tensor_tensor(out=centlo[:, j * 66:j * 66 + 64],
                                                   in0=centhi[:, j * 66:j * 66 + 64],
                                                   scalar=-1.0,
                                                   in1=bk[:, jj * 64:jj * 64 + 64],
                                                   op0=ALU.mult, op1=ALU.add)
                    return r
                p.emit("vector", csplit, waits=[("c_tr", p.now("c_tr"))], inc=("c_ev", 1))
            p.emit("vector", (lambda: lambda e: [
                e.memset(centhi.rearrange("p (j w) -> p j w", w=66)[:, :, 64:66], 0.0),
                e.memset(centlo.rearrange("p (j w) -> p j w", w=66)[:, :, 64:66], 0.0),
                e.memset(centhi.rearrange("p (j w) -> p j w", w=66)[:, :, 64:65], 1.0)][-1])(),
                waits=[("c_ev", 8)], inc=("s_v", 1))

            for b4 in range(8):
                p.emit("tensor", (lambda b=b4: lambda e: [
                    e.transpose(out=ps7[:, jj * 10:jj * 10 + 10],
                                in_=ad_nat[0:10, (b * 4 + jj) * 128:(b * 4 + jj + 1) * 128],
                                identity=idb[0:10, 0:10]) for jj in range(4)][-1])(),
                    waits=[("c_ld", nl + 32), ("ad_ev", b4)], inc=("ad_tr", 1))
                p.emit("vector", (lambda b=b4: lambda e: e.tensor_copy(
                    out=adapt_fm[:, b * 40:(b + 1) * 40], in_=ps7[:, 0:40]))(),
                    waits=[("ad_tr", p.now("ad_tr"))], inc=("ad_ev", 1))

        xtm = [arena_f[:, i * 512:(i + 1) * 512] for i in range(4)]
        xhat_t = [arena_bf[:, i * 4096:(i + 1) * 4096] for i in range(4)]
        xlo_t = [arena_bf[:, 16384 + i * 4096:16384 + (i + 1) * 4096] for i in range(4)]

        ph1a_pend = [None]

        def ph1a_flush():
            if ph1a_pend[0] is not None:
                ph1a_pend[0]()
                ph1a_pend[0] = None

        def ph1a_block(tb):
            for s in range(4 * tb, 4 * tb + 4):
                spr = s % 4
                for qq in range(8):
                    idx = s * 8 + qq
                    p.emit("scalar", (lambda s=s, qq=qq, idx=idx: lambda e: e.dma_start(
                        out=xtm[idx % 4][:],
                        in_=xp[s * 128:(s + 1) * 128, qq * 512:(qq + 1) * 512]))(),
                        waits=[("p1cast", max(0, idx - 3))], inc=("p1xld", 16))
                    xldv = p.now("p1xld")
                    p.emit("vector", (lambda spr=spr, qq=qq, idx=idx: lambda e: e.tensor_copy(
                        out=xhat_t[spr][:, qq * 512:(qq + 1) * 512], in_=xtm[idx % 4][:]))(),
                        waits=[("p1xld", xldv), ("p1sp", 32 * max(0, s - 3))])
                    p.emit("vector", (lambda spr=spr, qq=qq, idx=idx: lambda e: e.tensor_tensor(
                        out=xlo_t[spr][:, qq * 512:(qq + 1) * 512], in0=xtm[idx % 4][:],
                        in1=xhat_t[spr][:, qq * 512:(qq + 1) * 512], op=ALU.subtract))(),
                        inc=("p1cast", 1))
                ph1a_flush()
                cv = p.now("p1cast")

                def spill(s=s, spr=spr, cv=cv):
                    p.emit("sync", (lambda s=s, spr=spr, cv=cv: lambda e: e.dma_start(
                        out=xhat_d[s * 128:(s + 1) * 128, :], in_=xhat_t[spr][:]))(),
                        waits=[("p1cast", cv)], inc=("p1sp", 16))
                    p.emit("sync", (lambda s=s, spr=spr: lambda e: e.dma_start(
                        out=xlo_d[s * 128:(s + 1) * 128, :], in_=xlo_t[spr][:]))(),
                        inc=("p1sp", 16))
                ph1a_pend[0] = spill

        thi = [arena_bf[:, 32768 + i * 512:32768 + (i + 1) * 512] for i in range(4)]
        tlo = [arena_bf[:, 34816 + i * 512:34816 + (i + 1) * 512] for i in range(4)]
        mrow_pcb = arena_bf[0:1, 41088:42112]
        zt_sb = arena_f[:, 2048:2560]
        ztm4 = arena_f[:, 2560:2824]
        expz4 = arena_f[:, 2824:3080]
        diag_sc = arena_f[:, 3080:3208]

        def ph1b_block(tb):
            if True:
                zbank = ps0 if tb % 2 == 0 else ps5
                dbank = ps1 if tb % 2 == 0 else ps6
                for cc in range(NCH):
                    pr = (tb * NCH + cc) % 4
                    idx = tb * NCH + cc
                    p.emit("sync", (lambda tb=tb, cc=cc, pr=pr: lambda e: e.dma_start(
                        out=thi[pr][:],
                        in_=xhat_d[tb * 512:(tb + 1) * 512, cc * 128:(cc + 1) * 128],
                        transpose=True))(),
                        waits=[("p1mm", max(0, idx - 3)), ("p1fsp", 16 * max(0, idx - 3))] +
                              ([("p1sp", 32 * (4 * tb + 4))] if cc == 0 else []),
                        inc=("p1trd", 16))
                    p.emit("sync", (lambda tb=tb, cc=cc, pr=pr: lambda e: e.dma_start(
                        out=tlo[pr][:],
                        in_=xlo_d[tb * 512:(tb + 1) * 512, cc * 128:(cc + 1) * 128],
                        transpose=True))(),
                        waits=[("p1mm", max(0, idx - 3))] +
                              ([("p1sp", 32 * (4 * tb + 4))] if cc == 0 else []),
                        inc=("p1trd2", 16))
                    trd, trd2 = p.now("p1trd"), p.now("p1trd2")
                    mm_waits = [("p1trd", trd), ("p1trd2", trd2)]
                    if cc == 0:
                        mm_waits += [("p1zev", max(0, tb - 1)), ("p1v1", max(0, tb - 1))]

                    def mmz(e, cc=cc, pr=pr, zbank=zbank, dbank=dbank):
                        first = (cc == 0)
                        last_ = (cc == NCH - 1)
                        e.matmul(out=zbank[0:66, :], lhsT=centhi[:, cc * 66:(cc + 1) * 66],
                                 rhs=thi[pr][:], start=first, stop=False,
                                 skip_group_check=True)
                        e.matmul(out=zbank[0:66, :], lhsT=centlo[:, cc * 66:(cc + 1) * 66],
                                 rhs=thi[pr][:], start=False, stop=False,
                                 skip_group_check=True)
                        e.matmul(out=zbank[0:66, :], lhsT=centhi[:, cc * 66:(cc + 1) * 66],
                                 rhs=tlo[pr][:], start=False, stop=last_,
                                 skip_group_check=True)
                        r = None
                        for u in range(4):
                            r = e.matmul(out=dbank[:, u * 128:(u + 1) * 128],
                                         lhsT=thi[pr][:, u * 128:(u + 1) * 128],
                                         rhs=thi[pr][:, u * 128:(u + 1) * 128],
                                         start=(first and u == 0), stop=last_,
                                         skip_group_check=True)
                        return r
                    p.emit("tensor", mmz, waits=mm_waits, inc=("p1mm", 1))
                    p.emit("gpsimd", (lambda tb=tb, cc=cc, pr=pr: lambda e: e.dma_start(
                        out=xfm_d[tb * 4:(tb + 1) * 4, :, cc * 128:(cc + 1) * 128]
                            .rearrange("s p t -> p s t"),
                        in_=thi[pr][:].rearrange("p (s t) -> p s t", s=4)))(),
                        waits=[("p1trd", trd)], inc=("p1fsp", 16))

                mmv = p.now("p1mm")
                p.emit("vector", (lambda zbank=zbank: lambda e: e.tensor_copy(
                    out=zt_sb[0:66, :], in_=zbank[0:66, :]))(),
                       waits=[("p1mm", mmv), ("p1ztr", tb)], inc=("p1zev", 1))
                zev = p.now("p1zev")
                p.emit("tensor", (lambda: lambda e: [
                    e.transpose(out=ps2[:, u * 66:(u + 1) * 66],
                                in_=zt_sb[0:66, u * 128:(u + 1) * 128],
                                identity=idf[0:66, 0:66]) for u in range(4)][-1])(),
                    waits=[("p1zev", zev), ("p1v1", tb), ("p1v4", min(tb, 8))],
                    inc=("p1ztr", 1))
                ztr = p.now("p1ztr")

                def vstA(e, tb=tb, dbank=dbank):
                    for u in range(4):
                        e.tensor_tensor(out=diag_sc[:], in0=dbank[:, u * 128:(u + 1) * 128],
                                        in1=idf[:], op=ALU.mult)
                        e.reduce_sum(out=stat[:, u:u + 1], in_=diag_sc[:], axis=AXX)
                    e.tensor_copy(out=ztm4[:], in_=ps2[:, 0:264])
                    for u in range(4):
                        e.tensor_scalar(out=stat[:, 4 + u:5 + u], in0=ztm4[:, u * 66 + 64:u * 66 + 65],
                                        scalar1=INV_C, scalar2=None, op0=ALU.mult)
                    e.tensor_scalar(out=stat[:, 8:12], in0=stat[:, 0:4], scalar1=INV_C,
                                    scalar2=None, op0=ALU.mult)
                    e.tensor_tensor(out=stat[:, 12:16], in0=stat[:, 4:8], in1=stat[:, 4:8],
                                    op=ALU.mult)
                    e.tensor_tensor(out=stat[:, 8:12], in0=stat[:, 8:12], in1=stat[:, 12:16],
                                    op=ALU.subtract)
                    return e.tensor_scalar(out=stat[:, 8:12], in0=stat[:, 8:12], scalar1=EPS,
                                           scalar2=None, op0=ALU.add)
                p.emit("vector", vstA,
                       waits=[("p1mm", mmv), ("p1ztr", ztr), ("p1a1", tb), ("p1a2", tb)],
                       inc=("p1v1", 1))
                v1 = p.now("p1v1")
                p.emit("scalar", (lambda: lambda e: [
                    e.activation(out=stat[:, 12:16], in_=stat[:, 8:12], func=AF.Ln),
                    e.activation(out=stat[:, 16:20], in_=stat[:, 12:16], func=AF.Exp,
                                 scale=0.5)][-1])(),
                    waits=[("p1v1", v1), ("p1rtr", p.now("p1rtr"))], inc=("p1a1", 1))
                a1 = p.now("p1a1")

                def vstB(e, tb=tb):
                    e.reciprocal(out=rstd_all[:, 4 * tb:4 * tb + 4], in_=stat[:, 16:20])
                    e.tensor_scalar(out=stat[:, 20:24], in0=stat[:, 4:8], scalar1=-1.0,
                                    scalar2=None, op0=ALU.mult)
                    for u in range(4):
                        e.scalar_tensor_tensor(out=ztm4[:, u * 66:u * 66 + 64], in0=srow_bc[:],
                                               scalar=stat[:, 20 + u:21 + u],
                                               in1=ztm4[:, u * 66:u * 66 + 64],
                                               op0=ALU.mult, op1=ALU.add)
                    for u in range(4):
                        e.reduce_max(out=stat[:, 24 + u:25 + u], in_=ztm4[:, u * 66:u * 66 + 64],
                                     axis=AXX, negate=True)
                    e.tensor_scalar(out=stat[:, 28:32], in0=rstd_all[:, 4 * tb:4 * tb + 4],
                                    scalar1=TAU_INV, scalar2=None, op0=ALU.mult)
                    return e.tensor_tensor(out=stat[:, 32:36], in0=stat[:, 24:28],
                                           in1=stat[:, 28:32], op=ALU.mult)
                p.emit("vector", vstB, waits=[("p1a1", a1)], inc=("p1v2", 1))
                v2 = p.now("p1v2")
                p.emit("scalar", (lambda: lambda e: [
                    e.activation(out=expz4[:, u * 64:(u + 1) * 64],
                                 in_=ztm4[:, u * 66:u * 66 + 64], func=AF.Exp,
                                 bias=stat[:, 32 + u:33 + u], scale=stat[:, 28 + u:29 + u])
                    for u in range(4)][-1])(),
                    waits=[("p1v2", v2), ("p1cnt", tb)], inc=("p1a2", 1))
                a2 = p.now("p1a2")

                def vstC(e, tb=tb):
                    for u in range(4):
                        e.reduce_sum(out=stat[:, 36 + u:37 + u], in_=expz4[:, u * 64:(u + 1) * 64],
                                     axis=AXX)
                    e.tensor_scalar(out=stat[:, 40:44], in0=stat[:, 36:40], scalar1=THRESH,
                                    scalar2=None, op0=ALU.mult)
                    for u in range(4):
                        e.tensor_scalar(out=expz4[:, u * 64:(u + 1) * 64],
                                        in0=expz4[:, u * 64:(u + 1) * 64],
                                        scalar1=stat[:, 40 + u:41 + u], scalar2=None,
                                        op0=ALU.is_gt)
                    r = None
                    for u in range(4):
                        r = e.reduce_max(out=qsel_all[:, 4 * tb + u:4 * tb + u + 1],
                                         in_=expz4[:, u * 64:(u + 1) * 64], axis=AXX)
                    return r
                p.emit("vector", vstC, waits=[("p1a2", a2)], inc=("p1v3", 1))
                v3 = p.now("p1v3")
                p.emit("tensor", (lambda tb=tb: lambda e: [
                    e.matmul(out=ps3[0:1, 0:64], lhsT=o_cf[:],
                             rhs=expz4[:, u * 64:(u + 1) * 64],
                             start=(tb == 0 and u == 0), stop=(tb == 15 and u == 3),
                             skip_group_check=True) for u in range(4)][-1])(),
                    waits=[("p1v3", v3)], inc=("p1cnt", 1))
                if tb < 8:
                    p.emit("tensor", (lambda: lambda e: ([
                        e.transpose(out=ps4[0:1, u * 128:(u + 1) * 128],
                                    in_=stat[:, 4 + u:5 + u], identity=idf[:])
                        for u in range(4)] + [
                        e.transpose(out=ps2[0:1, u * 128:(u + 1) * 128],
                                    in_=stat[:, 16 + u:17 + u], identity=idf[:])
                        for u in range(4)])[-1])(),
                        waits=[("p1v2", v2), ("p1v4", tb)], inc=("p1rtr", 1))
                    rr = p.now("p1rtr")
                    p.emit("vector", (lambda: lambda e: [
                        e.tensor_copy(out=mrow_pcb[0:1, 0:512], in_=ps4[0:1, 0:512]),
                        e.tensor_copy(out=mrow_pcb[0:1, 512:1024], in_=ps2[0:1, 0:512]),
                    ][-1])(), waits=[("p1rtr", rr), ("p1rsp", 32 * tb)],
                        inc=("p1v4", 1))
                    v4 = p.now("p1v4")
                    p.emit("gpsimd", (lambda tb=tb: lambda e: e.dma_start(
                        out=mrow_d[:, tb * 512:(tb + 1) * 512], in_=mrow_pcb[0:1, 0:512]))(),
                        waits=[("p1v4", v4)], inc=("p1rsp", 16))
                    p.emit("gpsimd", (lambda tb=tb: lambda e: e.dma_start(
                        out=sqv_d[:, tb * 512:(tb + 1) * 512], in_=mrow_pcb[0:1, 512:1024]))(),
                        inc=("p1rsp", 16))

        def ph1c():
            p.emit("vector", lambda e: e.tensor_copy(out=rowsc[:, 64:128], in_=ps3[0:1, 0:64]),
                   waits=[("p1cnt", 16)], inc=("c1v", 1))
            p.emit("tensor", lambda e: e.transpose(out=ps4[0:64, 0:1],
                                                   in_=rowsc[0:1, 64:128][:, 0:64],
                                                   identity=idf[0:1, 0:1]),
                   waits=[("c1v", p.now("c1v")), ("p1v4", p.now("p1v4"))], inc=("c1t", 1))
            p.emit("vector", (lambda: lambda e: [
                e.tensor_copy(out=csel[:], in_=ps4[0:64, 0:1]),
                e.tensor_scalar(out=csel[:], in0=csel[:], scalar1=0.5, scalar2=None,
                                op0=ALU.is_gt)][-1])(),
                waits=[("c1t", p.now("c1t"))], inc=("c1v", 1))
            for j in range(8):
                p.emit("sync", (lambda j=j: lambda e: e.dma_start(
                    out=asg_ic[:], in_=asg[:, j * 512:(j + 1) * 512]))(),
                    waits=[("c1oh", 2 * j)], inc=("c1al", 16))
                p.emit("vector", (lambda: lambda e: e.tensor_copy(
                    out=asg_fc[:], in_=asg_ic[:]))(),
                    waits=[("c1al", p.now("c1al")), ("c1mm", p.now("c1mm"))], inc=("c1v", 1))
                p.emit("tensor", (lambda: lambda e: e.matmul(
                    out=ps5[0:64, :], lhsT=o_rf[:, 0:64], rhs=asg_fc[:],
                    start=True, stop=True))(),
                    waits=[("c1v", p.now("c1v")), ("c1oh", 2 * j), ("p1zev", 16)],
                    inc=("c1mm", 1))
                p.emit("vector", (lambda: lambda e: [
                    e.tensor_copy(out=ohc[:], in_=ps5[0:64, :]),
                    e.tensor_scalar(out=ohc[:], in0=ohc[:], scalar1=iota_sb[:, 0:1],
                                    scalar2=None, op0=ALU.is_equal)][-1])(),
                    waits=[("c1mm", p.now("c1mm"))], inc=("c1oh", 1))
                p.emit("tensor", (lambda: lambda e: e.matmul(
                    out=ps5[0:1, :], lhsT=csel[:], rhs=ohc[:], start=True, stop=True))(),
                    waits=[("c1oh", p.now("c1oh")), ("c1v", p.now("c1v"))], inc=("c1mm", 1))
                p.emit("vector", (lambda: lambda e: e.tensor_copy(
                    out=ohc[0:1, :], in_=ps5[0:1, :]))(),
                    waits=[("c1mm", p.now("c1mm"))], inc=("c1v", 1))
                p.emit("tensor", (lambda: lambda e: e.matmul(
                    out=ps6[:, :], lhsT=o_rf[:], rhs=ohc[0:1, :], start=True, stop=True))(),
                    waits=[("c1v", p.now("c1v")), ("p1v1", 16)], inc=("c1mm", 1))
                p.emit("vector", (lambda: lambda e: e.tensor_copy(
                    out=rm_blk[:], in_=ps6[:, :]))(),
                    waits=[("c1mm", p.now("c1mm")), ("c1rsp", 16 * j)], inc=("c1oh", 1))
                p.emit("gpsimd", (lambda j=j: lambda e: e.dma_start(
                    out=rm_d[:, j * 512:(j + 1) * 512], in_=rm_blk[:]))(),
                    waits=[("c1oh", p.now("c1oh"))], inc=("c1rsp", 16))

        xg = [arena_bf[:, i * 4096:(i + 1) * 4096] for i in range(G2)]
        ev_st = [arena_bf[:, 45056 + i * 512:45056 + (i + 1) * 512] for i in range(2)]
        wblk = [arena_w[:, 0:16384], arena_w[:, 16384:32768]]
        rp_t1 = arena_f[:, 0:256]
        rp_t2 = arena_f[:, 256:512]
        costl = arena_f[:, 512:576]
        sintl = arena_f[:, 576:640]
        cosP = [arena_bf[:, 46080 + i * 256:46080 + (i + 1) * 256] for i in range(G2)]
        sinP = [arena_bf[:, 48896 + i * 256:48896 + (i + 1) * 256] for i in range(G2)]

        state = {"evc": 0, "wlast": {0: 0, 1: 0}, "adlast": {0: 0, 1: 0}}

        CAST_CHUNKS = ([(wattn_bf, w_attn, 4096 + i * 512) for i in range(16)] +
                       [(wattn_bf, w_attn, i * 512) for i in range(8)] +
                       [(wffn_bf, w_ffn, i * 512) for i in range(8)] +
                       [(wproj_bf, w_proj, i * 512) for i in range(8)])

        def emit_cast_chunks(lo, hi):
            for ci in range(lo, min(hi, len(CAST_CHUNKS))):
                dst, srcw, r0 = CAST_CHUNKS[ci]
                p.emit("gpsimd", (lambda dst=dst, srcw=srcw, r0=r0: lambda e: e.dma_start(
                    out=dst[r0:r0 + 512, :], in_=srcw[r0:r0 + 512, :]))(),
                    inc=("wbc", 16))

        def w_src(kind, blk):
            if kind == "kv":
                return wattn_bf, 4096 + blk * 512, 16 * (blk + 1)
            if kind == "q":
                return wattn_bf, blk * 512, 16 * (17 + blk)
            if kind == "ffn":
                return wffn_bf, blk * 512, 16 * (25 + blk)
            return wproj_bf, blk * 512, 16 * (33 + blk)

        def load_wblk(kind, blk, par, mm_sem):
            src, ro, wneed = w_src(kind, blk)
            guard = [(mm_sem, state["wlast"][par]), ("p2ad", state["adlast"][par])]
            for cc in range(NCH):
                p.emit("sync", (lambda src=src, ro=ro, cc=cc, par=par: lambda e: e.dma_start(
                    out=wblk[par][:, cc * 512:(cc + 1) * 512],
                    in_=src[ro:ro + 512, cc * 128:(cc + 1) * 128], transpose=True))(),
                    waits=[("wbc", wneed)] + (guard if cc < 1 else []), inc=("w_ls", 16))
            return p.now("w_ls"), p.now("w_lc")

        def wbar_compute(par, wls, wlc, mm_sem, store_blk=None):
            p.emit("tensor", (lambda par=par: lambda e: [
                e.matmul(out=ps2[0:1, :], lhsT=o_cb[:],
                         rhs=wblk[par][:, cc * 512:(cc + 1) * 512],
                         start=(cc == 0), stop=(cc == NCH - 1)) for cc in range(NCH)][-1])(),
                waits=[("w_ls", wls), ("w_lc", wlc), ("p2wbv", p.now("p2wbv"))],
                inc=("p2wb", 1))
            p.emit("vector", lambda e: e.tensor_scalar(out=nwbar[:], in0=ps2[0:1, :],
                                                       scalar1=-1.0, scalar2=None,
                                                       op0=ALU.mult),
                   waits=[("p2wb", p.now("p2wb")), (mm_sem, p.now(mm_sem)),
                          ("p2wbsp", p.now("p2wbsp"))],
                   inc=("p2wbv", 1))
            if store_blk is not None:
                p.emit("gpsimd", (lambda b=store_blk: lambda e: e.dma_start(
                    out=wbar_d[b:b + 1, :], in_=nwbar[:]))(),
                    waits=[("p2wbv", p.now("p2wbv"))], inc=("p2wbsp", 16))

        def wbar_load(blk):
            p.emit("sync", (lambda b=blk: lambda e: e.dma_start(
                out=nwbar[:], in_=wbar_d[b:b + 1, :]))(),
                waits=[("p2wbsp", 16 * 16), ("p2mm", p.now("p2mm"))],
                inc=("p2wbl", 16))

        def slot_prologue(gslot, si):
            p.emit("sync", (lambda s=gslot: lambda e: e.dma_start(
                out=costl[:], in_=cos_tab[s * 128:(s + 1) * 128, :]))(),
                waits=[("p2rope", p.now("p2rope"))], inc=("p2cs", 16))
            p.emit("sync", (lambda s=gslot: lambda e: e.dma_start(
                out=sintl[:], in_=sin_tab[s * 128:(s + 1) * 128, :]))(),
                inc=("p2cs", 16))
            cs = p.now("p2cs")
            def mkrep(e, s=gslot, si=si):
                e.tensor_scalar(out=rp_t1[:, 0:64], in0=costl[:],
                                scalar1=rstd_all[:, s:s + 1], scalar2=None, op0=ALU.mult)
                e.tensor_scalar(out=rp_t2[:, 0:64], in0=sintl[:],
                                scalar1=rstd_all[:, s:s + 1], scalar2=None, op0=ALU.mult)
                for r in range(4):
                    e.tensor_copy(out=cosP[si][:, r * 64:(r + 1) * 64], in_=rp_t1[:, 0:64])
                r_ = None
                for r in range(4):
                    r_ = e.tensor_copy(out=sinP[si][:, r * 64:(r + 1) * 64],
                                       in_=rp_t2[:, 0:64])
                return r_
            p.emit("vector", mkrep, waits=[("p2cs", cs)], inc=("p2rope", 1))

        def slot_mms(kind, blk, par, xg_i, sw_i, psum, extra):
            def mm(e, kind=kind, blk=blk, par=par, xg_i=xg_i, sw_i=sw_i, psum=psum):
                for cc in range(NCH):
                    e.matmul(out=psum[:, :], lhsT=xg[xg_i][:, cc * 128:(cc + 1) * 128],
                             rhs=wblk[par][:, cc * 512:(cc + 1) * 512],
                             start=(cc == 0), stop=False)
                r = e.matmul(out=psum[:, :], lhsT=mrow_sw[:, sw_i * 128:(sw_i + 1) * 128],
                             rhs=nwbar[:], start=False, stop=(kind != "ffn"))
                if kind == "ffn":
                    r = e.matmul(out=psum[:, :],
                                 lhsT=sqv_sw[:, sw_i * 128:(sw_i + 1) * 128],
                                 rhs=bias_blk[:], start=False, stop=True)
                return r
            p.emit("tensor", mm, waits=extra, inc=("p2mm", 1))
            return p.now("p2mm")

        def spill_guard():
            return [("p2sp", 16 * max(0, state["evc"] - 1))]

        def evac_v(blk, gslot, psum, mmv):
            pr = state["evc"] % 2
            p.emit("vector", (lambda gslot=gslot, psum=psum, pr=pr: lambda e: e.tensor_scalar(
                out=ev_st[pr][:], in0=psum[:, :], scalar1=rstd_all[:, gslot:gslot + 1],
                scalar2=None, op0=ALU.mult))(),
                waits=[("p2mm", mmv)] + spill_guard(), inc=("p2ev", 1))
            p.emit("gpsimd", (lambda blk=blk, gslot=gslot, pr=pr: lambda e: e.dma_start(
                out=v_d[gslot * 128:(gslot + 1) * 128, (blk - 8) * 512:(blk - 7) * 512],
                in_=ev_st[pr][:]))(),
                waits=[("p2ev", p.now("p2ev"))], inc=("p2sp", 16))
            state["evc"] += 1

        def evac_rope(kind, blk, row, si, psum, mmv):
            pr = state["evc"] % 2
            dst = k_d if kind == "kv" else q_d

            def ev(e, psum=psum, pr=pr, si=si):
                pv = psum.rearrange("p (h two i) -> p h two i", two=2, i=64)
                ev_ = ev_st[pr].rearrange("p (h two i) -> p h two i", two=2, i=64)
                cp = cosP[si].rearrange("p (h i) -> p h i", i=64)
                sp_ = sinP[si].rearrange("p (h i) -> p h i", i=64)
                t1 = rp_t1.rearrange("p (h i) -> p h i", i=64)
                t2 = rp_t2.rearrange("p (h i) -> p h i", i=64)
                e.tensor_tensor(out=t1, in0=pv[:, :, 0, :], in1=cp, op=ALU.mult)
                e.tensor_tensor(out=t2, in0=pv[:, :, 1, :], in1=sp_, op=ALU.mult)
                e.tensor_tensor(out=ev_[:, :, 0, :], in0=t1, in1=t2, op=ALU.subtract)
                e.tensor_tensor(out=t1, in0=pv[:, :, 1, :], in1=cp, op=ALU.mult)
                e.tensor_tensor(out=t2, in0=pv[:, :, 0, :], in1=sp_, op=ALU.mult)
                return e.tensor_tensor(out=ev_[:, :, 1, :], in0=t1, in1=t2, op=ALU.add)
            p.emit("vector", ev,
                   waits=[("p2mm", mmv), ("p2rope", p.now("p2rope"))] + spill_guard(),
                   inc=("p2ev", 1))
            p.emit("gpsimd", (lambda dst=dst, row=row, blk=blk, pr=pr: lambda e: e.dma_start(
                out=dst[row * 128:(row + 1) * 128, blk * 512:(blk + 1) * 512],
                in_=ev_st[pr][:]))(),
                waits=[("p2ev", p.now("p2ev"))], inc=("p2sp", 16))
            state["evc"] += 1

        def evac_ffn(blk, gslot, jown, psum, mmv):
            pr = state["evc"] % 2
            p.emit("scalar", (lambda gslot=gslot, psum=psum, pr=pr: lambda e: e.activation(
                out=ev_st[pr][:], in_=psum[:, :], func=AF.Silu,
                scale=rstd_all[:, gslot:gslot + 1]))(),
                waits=[("p2mm", mmv)] + spill_guard(), inc=("p2ea", 1))
            p.emit("vector", (lambda gslot=gslot, pr=pr: lambda e: [
                e.tensor_tensor(out=ev_st[pr][:], in0=ev_st[pr][:], in1=rm_blk[:],
                                op=ALU.mult),
                e.tensor_scalar(out=ev_st[pr][:], in0=ev_st[pr][:],
                                scalar1=qsel_all[:, gslot:gslot + 1], scalar2=None,
                                op0=ALU.mult)][-1])(),
                waits=[("p2ea", p.now("p2ea")), ("p2rml", p.now("p2rml"))], inc=("p2ev", 1))
            p.emit("gpsimd", (lambda blk=blk, jown=jown, pr=pr: lambda e: e.dma_start(
                out=g_d[jown * 128:(jown + 1) * 128, blk * 512:(blk + 1) * 512],
                in_=ev_st[pr][:]))(),
                waits=[("p2ev", p.now("p2ev"))], inc=("p2sp", 16))
            state["evc"] += 1

        def adapter_mms(blk, par, wls, wlc):
            def amm(e, par=par):
                r = None
                for u in range(4):
                    for cc in range(NCH):
                        r = e.matmul(out=ps3[:, u * 10:(u + 1) * 10],
                                     lhsT=wblk[par][:, cc * 512 + u * 128:cc * 512 + (u + 1) * 128],
                                     rhs=adapt_fm[:, cc * 10:(cc + 1) * 10],
                                     start=(cc == 0), stop=(cc == NCH - 1),
                                     skip_group_check=True)
                return r
            p.emit("tensor", amm,
                   waits=[("ad_ev", 8), ("p2adv", p.now("p2adv")), ("w_ls", wls),
                          ("w_lc", wlc)],
                   inc=("p2ad", 1))
            state["adlast"][par] = p.now("p2ad")
            if blk < 8:
                p.emit("vector", (lambda blk=blk: lambda e: [
                    e.tensor_copy(out=ak_all[:, (blk * 4 + u) * 10:(blk * 4 + u + 1) * 10],
                                  in_=ps3[:, u * 10:(u + 1) * 10]) for u in range(4)][-1])(),
                    waits=[("p2ad", p.now("p2ad"))], inc=("p2adv", 1))
            else:
                p.emit("vector", lambda e: e.tensor_copy(out=adsc[:], in_=ps3[:, 0:40]),
                       waits=[("p2ad", p.now("p2ad"))], inc=("p2adv", 1))
                p.emit("tensor", (lambda: lambda e: [
                    e.transpose(out=ps7[0:10, u * 128:(u + 1) * 128],
                                in_=adsc[:, u * 10:(u + 1) * 10], identity=idb[:])
                    for u in range(4)][-1])(),
                    waits=[("p2adv", p.now("p2adv"))], inc=("p2ad", 1))
                p.emit("vector", (lambda blk=blk: lambda e: e.tensor_copy(
                    out=av_all[0:10, (blk - 8) * 512:(blk - 7) * 512],
                    in_=ps7[0:10, 0:512]))(),
                    waits=[("p2ad", p.now("p2ad"))], inc=("p2adv", 1))

        def load_sweep_rows(slots):
            runs = []
            st0 = slots[0]
            prev = st0
            for s in slots[1:] + [None]:
                if s is not None and s == prev + 1:
                    prev = s
                    continue
                runs.append((st0, prev))
                if s is not None:
                    st0 = prev = s
            ofs = 0
            for (a, b_) in runs:
                n = (b_ - a + 1) * 128
                p.emit("sync", (lambda a=a, n=n, ofs=ofs: lambda e: e.dma_start(
                    out=mrow_sw[:, ofs:ofs + n], in_=mrow_d[:, a * 128:a * 128 + n]))(),
                    waits=[("p2mm", p.now("p2mm")), ("p1rsp", 32 * 8)],
                    inc=("p2mr", 16))
                p.emit("sync", (lambda a=a, n=n, ofs=ofs: lambda e: e.dma_start(
                    out=sqv_sw[:, ofs:ofs + n], in_=sqv_d[:, a * 128:a * 128 + n]))(),
                    inc=("p2mr", 16))
                ofs += n
            return p.now("p2mr")

        def ph2():
            sweeps = [list(range(0, G2)), list(range(G2, 2 * G2)),
                      list(range(2 * G2, MYS))]
            for si, slots in enumerate(sweeps):
                for i, s in enumerate(slots):
                    p.emit("sync", (lambda i=i, s=s: lambda e: e.dma_start(
                        out=xg[i][:], in_=xfm_d[s, :, :]))(),
                        waits=[("p2mm", p.now("p2mm")), ("p1fsp", 16 * 512),
                               ("p1rsp", 16 * 16)],
                        inc=("p2xg", 16))
                mrn = load_sweep_rows(slots)
                for i, s in enumerate(slots):
                    slot_prologue(s, i)
                xgn = p.now("p2xg")
                ropen = p.now("p2rope")
                for bi in range(16):
                    par = bi % 2
                    wls, wlc = load_wblk("kv", bi, par, "p2mm")
                    if si == 0:
                        wbar_compute(par, wls, wlc, "p2mm", store_blk=bi)
                        adapter_mms(bi, par, wls, wlc)
                    else:
                        wbar_load(bi)
                    wbv, wbl = p.now("p2wbv"), p.now("p2wbl")
                    for s_i, s in enumerate(slots):
                        psum = ps0 if state["evc"] % 2 == 0 else ps1
                        waits = [("w_ls", wls), ("w_lc", wlc), ("p2xg", xgn),
                                 ("p2wbv", wbv), ("p2wbl", wbl), ("p2rope", ropen),
                                 ("p2mr", mrn),
                                 ("p2ev", max(0, p.now("p2ev") - 1)),
                                 ("p2ea", max(0, p.now("p2ea") - 1))]
                        mmv = slot_mms("kv", bi, par, s_i, s_i, psum, waits)
                        if bi < 8:
                            evac_rope("kv", bi, s, s_i, psum, mmv)
                        else:
                            evac_v(bi, s, psum, mmv)
                    state["wlast"][par] = p.now("p2mm")
            own = [4 * j + 3 for j in range(8)]
            for i, s in enumerate(own):
                p.emit("sync", (lambda i=i, s=s: lambda e: e.dma_start(
                    out=xg[i][:], in_=xfm_d[s, :, :]))(),
                    waits=[("p2mm", p.now("p2mm"))], inc=("p2xg", 16))
            mrn = load_sweep_rows(own)
            for i, s in enumerate(own):
                slot_prologue(s, i)
            xgn = p.now("p2xg")
            ropen = p.now("p2rope")
            for kind in ("q", "ffn"):
                for bi in range(8):
                    par = bi % 2
                    wls, wlc = load_wblk(kind, bi, par, "p2mm")
                    wbar_compute(par, wls, wlc, "p2mm")
                    wbv = p.now("p2wbv")
                    if kind == "ffn":
                        p.emit("sync", (lambda bi=bi: lambda e: e.dma_start(
                            out=bias_blk[:], in_=bias_d[:, bi * 512:(bi + 1) * 512]))(),
                            waits=[("wb_b", 16), ("p2mm", p.now("p2mm"))], inc=("p2bl", 16))
                        p.emit("sync", (lambda bi=bi: lambda e: e.dma_start(
                            out=rm_blk[:], in_=rm_d[:, bi * 512:(bi + 1) * 512]))(),
                            waits=[("c1rsp", 128), ("p2ev", p.now("p2ev"))],
                            inc=("p2rml", 16))
                    bln = p.now("p2bl")
                    rmn = p.now("p2rml")
                    for j in range(8):
                        s = own[j]
                        psum = ps0 if state["evc"] % 2 == 0 else ps1
                        waits = [("w_ls", wls), ("w_lc", wlc), ("p2xg", xgn),
                                 ("p2wbv", wbv), ("p2rope", ropen), ("p2mr", mrn),
                                 ("p2bl", bln), ("p2rml", rmn),
                                 ("p2ev", max(0, p.now("p2ev") - 1)),
                                 ("p2ea", max(0, p.now("p2ea") - 1))]
                        mmv = slot_mms(kind, bi, par, j, j, psum, waits)
                        if kind == "q":
                            evac_rope("q", bi, j, j, psum, mmv)
                        else:
                            evac_ffn(bi, s, j, psum, mmv)
                    state["wlast"][par] = p.now("p2mm")

        SP2_TOTAL = 16 * (16 * MYS + 16 * NOWN)
        kfm = [arena_bf[:, 0:4096], arena_bf[:, 4096:8192]]
        qfm = [arena_bf[:, 8192:9216], arena_bf[:, 9216:10240]]
        vsb = [arena_bf[:, 10240:14368], arena_bf[:, 14368:18496]]
        expS = arena_bf[:, 18496:36928]
        mks = arena_bf[:, 36928:41024]
        y_sb = arena_bf[:, 41216:42240]
        pa_all = arena_bf[:, 42240:42320]
        paT_all = arena_bf[0:10, 42320:43344]
        ev8 = arena_bf[:, 43344:44368]
        easc_in = arena_f[:, 0:80]
        easc = arena_f[:, 128:208]
        fac = arena_f[:, 256:264]
        sofs = []
        off = 0
        for s in range(MYS):
            w = (8 - s // 4) * 128
            sofs.append((off, w))
            off += w

        def ph3():
            p.emit("gpsimd", lambda e: e.dma_start(
                out=mks.rearrange("p (s q) -> p s q", s=32),
                in_=amask.rearrange("(s k) q -> k s q", k=128)), inc=("p3mk", 16))
            for h in range(H):
                pr = h % 2
                guard = [("p3m", MYS * max(0, h - 1)), ("p3av0", 8 * max(0, h - 1)),
                         ("p3as", 8 * max(0, h - 1))]
                p.emit("sync", (lambda h=h, pr=pr: lambda e: e.dma_start(
                    out=kfm[pr][:], in_=k_d[:, h * 128:(h + 1) * 128], transpose=True))(),
                    waits=guard + [("p2sp", SP2_TOTAL)], inc=("p3kl", 16))
                p.emit("sync", (lambda h=h, pr=pr: lambda e: e.dma_start(
                    out=qfm[pr][:], in_=q_d[:, h * 128:(h + 1) * 128], transpose=True))(),
                    inc=("p3kl", 16))
                p.emit("vector", (lambda pr=pr: lambda e: e.memset(
                    vsb[pr].rearrange("p (s w) -> p s w", w=129)[:, :, 128:129], 1.0))(),
                    waits=[("p3av0", 8 * max(0, h - 1))], inc=("p3vm", 1))
                p.emit("gpsimd", (lambda h=h, pr=pr: lambda e: e.dma_start(
                    out=vsb[pr].rearrange("p (s w) -> p s w", w=129)[:, :, 0:128],
                    in_=v_d.rearrange("(s t) c -> t s c", t=128)[:, :, h * 128:(h + 1) * 128]))(),
                    waits=[("p3vm", p.now("p3vm")), ("p2sp", SP2_TOTAL)],
                    inc=("p3vl", 16))
                kln, vln = p.now("p3kl"), p.now("p3vl")
                for s in range(MYS):
                    o0, w = sofs[s]
                    nmm = (w + 511) // 512
                    for m_i in range(nmm):
                        n0 = m_i * 512
                        n1 = min(w, n0 + 512)
                        bank = ps0 if p.now("p3s") % 2 == 0 else ps1
                        qc0 = (s // 4) * 128 + n0
                        p.emit("tensor", (lambda s=s, pr=pr, bank=bank, n0=n0, n1=n1,
                                          qc0=qc0: lambda e: e.matmul(
                                              out=bank[:, 0:n1 - n0],
                                              lhsT=kfm[pr][:, s * 128:(s + 1) * 128],
                                              rhs=qfm[pr][:, qc0:qc0 + (n1 - n0)],
                                              start=True, stop=True))(),
                               waits=[("p3kl", kln), ("p3x", max(0, p.now("p3x") - 1))],
                               inc=("p3s", 1))
                        p.emit("scalar", (lambda bank=bank, o0=o0, n0=n0, n1=n1:
                                          lambda e: e.activation(
                                              out=expS[:, o0 + n0:o0 + n1],
                                              in_=bank[:, 0:n1 - n0], func=AF.Exp,
                                              scale=INV_SQRT_HD))(),
                               waits=[("p3s", p.now("p3s"))], inc=("p3x", 1))
                    p.emit("vector", (lambda s=s, o0=o0: lambda e: e.tensor_tensor(
                        out=expS[:, o0:o0 + 128], in0=expS[:, o0:o0 + 128],
                        in1=mks[:, s * 128:(s + 1) * 128], op=ALU.mult))(),
                        waits=[("p3x", p.now("p3x")), ("p3mk", 16)], inc=("p3m", 1))
                mn = p.now("p3m")
                for j in range(8):
                    ybank = ps2 if j % 2 == 0 else ps4
                    nkt = 4 * j + 4
                    p.emit("tensor", (lambda j=j, pr=pr, ybank=ybank, nkt=nkt: lambda e: [
                        e.matmul(out=ybank[:, 0:129],
                                 lhsT=expS[:, sofs[s][0] + (j - s // 4) * 128:
                                           sofs[s][0] + (j - s // 4) * 128 + 128],
                                 rhs=vsb[pr][:, s * 129:(s + 1) * 129],
                                 start=(s == 0), stop=(s == nkt - 1),
                                 skip_group_check=True)
                        for s in range(nkt)][-1])(),
                        waits=[("p3m", mn), ("p3vl", vln),
                               ("p3ye", max(0, p.now("p3ye") - 1))],
                        inc=("p3av0", 1))
                    ywaits = [("p3av0", p.now("p3av0"))]
                    if j == 0:
                        ywaits.append(("p3ya", max(0, h - 1)))
                    p.emit("vector", (lambda j=j, ybank=ybank: lambda e: [
                        e.reciprocal(out=stat[:, 7:8], in_=ybank[:, 128:129]),
                        e.tensor_scalar(out=y_sb[:, j * 128:(j + 1) * 128],
                                        in0=ybank[:, 0:128],
                                        scalar1=stat[:, 7:8], scalar2=None,
                                        op0=ALU.mult)][-1])(),
                        waits=ywaits, inc=("p3ye", 1))
                for j in range(8):
                    p.emit("tensor", (lambda j=j, pr=pr, h=h: lambda e: e.matmul(
                        out=ps5[:, j * 10:(j + 1) * 10],
                        lhsT=qfm[pr][:, j * 128:(j + 1) * 128],
                        rhs=ak_all[:, h * 10:(h + 1) * 10], start=True, stop=True,
                        skip_group_check=True))(),
                        waits=[("p3ax", p.now("p3ax"))], inc=("p3as", 1))
                p.emit("vector", (lambda: lambda e: ([
                    e.reduce_max(out=stat[:, j:j + 1], in_=ps5[:, j * 10:(j + 1) * 10],
                                 axis=AXX, negate=True) for j in range(8)] + [
                    e.tensor_scalar(out=easc_in[:, j * 10:(j + 1) * 10],
                                    in0=ps5[:, j * 10:(j + 1) * 10],
                                    scalar1=stat[:, j:j + 1], scalar2=INV_64,
                                    op0=ALU.add, op1=ALU.mult) for j in range(8)])[-1])(),
                    waits=[("p3as", p.now("p3as"))], inc=("p3adv", 1))
                p.emit("scalar", (lambda: lambda e: e.activation(
                    out=easc[:], in_=easc_in[:], func=AF.Exp))(),
                    waits=[("p3adv", p.now("p3adv"))], inc=("p3ax", 1))
                p.emit("vector", (lambda: lambda e: ([
                    e.reduce_sum(out=stat[:, 8 + j:9 + j], in_=easc[:, j * 10:(j + 1) * 10],
                                 axis=AXX) for j in range(8)] + [
                    e.reciprocal(out=fac[:], in_=stat[:, 8:16]),
                    e.tensor_scalar(out=fac[:], in0=fac[:], scalar1=gat_bc[:, 0:1],
                                    scalar2=None, op0=ALU.mult)] + [
                    e.tensor_scalar(out=pa_all[:, j * 10:(j + 1) * 10],
                                    in0=easc[:, j * 10:(j + 1) * 10],
                                    scalar1=fac[:, j:j + 1], scalar2=None,
                                    op0=ALU.mult) for j in range(8)])[-1])(),
                    waits=[("p3ax", p.now("p3ax"))], inc=("p3adv", 1))
                p.emit("tensor", (lambda: lambda e: [
                    e.transpose(out=ps7[0:10, j * 128:(j + 1) * 128],
                                in_=pa_all[:, j * 10:(j + 1) * 10], identity=idb[:])
                    for j in range(8)][-1])(),
                    waits=[("p3adv", p.now("p3adv"))], inc=("p3at", 1))
                p.emit("vector", (lambda: lambda e: e.tensor_copy(
                    out=paT_all, in_=ps7[0:10, 0:1024]))(),
                    waits=[("p3at", p.now("p3at"))], inc=("p3adv", 1))
                p.emit("tensor", (lambda h=h: lambda e: [
                    e.matmul(out=(ps6 if j < 4 else ps3)[:, (j % 4) * 128:(j % 4 + 1) * 128],
                             lhsT=paT_all[0:10, j * 128:(j + 1) * 128],
                             rhs=av_all[0:10, h * 128:(h + 1) * 128], start=True, stop=True,
                             skip_group_check=True) for j in range(8)][-1])(),
                    waits=[("p3adv", p.now("p3adv")), ("p3ya", max(0, h - 1))],
                    inc=("p3avm", 1))
                p.emit("vector", (lambda: lambda e: [
                    e.tensor_tensor(out=ev8[:, j * 128:(j + 1) * 128],
                                    in0=y_sb[:, j * 128:(j + 1) * 128],
                                    in1=(ps6 if j < 4 else ps3)[:, (j % 4) * 128:(j % 4 + 1) * 128],
                                    op=ALU.add) for j in range(8)][-1])(),
                    waits=[("p3avm", p.now("p3avm")), ("p3ye", p.now("p3ye")),
                           ("p3ysp", 16 * 8 * h)],
                    inc=("p3ya", 1))
                yan = p.now("p3ya")
                for j in range(8):
                    p.emit("gpsimd", (lambda j=j, h=h: lambda e: e.dma_start(
                        out=y_d[j * 128:(j + 1) * 128, h * 128:(h + 1) * 128],
                        in_=ev8[:, j * 128:(j + 1) * 128]))(),
                        waits=[("p3ya", yan)], inc=("p3ysp", 16))

        yfm = arena_bf[:, 0:32768]
        gsl = [arena_bf[:, 32768 + i * 512:32768 + (i + 1) * 512] for i in range(2)]
        xsl = [arena_f[:, 0:512], arena_f[:, 512:1024]]
        osl = [arena_f[:, 1024:1536], arena_f[:, 1536:2048]]

        def ph4():
            for cc in range(NCH):
                p.emit("sync", (lambda cc=cc: lambda e: e.dma_start(
                    out=yfm[:, cc * 1024:(cc + 1) * 1024],
                    in_=y_d[:, cc * 128:(cc + 1) * 128], transpose=True))(),
                    waits=[("p3ysp", 16 * 256)], inc=("p4ys", 16))
            yl = (p.now("p4ys"), p.now("p4yc"))
            state["wlast"] = {0: 0, 1: 0}
            state["adlast"] = {0: p.now("p2ad"), 1: p.now("p2ad")}
            for bi in range(8):
                par = bi % 2
                wls, wlc = load_wblk("proj", bi, par, "p4mm")
                for j in range(8):
                    psum = ps0 if j % 2 == 0 else ps1
                    pr = j % 2
                    p.emit("sync", (lambda j=j, bi=bi, pr=pr: lambda e: e.dma_start(
                        out=gsl[pr][:],
                        in_=g_d[j * 128:(j + 1) * 128, bi * 512:(bi + 1) * 512]))(),
                        waits=[("p4o", max(0, p.now("p4o") - 1)), ("p2sp", SP2_TOTAL)],
                        inc=("p4g", 16))
                    p.emit("scalar", (lambda j=j, bi=bi, pr=pr: lambda e: e.dma_start(
                        out=xsl[pr][:], in_=xp[(4 * j + 3) * 128:(4 * j + 4) * 128,
                                               bi * 512:(bi + 1) * 512]))(),
                        waits=[("p4o", max(0, p.now("p4o") - 1))], inc=("p4x", 16))
                    gn, xn_ = p.now("p4g"), p.now("p4x")
                    p.emit("tensor", (lambda j=j, par=par, psum=psum: lambda e: [
                        e.matmul(out=psum[:, :],
                                 lhsT=yfm[:, cc * 1024 + j * 128:cc * 1024 + (j + 1) * 128],
                                 rhs=wblk[par][:, cc * 512:(cc + 1) * 512],
                                 start=(cc == 0), stop=(cc == NCH - 1))
                        for cc in range(NCH)][-1])(),
                        waits=[("p4ys", yl[0]), ("p4yc", yl[1]), ("w_ls", wls),
                               ("w_lc", wlc), ("p4o", max(0, p.now("p4o") - 1))],
                        inc=("p4mm", 1))
                    p.emit("vector", (lambda psum=psum, pr=pr: lambda e: [
                        e.tensor_tensor(out=osl[pr][:], in0=psum[:, :], in1=gsl[pr][:],
                                        op=ALU.mult),
                        e.tensor_tensor(out=osl[pr][:], in0=osl[pr][:], in1=xsl[pr][:],
                                        op=ALU.add)][-1])(),
                        waits=[("p4mm", p.now("p4mm")), ("p4g", gn), ("p4x", xn_),
                               ("p4st", 16 * max(0, p.now("p4o") - 1))],
                        inc=("p4o", 1))
                    p.emit("gpsimd", (lambda j=j, bi=bi, pr=pr: lambda e: e.dma_start(
                        out=out[j * 128:(j + 1) * 128, bi * 512:(bi + 1) * 512],
                        in_=osl[pr][:]))(),
                        waits=[("p4o", p.now("p4o"))], inc=("p4st", 16))
                state["wlast"][par] = p.now("p4mm")
            p.emit("gpsimd", None, waits=[("p4st", p.now("p4st"))])

        def phdbg():
            def d(dst, src):
                p.emit("gpsimd", (lambda dst=dst, src=src: lambda e: e.dma_start(
                    out=dst, in_=src))(), inc=("dbgs", 16))
            d(dbg_rstd, rstd_all[:])
            d(dbg_qsel, qsel_all[:])
            d(dbg_mrow, mrow_d)
            d(dbg_sqv, sqv_d)
            d(dbg_rm, rm_d)
            d(dbg_xfm, xfm_d[3, :, :])
            d(dbg_k, k_d)
            d(dbg_v, v_d)
            d(dbg_q, q_d)
            d(dbg_g, g_d)
            d(dbg_y, y_d)
            d(dbg_ak, ak_all[:])
            d(dbg_av, av_all[:])
            d(dbg_srow, srow_bc[:])
            d(dbg_xh, xhat_d[3 * 128:5 * 128, :])
            d(dbg_xl, xlo_d[3 * 128:5 * 128, :])
            p.emit("gpsimd", None, waits=[("dbgs", p.now("dbgs"))])

        def ph1():
            for tb in range(16):
                ph1a_block(tb)
                if tb >= 1:
                    ph1b_block(tb - 1)
                emit_cast_chunks((40 * tb) // 16, (40 * (tb + 1)) // 16)
            ph1a_flush()
            ph1b_block(15)
            emit_cast_chunks(40, 40)
            ph1c()

        phases = [ph0, ph1, ph2, ph3, ph4]
        for i, phf in enumerate(phases):
            if i >= upto:
                break
            phf()
            if i < len(phases) - 1:
                p.barrier()
        if dbg:
            p.barrier()
            phdbg()

        handles = {}
        guards = []
        for name in sorted(p.counts.keys()):
            g = nc.semaphore(name)
            handles[name] = g.__enter__()
            guards.append(g)
        chain_guards = []
        chain_sems = {}
        for en in CHAINED:
            g = nc.semaphore("__chain_" + en)
            chain_sems[en] = g.__enter__()
            chain_guards.append(g)
        handles.update({"__chain_" + en: chain_sems[en] for en in CHAINED})

        _CHAIN_OPS = {"tensor_copy", "tensor_tensor", "tensor_scalar",
                      "scalar_tensor_tensor", "tensor_single_scalar",
                      "tensor_reduce", "reduce_max", "reduce_sum", "reciprocal",
                      "memset", "memzero", "activation", "sqrt", "copy", "mul",
                      "add", "sign", "square", "select", "tensor_add",
                      "tensor_sub", "tensor_mul", "tensor_max"}

        class Chained:
            def __init__(self, eng, sem):
                self._eng = eng
                self._sem = sem
                self._n = 0

            def __getattr__(self, name):
                f = getattr(self._eng, name)
                if name in _CHAIN_OPS:
                    def wrapped(*a, **kw):
                        if self._n > 0:
                            self._eng.wait_ge(self._sem, self._n)
                        ins = f(*a, **kw)
                        ins.then_inc(self._sem, 1)
                        self._n += 1
                        return ins
                    return wrapped
                return f

        try:
            def run_items(eng_name):
                def body(eng):
                    if eng_name in chain_sems:
                        eng_w = Chained(eng, chain_sems[eng_name])
                    else:
                        eng_w = eng
                    for fn, waits, inc in p.items[eng_name]:
                        for s, v in waits:
                            eng.wait_ge(handles[s], v)
                        if fn is not None:
                            last = fn(eng_w)
                            if isinstance(last, list):
                                last = last[-1]
                            if inc is not None:
                                last.then_inc(handles[inc[0]], inc[1])
                        elif inc is not None:
                            eng.sem_inc(handles[inc[0]], inc[1])
                return body
            block.sync(run_items("sync"))
            block.tensor(run_items("tensor"))
            block.vector(run_items("vector"))
            block.scalar(run_items("scalar"))
            block.gpsimd(run_items("gpsimd"))
        finally:
            for g in reversed(chain_guards):
                g.__exit__(None, None, None)
            for g in reversed(guards):
                g.__exit__(None, None, None)
    _stack.close()
    return nc



def _rope_tables():
    d = np.arange(0, 128, 2, dtype=np.float64) / 128.0
    theta = 1.0 / (10000.0 ** d)
    pos = np.arange(T, dtype=np.float64)
    idx = np.outer(pos, theta)
    return np.cos(idx).astype(np.float32), np.sin(idx).astype(np.float32)


def _core_order(lane):
    order = []
    for g in range(8):
        others = [t for t in range(4 * g, 4 * g + 4) if t != 4 * g + lane]
        order.extend(others + [4 * g + lane])
    return order


def kernel(L0_attn_x, L0_attn_w_attn, L0_attn_w_proj, L0_attn_adapter_wte,
           L0_attn_gating, L0_ffn_weight, L0_ffn_bias, L0_ffn_centroids,
           L0_ffn_assignments):
    x = np.ascontiguousarray(np.asarray(L0_attn_x, np.float32))
    cosf, sinf = _rope_tables()
    tri = np.triu(np.ones((128, 128), np.float32)).astype(ml_dtypes.bfloat16)
    ones_m = np.ones((128, 128), ml_dtypes.bfloat16)
    zero_m = np.zeros((128, 128), ml_dtypes.bfloat16)

    shared = {
        "w_attn": np.ascontiguousarray(np.asarray(L0_attn_w_attn, np.float32)),
        "w_proj": np.ascontiguousarray(np.asarray(L0_attn_w_proj, np.float32)),
        "w_ffn": np.ascontiguousarray(np.asarray(L0_ffn_weight, np.float32)),
        "ffn_bias": np.ascontiguousarray(np.asarray(L0_ffn_bias, np.float32))[None, :],
        "cents": np.ascontiguousarray(np.asarray(L0_ffn_centroids, np.float32)),
        "asg": np.ascontiguousarray(np.asarray(L0_ffn_assignments, np.int32))[None, :],
        "adapter": np.ascontiguousarray(np.asarray(L0_attn_adapter_wte, np.float32)),
        "gating": np.asarray(L0_attn_gating, np.float32).reshape(1, 1),
        "ident_bf": np.eye(128, dtype=ml_dtypes.bfloat16),
        "ident_f": np.eye(128, dtype=np.float32),
        "iota64": np.arange(64, dtype=np.float32)[:, None],
        "onescol_b": np.ones((128, 1), ml_dtypes.bfloat16),
        "onescol_f": np.ones((128, 1), np.float32),
        "onesrow_f": np.ones((1, 128), np.float32),
    }

    in_maps = []
    metas = []
    for c in range(8):
        lane, seq = c % 4, c // 4
        order = _core_order(lane)
        xt = x[seq].reshape(32, 128, C)
        xo = x[1 - seq].reshape(32, 128, C)
        xp_rows = np.concatenate([xt[order], xo], axis=0).reshape(NSLOT * 128, C)
        ctab = cosf.reshape(32, 128, 64)[order].reshape(MYS * 128, 64)
        stab = sinf.reshape(32, 128, 64)[order].reshape(MYS * 128, 64)
        masks = np.empty((32, 128, 128), ml_dtypes.bfloat16)
        for s in range(32):
            ktrue = order[s]
            town = 4 * (s // 4) + lane
            if ktrue < town:
                masks[s] = ones_m
            elif ktrue == town:
                masks[s] = tri
            else:
                masks[s] = zero_m
        m = dict(shared)
        m["xp"] = np.ascontiguousarray(xp_rows)
        m["cos_tab"] = np.ascontiguousarray(ctab)
        m["sin_tab"] = np.ascontiguousarray(stab)
        m["amask"] = np.ascontiguousarray(masks.reshape(MYS * 128, 128))
        in_maps.append(m)
        metas.append((seq, lane))

    nc = build_program()
    res = run_bass_kernel_spmd(nc, in_maps, core_ids=list(range(8)))

    out_full = np.empty((B, T, C), np.float32)
    for c in range(8):
        seq, lane = metas[c]
        shard = res.results[c]["out"]
        for j in range(8):
            t0 = (4 * j + lane) * 128
            out_full[seq, t0:t0 + 128, :] = shard[j * 128:(j + 1) * 128, :]
    return out_full


if __name__ == "__main__":
    build_program()
    print("build ok")


# version 13
# speedup vs baseline: 1.0591x; 1.0551x over previous
import numpy as np
import ml_dtypes
import concourse.bass as bass
import concourse.mybir as mybir
from concourse.bass_utils import run_bass_kernel_spmd

F32 = mybir.dt.float32
BF16 = mybir.dt.bfloat16
I32 = mybir.dt.int32
AF = mybir.ActivationFunctionType
ALU = mybir.AluOpType
AXX = mybir.AxisListType.X

B, T, C = 2, 4096, 4096
H = 32
NCH = 32
MYS = 32
NSLOT = 64
NOWN = 8
EPS = 1e-5
TAU_INV = 10.0
THRESH = 0.01
INV_SQRT_HD = float(1.0 / np.sqrt(128.0))
INV_64 = float(1.0 / 64.0)
INV_C = float(1.0 / 4096.0)
G2 = 11

ENGINES = ("sync", "tensor", "vector", "scalar", "gpsimd")


class _CountStub:

    def __init__(self):
        self.n = 0

    def __getattr__(self, name):
        def f(*a, **kw):
            self.n += 1

            class _D:
                def then_inc(self, *a2, **k2):
                    return self

                def __getitem__(self, *a2):
                    return self
            return _D()
        return f


CHAINED = ("vector", "scalar")


class Plan:
    def __init__(self):
        self.items = {e: [] for e in ENGINES}
        self.counts = {}
        self.chain = {e: 0 for e in CHAINED}
        self.hist = {}

    def now(self, s):
        return self.counts.get(s, 0)

    def _xlate(self, waits):
        out = []
        for s, v in waits:
            if v <= 0:
                continue
            if s in self.hist:
                eng, marks = self.hist[s]
                out.append(("__chain_" + eng, marks[min(v, len(marks)) - 1]))
            else:
                out.append((s, v))
        return out

    def emit(self, eng, fn, waits=(), inc=None):
        waits = self._xlate(waits)
        if eng in CHAINED and fn is not None:
            is_dma = inc is not None and inc[1] % 16 == 0
            if not is_dma:
                stub = _CountStub()
                fn(stub)
                assert stub.n > 0, "chained emit with no ops"
                self.chain[eng] += stub.n
                if inc is not None:
                    name = inc[0]
                    self.counts[name] = self.counts.get(name, 0) + 1
                    self.hist.setdefault(name, (eng, []))
                    h = self.hist[name]
                    assert h[0] == eng, f"sem {name} produced by two engines"
                    h[1].append(self.chain[eng])
                self.items[eng].append((fn, waits, None))
                return
        if inc is not None and inc[1] % 16 == 0 and inc[1] > 0:
            prior = self.counts.get(inc[0], 0)
            if prior > 0:
                waits.append((inc[0], prior))
        self.items[eng].append((fn, waits, inc))
        if inc is not None:
            self.counts[inc[0]] = self.counts.get(inc[0], 0) + inc[1]

    def barrier(self, name="bar"):
        base = self.now(name)
        marks = dict(self.chain)
        for e in ENGINES:
            self.emit(e, None, inc=(name, 1))
            self.counts[name] = self.counts.get(name, 0)
        for e in ENGINES:
            w = [(name, base + 5)]
            for ce in CHAINED:
                if marks[ce] > 0 and ce != e:
                    w.append(("__chain_" + ce, marks[ce]))
            self.items[e].append((None, w, None))


def build_program(upto=99):
    import os as _os
    dbg = bool(_os.environ.get("KMK_DEBUG"))
    nc = bass.Bass("TRN2", target_bir_lowering=False)

    xp = nc.dram_tensor("xp", [NSLOT * 128, C], F32, kind="ExternalInput").ap()
    w_attn = nc.dram_tensor("w_attn", [3 * C, C], F32, kind="ExternalInput").ap()
    w_proj = nc.dram_tensor("w_proj", [C, C], F32, kind="ExternalInput").ap()
    w_ffn = nc.dram_tensor("w_ffn", [C, C], F32, kind="ExternalInput").ap()
    ffn_bias = nc.dram_tensor("ffn_bias", [1, C], F32, kind="ExternalInput").ap()
    cents = nc.dram_tensor("cents", [64, C], F32, kind="ExternalInput").ap()
    asg = nc.dram_tensor("asg", [1, C], I32, kind="ExternalInput").ap()
    adapter = nc.dram_tensor("adapter", [10, C], F32, kind="ExternalInput").ap()
    gating = nc.dram_tensor("gating", [1, 1], F32, kind="ExternalInput").ap()
    cos_tab = nc.dram_tensor("cos_tab", [MYS * 128, 64], F32, kind="ExternalInput").ap()
    sin_tab = nc.dram_tensor("sin_tab", [MYS * 128, 64], F32, kind="ExternalInput").ap()
    amask = nc.dram_tensor("amask", [MYS * 128, 128], BF16, kind="ExternalInput").ap()
    ident_bf = nc.dram_tensor("ident_bf", [128, 128], BF16, kind="ExternalInput").ap()
    ident_f = nc.dram_tensor("ident_f", [128, 128], F32, kind="ExternalInput").ap()
    iota64 = nc.dram_tensor("iota64", [64, 1], F32, kind="ExternalInput").ap()
    onescol_b = nc.dram_tensor("onescol_b", [128, 1], BF16, kind="ExternalInput").ap()
    onescol_f = nc.dram_tensor("onescol_f", [128, 1], F32, kind="ExternalInput").ap()
    onesrow_f = nc.dram_tensor("onesrow_f", [1, 128], F32, kind="ExternalInput").ap()

    out = nc.dram_tensor("out", [NOWN * 128, C], F32, kind="ExternalOutput").ap()
    if dbg:
        dbg_rstd = nc.dram_tensor("dbg_rstd", [128, NSLOT], F32, kind="ExternalOutput").ap()
        dbg_qsel = nc.dram_tensor("dbg_qsel", [128, NSLOT], F32, kind="ExternalOutput").ap()
        dbg_mrow = nc.dram_tensor("dbg_mrow", [1, MYS * 128], BF16, kind="ExternalOutput").ap()
        dbg_sqv = nc.dram_tensor("dbg_sqv", [1, MYS * 128], BF16, kind="ExternalOutput").ap()
        dbg_rm = nc.dram_tensor("dbg_rm", [128, C], BF16, kind="ExternalOutput").ap()
        dbg_xfm = nc.dram_tensor("dbg_xfm", [128, C], BF16, kind="ExternalOutput").ap()
        dbg_k = nc.dram_tensor("dbg_k", [MYS * 128, C], BF16, kind="ExternalOutput").ap()
        dbg_v = nc.dram_tensor("dbg_v", [MYS * 128, C], BF16, kind="ExternalOutput").ap()
        dbg_q = nc.dram_tensor("dbg_q", [NOWN * 128, C], BF16, kind="ExternalOutput").ap()
        dbg_g = nc.dram_tensor("dbg_g", [NOWN * 128, C], BF16, kind="ExternalOutput").ap()
        dbg_y = nc.dram_tensor("dbg_y", [NOWN * 128, C], BF16, kind="ExternalOutput").ap()
        dbg_ak = nc.dram_tensor("dbg_ak", [128, H * 10], BF16, kind="ExternalOutput").ap()
        dbg_av = nc.dram_tensor("dbg_av", [10, H * 128], BF16, kind="ExternalOutput").ap()
        dbg_srow = nc.dram_tensor("dbg_srow", [128, 64], F32, kind="ExternalOutput").ap()
        dbg_xh = nc.dram_tensor("dbg_xh", [256, C], BF16, kind="ExternalOutput").ap()
        dbg_xl = nc.dram_tensor("dbg_xl", [256, C], BF16, kind="ExternalOutput").ap()

    wattn_bf = nc.dram_tensor("wattn_bf", [3 * C, C], BF16, kind="Internal").ap()
    wproj_bf = nc.dram_tensor("wproj_bf", [C, C], BF16, kind="Internal").ap()
    wffn_bf = nc.dram_tensor("wffn_bf", [C, C], BF16, kind="Internal").ap()
    adapt_bf = nc.dram_tensor("adapt_bf", [10, C], BF16, kind="Internal").ap()
    bias_d = nc.dram_tensor("bias_d", [1, C], BF16, kind="Internal").ap()
    xhat_d = nc.dram_tensor("xhat_d", [NSLOT * 128, C], BF16, kind="Internal").ap()
    xlo_d = nc.dram_tensor("xlo_d", [NSLOT * 128, C], BF16, kind="Internal").ap()
    xfm_d = nc.dram_tensor("xfm_d", [NSLOT, 128, C], BF16, kind="Internal").ap()
    wbar_d = nc.dram_tensor("wbar_d", [16, 512], BF16, kind="Internal").ap()
    mrow_d = nc.dram_tensor("mrow_d", [1, MYS * 128], BF16, kind="Internal").ap()
    sqv_d = nc.dram_tensor("sqv_d", [1, MYS * 128], BF16, kind="Internal").ap()
    rm_d = nc.dram_tensor("rm_d", [128, C], BF16, kind="Internal").ap()
    k_d = nc.dram_tensor("k_d", [MYS * 128, C], BF16, kind="Internal").ap()
    v_d = nc.dram_tensor("v_d", [MYS * 128, C], BF16, kind="Internal").ap()
    q_d = nc.dram_tensor("q_d", [NOWN * 128, C], BF16, kind="Internal").ap()
    g_d = nc.dram_tensor("g_d", [NOWN * 128, C], BF16, kind="Internal").ap()
    y_d = nc.dram_tensor("y_d", [NOWN * 128, C], BF16, kind="Internal").ap()

    p = Plan()

    from contextlib import ExitStack
    _stack = ExitStack()
    idb = _stack.enter_context(nc.sbuf_tensor("idb", [128, 128], BF16))
    idf = _stack.enter_context(nc.sbuf_tensor("idf", [128, 128], F32))
    iota_sb = _stack.enter_context(nc.sbuf_tensor("iota_sb", [64, 1], F32))
    o_cb = _stack.enter_context(nc.sbuf_tensor("o_cb", [128, 1], BF16))
    o_cf = _stack.enter_context(nc.sbuf_tensor("o_cf", [128, 1], F32))
    o_rf = _stack.enter_context(nc.sbuf_tensor("o_rf", [1, 128], F32))
    gat1 = _stack.enter_context(nc.sbuf_tensor("gat1", [1, 1], F32))
    gat_bc = _stack.enter_context(nc.sbuf_tensor("gat_bc", [128, 1], F32))
    srow_bc = _stack.enter_context(nc.sbuf_tensor("srow_bc", [128, 64], F32))
    rstd_all = _stack.enter_context(nc.sbuf_tensor("rstd_all", [128, NSLOT], F32))
    qsel_all = _stack.enter_context(nc.sbuf_tensor("qsel_all", [128, NSLOT], F32))
    adapt_fm = _stack.enter_context(nc.sbuf_tensor("adapt_fm", [128, NCH * 10], BF16))
    ak_all = _stack.enter_context(nc.sbuf_tensor("ak_all", [128, H * 10], BF16))
    av_all = _stack.enter_context(nc.sbuf_tensor("av_all", [10, H * 128], BF16))
    stat = _stack.enter_context(nc.sbuf_tensor("stat", [128, 48], F32))
    rowsc = _stack.enter_context(nc.sbuf_tensor("rowsc", [1, 256], F32))
    mrow_pc = _stack.enter_context(nc.sbuf_tensor("mrow_pc", [1, 256], F32))
    mrow_sw = _stack.enter_context(nc.sbuf_tensor("mrow_sw", [1, G2 * 128 + 128], BF16))
    sqv_sw = _stack.enter_context(nc.sbuf_tensor("sqv_sw", [1, G2 * 128 + 128], BF16))
    bias_blk = _stack.enter_context(nc.sbuf_tensor("bias_blk", [1, 512], BF16))
    rm_blk = _stack.enter_context(nc.sbuf_tensor("rm_blk", [128, 512], BF16))
    nwbar = _stack.enter_context(nc.sbuf_tensor("nwbar", [1, 512], BF16))
    csel = _stack.enter_context(nc.sbuf_tensor("csel", [64, 1], F32))
    adsc = _stack.enter_context(nc.sbuf_tensor("adsc", [128, 40], BF16))
    asg_ic = _stack.enter_context(nc.sbuf_tensor("asg_ic", [1, 512], I32))
    arena_bf = _stack.enter_context(nc.sbuf_tensor("arena_bf", [128, 51712], BF16))
    arena_w = _stack.enter_context(nc.sbuf_tensor("arena_w", [128, 32768], BF16))
    arena_f = _stack.enter_context(nc.sbuf_tensor("arena_f", [128, 4096], F32))
    ps0 = _stack.enter_context(nc.psum_tensor("ps0", [128, 512], F32))
    ps1 = _stack.enter_context(nc.psum_tensor("ps1", [128, 512], F32))
    ps2 = _stack.enter_context(nc.psum_tensor("ps2", [128, 512], F32))
    ps3 = _stack.enter_context(nc.psum_tensor("ps3", [128, 512], F32))
    ps4 = _stack.enter_context(nc.psum_tensor("ps4", [128, 512], F32))
    ps5 = _stack.enter_context(nc.psum_tensor("ps5", [128, 512], F32))
    ps6 = _stack.enter_context(nc.psum_tensor("ps6", [128, 512], F32))
    ps7 = _stack.enter_context(nc.psum_tensor("ps7", [128, 1024], BF16))
    block = _stack.enter_context(nc.Block())
    if True:
        centhi = arena_bf[:, 36864:38976]
        centlo = arena_bf[:, 38976:41088]
        ohc = arena_f[0:64, 3072:3584]
        asg_fc = arena_f[0:1, 2560:3072]

        def ph0():
            loads = [(idb[:], ident_bf), (idf[:], ident_f), (iota_sb[:], iota64),
                     (o_cb[:], onescol_b), (o_cf[:], onescol_f), (o_rf[:], onesrow_f),
                     (gat1[:], gating)]
            for dst, src in loads:
                p.emit("sync", (lambda d=dst, s_=src: lambda e: e.dma_start(out=d, in_=s_))(),
                       inc=("c_ld", 16))
            nl = 16 * len(loads)
            p.emit("gpsimd", lambda e: e.dma_start(out=adapt_bf, in_=adapter), inc=("wb_ad", 16))
            p.emit("gpsimd", lambda e: e.dma_start(out=bias_d, in_=ffn_bias), inc=("wb_b", 16))
            cent_nat = arena_f[0:64, 0:4096]
            p.emit("sync", lambda e: e.dma_start(out=cent_nat, in_=cents), inc=("c_ld", 16))
            ad_nat = arena_bf[0:10, 0:4096]
            p.emit("sync", lambda e: e.dma_start(out=ad_nat, in_=adapt_bf),
                   waits=[("wb_ad", 16)], inc=("c_ld", 16))

            p.emit("tensor", lambda e: e.matmul(out=ps6[:, 0:1], lhsT=o_rf[:], rhs=gat1[:],
                                                start=True, stop=True),
                   waits=[("c_ld", nl)], inc=("s_t", 1))
            p.emit("vector", lambda e: e.tensor_copy(out=gat_bc[:], in_=ps6[:, 0:1]),
                   waits=[("s_t", p.now("s_t"))], inc=("s_v", 1))
            gv = p.now("s_v")

            p.emit("vector", lambda e: e.reduce_sum(out=csel[:], in_=cent_nat, axis=AXX),
                   waits=[("c_ld", nl + 16)], inc=("s_v", 1))
            p.emit("tensor", lambda e: e.transpose(out=ps5[0:1, 0:64], in_=csel[:],
                                                   identity=idf[0:64, 0:64]),
                   waits=[("s_v", p.now("s_v"))], inc=("s_t", 1))
            p.emit("vector", lambda e: e.tensor_copy(out=rowsc[:, 0:64], in_=ps5[0:1, 0:64]),
                   waits=[("s_t", p.now("s_t"))], inc=("s_v", 1))
            p.emit("tensor", lambda e: e.matmul(out=ps6[:, 0:64], lhsT=o_rf[:],
                                                rhs=rowsc[0:1, 0:64], start=True, stop=True),
                   waits=[("s_v", p.now("s_v"))], inc=("s_t", 1))
            p.emit("vector", lambda e: e.tensor_copy(out=srow_bc[:], in_=ps6[:, 0:64]),
                   waits=[("s_t", p.now("s_t"))], inc=("s_v", 1))

            for b4 in range(8):
                bank = ps4 if b4 % 2 == 0 else ps5
                p.emit("tensor", (lambda b=b4, bk=bank: lambda e: [
                    e.transpose(out=bk[:, jj * 64:jj * 64 + 64],
                                in_=cent_nat[0:64, (b * 4 + jj) * 128:(b * 4 + jj + 1) * 128],
                                identity=idf[0:64, 0:64]) for jj in range(4)][-1])(),
                    waits=[("c_ld", nl + 16), ("c_ev", max(0, b4 - 1)),
                           ("s_v", p.now("s_v"))],
                    inc=("c_tr", 1))

                def csplit(e, b=b4, bk=bank):
                    r = None
                    for jj in range(4):
                        j = b * 4 + jj
                        e.tensor_copy(out=centhi[:, j * 66:j * 66 + 64],
                                      in_=bk[:, jj * 64:jj * 64 + 64])
                        r = e.scalar_tensor_tensor(out=centlo[:, j * 66:j * 66 + 64],
                                                   in0=centhi[:, j * 66:j * 66 + 64],
                                                   scalar=-1.0,
                                                   in1=bk[:, jj * 64:jj * 64 + 64],
                                                   op0=ALU.mult, op1=ALU.add)
                    return r
                p.emit("vector", csplit, waits=[("c_tr", p.now("c_tr"))], inc=("c_ev", 1))
            p.emit("vector", (lambda: lambda e: [
                e.memset(centhi.rearrange("p (j w) -> p j w", w=66)[:, :, 64:66], 0.0),
                e.memset(centlo.rearrange("p (j w) -> p j w", w=66)[:, :, 64:66], 0.0),
                e.memset(centhi.rearrange("p (j w) -> p j w", w=66)[:, :, 64:65], 1.0)][-1])(),
                waits=[("c_ev", 8)], inc=("s_v", 1))

            for b4 in range(8):
                p.emit("tensor", (lambda b=b4: lambda e: [
                    e.transpose(out=ps7[:, jj * 10:jj * 10 + 10],
                                in_=ad_nat[0:10, (b * 4 + jj) * 128:(b * 4 + jj + 1) * 128],
                                identity=idb[0:10, 0:10]) for jj in range(4)][-1])(),
                    waits=[("c_ld", nl + 32), ("ad_ev", b4)], inc=("ad_tr", 1))
                p.emit("vector", (lambda b=b4: lambda e: e.tensor_copy(
                    out=adapt_fm[:, b * 40:(b + 1) * 40], in_=ps7[:, 0:40]))(),
                    waits=[("ad_tr", p.now("ad_tr"))], inc=("ad_ev", 1))

        xtm = [arena_f[:, i * 512:(i + 1) * 512] for i in range(4)]
        xhat_t = [arena_bf[:, i * 4096:(i + 1) * 4096] for i in range(4)]
        xlo_t = [arena_bf[:, 16384 + i * 4096:16384 + (i + 1) * 4096] for i in range(4)]

        ph1a_pend = [None]

        def ph1a_flush():
            if ph1a_pend[0] is not None:
                ph1a_pend[0]()
                ph1a_pend[0] = None

        def ph1a_block(tb):
            for s in range(4 * tb, 4 * tb + 4):
                spr = s % 4
                for qq in range(8):
                    idx = s * 8 + qq
                    p.emit("scalar", (lambda s=s, qq=qq, idx=idx: lambda e: e.dma_start(
                        out=xtm[idx % 4][:],
                        in_=xp[s * 128:(s + 1) * 128, qq * 512:(qq + 1) * 512]))(),
                        waits=[("p1cast", max(0, idx - 3))], inc=("p1xld", 16))
                    xldv = p.now("p1xld")
                    p.emit("vector", (lambda spr=spr, qq=qq, idx=idx: lambda e: e.tensor_copy(
                        out=xhat_t[spr][:, qq * 512:(qq + 1) * 512], in_=xtm[idx % 4][:]))(),
                        waits=[("p1xld", xldv), ("p1sp", 32 * max(0, s - 3))])
                    p.emit("vector", (lambda spr=spr, qq=qq, idx=idx: lambda e: e.tensor_tensor(
                        out=xlo_t[spr][:, qq * 512:(qq + 1) * 512], in0=xtm[idx % 4][:],
                        in1=xhat_t[spr][:, qq * 512:(qq + 1) * 512], op=ALU.subtract))(),
                        inc=("p1cast", 1))
                ph1a_flush()
                cv = p.now("p1cast")

                def spill(s=s, spr=spr, cv=cv):
                    p.emit("scalar", (lambda s=s, spr=spr, cv=cv: lambda e: e.dma_start(
                        out=xhat_d[s * 128:(s + 1) * 128, :], in_=xhat_t[spr][:]))(),
                        waits=[("p1cast", cv)], inc=("p1sp", 16))
                    p.emit("scalar", (lambda s=s, spr=spr: lambda e: e.dma_start(
                        out=xlo_d[s * 128:(s + 1) * 128, :], in_=xlo_t[spr][:]))(),
                        inc=("p1sp", 16))
                ph1a_pend[0] = spill

        thi = [arena_bf[:, 32768 + i * 512:32768 + (i + 1) * 512] for i in range(4)]
        tlo = [arena_bf[:, 34816 + i * 512:34816 + (i + 1) * 512] for i in range(4)]
        mrow_pcb = arena_bf[0:1, 41088:42112]
        zt_sb = arena_f[:, 2048:2560]
        ztm4 = arena_f[:, 2560:2824]
        expz4 = arena_f[:, 2824:3080]
        diag_sc = arena_f[:, 3080:3208]

        def ph1b_block(tb):
            if True:
                zbank = ps0 if tb % 2 == 0 else ps5
                dbank = ps1 if tb % 2 == 0 else ps6
                for cc in range(NCH):
                    pr = (tb * NCH + cc) % 4
                    idx = tb * NCH + cc
                    p.emit("sync", (lambda tb=tb, cc=cc, pr=pr: lambda e: e.dma_start(
                        out=thi[pr][:],
                        in_=xhat_d[tb * 512:(tb + 1) * 512, cc * 128:(cc + 1) * 128],
                        transpose=True))(),
                        waits=[("p1mm", max(0, idx - 3)), ("p1fsp", 16 * max(0, idx - 3))] +
                              ([("p1sp", 32 * (4 * tb + 4))] if cc == 0 else []),
                        inc=("p1trd", 16))
                    p.emit("sync", (lambda tb=tb, cc=cc, pr=pr: lambda e: e.dma_start(
                        out=tlo[pr][:],
                        in_=xlo_d[tb * 512:(tb + 1) * 512, cc * 128:(cc + 1) * 128],
                        transpose=True))(),
                        waits=[("p1mm", max(0, idx - 3))] +
                              ([("p1sp", 32 * (4 * tb + 4))] if cc == 0 else []),
                        inc=("p1trd2", 16))
                    trd, trd2 = p.now("p1trd"), p.now("p1trd2")
                    mm_waits = [("p1trd", trd), ("p1trd2", trd2)]
                    if cc == 0:
                        mm_waits += [("p1zev", max(0, tb - 1)), ("p1v1", max(0, tb - 1))]

                    def mmz(e, cc=cc, pr=pr, zbank=zbank, dbank=dbank):
                        first = (cc == 0)
                        last_ = (cc == NCH - 1)
                        e.matmul(out=zbank[0:66, :], lhsT=centhi[:, cc * 66:(cc + 1) * 66],
                                 rhs=thi[pr][:], start=first, stop=False,
                                 skip_group_check=True)
                        e.matmul(out=zbank[0:66, :], lhsT=centlo[:, cc * 66:(cc + 1) * 66],
                                 rhs=thi[pr][:], start=False, stop=False,
                                 skip_group_check=True)
                        e.matmul(out=zbank[0:66, :], lhsT=centhi[:, cc * 66:(cc + 1) * 66],
                                 rhs=tlo[pr][:], start=False, stop=last_,
                                 skip_group_check=True)
                        r = None
                        for u in range(4):
                            r = e.matmul(out=dbank[:, u * 128:(u + 1) * 128],
                                         lhsT=thi[pr][:, u * 128:(u + 1) * 128],
                                         rhs=thi[pr][:, u * 128:(u + 1) * 128],
                                         start=(first and u == 0), stop=last_,
                                         skip_group_check=True)
                        return r
                    p.emit("tensor", mmz, waits=mm_waits, inc=("p1mm", 1))
                    p.emit("gpsimd", (lambda tb=tb, cc=cc, pr=pr: lambda e: e.dma_start(
                        out=xfm_d[tb * 4:(tb + 1) * 4, :, cc * 128:(cc + 1) * 128]
                            .rearrange("s p t -> p s t"),
                        in_=thi[pr][:].rearrange("p (s t) -> p s t", s=4)))(),
                        waits=[("p1trd", trd)], inc=("p1fsp", 16))

                mmv = p.now("p1mm")
                p.emit("vector", (lambda zbank=zbank: lambda e: e.tensor_copy(
                    out=zt_sb[0:66, :], in_=zbank[0:66, :]))(),
                       waits=[("p1mm", mmv), ("p1ztr", tb)], inc=("p1zev", 1))
                zev = p.now("p1zev")
                p.emit("tensor", (lambda: lambda e: [
                    e.transpose(out=ps2[:, u * 66:(u + 1) * 66],
                                in_=zt_sb[0:66, u * 128:(u + 1) * 128],
                                identity=idf[0:66, 0:66]) for u in range(4)][-1])(),
                    waits=[("p1zev", zev), ("p1v1", tb), ("p1v4", min(tb, 8))],
                    inc=("p1ztr", 1))
                ztr = p.now("p1ztr")

                def vstA(e, tb=tb, dbank=dbank):
                    for u in range(4):
                        e.tensor_tensor(out=diag_sc[:], in0=dbank[:, u * 128:(u + 1) * 128],
                                        in1=idf[:], op=ALU.mult)
                        e.reduce_sum(out=stat[:, u:u + 1], in_=diag_sc[:], axis=AXX)
                    e.tensor_copy(out=ztm4[:], in_=ps2[:, 0:264])
                    for u in range(4):
                        e.tensor_scalar(out=stat[:, 4 + u:5 + u], in0=ztm4[:, u * 66 + 64:u * 66 + 65],
                                        scalar1=INV_C, scalar2=None, op0=ALU.mult)
                    e.tensor_scalar(out=stat[:, 8:12], in0=stat[:, 0:4], scalar1=INV_C,
                                    scalar2=None, op0=ALU.mult)
                    e.tensor_tensor(out=stat[:, 12:16], in0=stat[:, 4:8], in1=stat[:, 4:8],
                                    op=ALU.mult)
                    e.tensor_tensor(out=stat[:, 8:12], in0=stat[:, 8:12], in1=stat[:, 12:16],
                                    op=ALU.subtract)
                    return e.tensor_scalar(out=stat[:, 8:12], in0=stat[:, 8:12], scalar1=EPS,
                                           scalar2=None, op0=ALU.add)
                p.emit("vector", vstA,
                       waits=[("p1mm", mmv), ("p1ztr", ztr), ("p1a1", tb), ("p1a2", tb)],
                       inc=("p1v1", 1))
                v1 = p.now("p1v1")
                p.emit("scalar", (lambda: lambda e: [
                    e.activation(out=stat[:, 12:16], in_=stat[:, 8:12], func=AF.Ln),
                    e.activation(out=stat[:, 16:20], in_=stat[:, 12:16], func=AF.Exp,
                                 scale=0.5)][-1])(),
                    waits=[("p1v1", v1), ("p1rtr", p.now("p1rtr"))], inc=("p1a1", 1))
                a1 = p.now("p1a1")

                def vstB(e, tb=tb):
                    e.reciprocal(out=rstd_all[:, 4 * tb:4 * tb + 4], in_=stat[:, 16:20])
                    e.tensor_scalar(out=stat[:, 20:24], in0=stat[:, 4:8], scalar1=-1.0,
                                    scalar2=None, op0=ALU.mult)
                    for u in range(4):
                        e.scalar_tensor_tensor(out=ztm4[:, u * 66:u * 66 + 64], in0=srow_bc[:],
                                               scalar=stat[:, 20 + u:21 + u],
                                               in1=ztm4[:, u * 66:u * 66 + 64],
                                               op0=ALU.mult, op1=ALU.add)
                    for u in range(4):
                        e.reduce_max(out=stat[:, 24 + u:25 + u], in_=ztm4[:, u * 66:u * 66 + 64],
                                     axis=AXX, negate=True)
                    e.tensor_scalar(out=stat[:, 28:32], in0=rstd_all[:, 4 * tb:4 * tb + 4],
                                    scalar1=TAU_INV, scalar2=None, op0=ALU.mult)
                    return e.tensor_tensor(out=stat[:, 32:36], in0=stat[:, 24:28],
                                           in1=stat[:, 28:32], op=ALU.mult)
                p.emit("vector", vstB, waits=[("p1a1", a1)], inc=("p1v2", 1))
                v2 = p.now("p1v2")
                p.emit("scalar", (lambda: lambda e: [
                    e.activation(out=expz4[:, u * 64:(u + 1) * 64],
                                 in_=ztm4[:, u * 66:u * 66 + 64], func=AF.Exp,
                                 bias=stat[:, 32 + u:33 + u], scale=stat[:, 28 + u:29 + u])
                    for u in range(4)][-1])(),
                    waits=[("p1v2", v2), ("p1cnt", tb)], inc=("p1a2", 1))
                a2 = p.now("p1a2")

                def vstC(e, tb=tb):
                    for u in range(4):
                        e.reduce_sum(out=stat[:, 36 + u:37 + u], in_=expz4[:, u * 64:(u + 1) * 64],
                                     axis=AXX)
                    e.tensor_scalar(out=stat[:, 40:44], in0=stat[:, 36:40], scalar1=THRESH,
                                    scalar2=None, op0=ALU.mult)
                    for u in range(4):
                        e.tensor_scalar(out=expz4[:, u * 64:(u + 1) * 64],
                                        in0=expz4[:, u * 64:(u + 1) * 64],
                                        scalar1=stat[:, 40 + u:41 + u], scalar2=None,
                                        op0=ALU.is_gt)
                    r = None
                    for u in range(4):
                        r = e.reduce_max(out=qsel_all[:, 4 * tb + u:4 * tb + u + 1],
                                         in_=expz4[:, u * 64:(u + 1) * 64], axis=AXX)
                    return r
                p.emit("vector", vstC, waits=[("p1a2", a2)], inc=("p1v3", 1))
                v3 = p.now("p1v3")
                p.emit("tensor", (lambda tb=tb: lambda e: [
                    e.matmul(out=ps3[0:1, 0:64], lhsT=o_cf[:],
                             rhs=expz4[:, u * 64:(u + 1) * 64],
                             start=(tb == 0 and u == 0), stop=(tb == 15 and u == 3),
                             skip_group_check=True) for u in range(4)][-1])(),
                    waits=[("p1v3", v3)], inc=("p1cnt", 1))
                if tb < 8:
                    p.emit("tensor", (lambda: lambda e: ([
                        e.transpose(out=ps4[0:1, u * 128:(u + 1) * 128],
                                    in_=stat[:, 4 + u:5 + u], identity=idf[:])
                        for u in range(4)] + [
                        e.transpose(out=ps2[0:1, u * 128:(u + 1) * 128],
                                    in_=stat[:, 16 + u:17 + u], identity=idf[:])
                        for u in range(4)])[-1])(),
                        waits=[("p1v2", v2), ("p1v4", tb)], inc=("p1rtr", 1))
                    rr = p.now("p1rtr")
                    p.emit("vector", (lambda: lambda e: [
                        e.tensor_copy(out=mrow_pcb[0:1, 0:512], in_=ps4[0:1, 0:512]),
                        e.tensor_copy(out=mrow_pcb[0:1, 512:1024], in_=ps2[0:1, 0:512]),
                    ][-1])(), waits=[("p1rtr", rr), ("p1rsp", 32 * tb)],
                        inc=("p1v4", 1))
                    v4 = p.now("p1v4")
                    p.emit("gpsimd", (lambda tb=tb: lambda e: e.dma_start(
                        out=mrow_d[:, tb * 512:(tb + 1) * 512], in_=mrow_pcb[0:1, 0:512]))(),
                        waits=[("p1v4", v4)], inc=("p1rsp", 16))
                    p.emit("gpsimd", (lambda tb=tb: lambda e: e.dma_start(
                        out=sqv_d[:, tb * 512:(tb + 1) * 512], in_=mrow_pcb[0:1, 512:1024]))(),
                        inc=("p1rsp", 16))

        def ph1c():
            p.emit("vector", lambda e: e.tensor_copy(out=rowsc[:, 64:128], in_=ps3[0:1, 0:64]),
                   waits=[("p1cnt", 16)], inc=("c1v", 1))
            p.emit("tensor", lambda e: e.transpose(out=ps4[0:64, 0:1],
                                                   in_=rowsc[0:1, 64:128][:, 0:64],
                                                   identity=idf[0:1, 0:1]),
                   waits=[("c1v", p.now("c1v")), ("p1v4", p.now("p1v4"))], inc=("c1t", 1))
            p.emit("vector", (lambda: lambda e: [
                e.tensor_copy(out=csel[:], in_=ps4[0:64, 0:1]),
                e.tensor_scalar(out=csel[:], in0=csel[:], scalar1=0.5, scalar2=None,
                                op0=ALU.is_gt)][-1])(),
                waits=[("c1t", p.now("c1t"))], inc=("c1v", 1))
            for j in range(8):
                p.emit("sync", (lambda j=j: lambda e: e.dma_start(
                    out=asg_ic[:], in_=asg[:, j * 512:(j + 1) * 512]))(),
                    waits=[("c1oh", 2 * j)], inc=("c1al", 16))
                p.emit("vector", (lambda: lambda e: e.tensor_copy(
                    out=asg_fc[:], in_=asg_ic[:]))(),
                    waits=[("c1al", p.now("c1al")), ("c1mm", p.now("c1mm"))], inc=("c1v", 1))
                p.emit("tensor", (lambda: lambda e: e.matmul(
                    out=ps5[0:64, :], lhsT=o_rf[:, 0:64], rhs=asg_fc[:],
                    start=True, stop=True))(),
                    waits=[("c1v", p.now("c1v")), ("c1oh", 2 * j), ("p1zev", 16)],
                    inc=("c1mm", 1))
                p.emit("vector", (lambda: lambda e: [
                    e.tensor_copy(out=ohc[:], in_=ps5[0:64, :]),
                    e.tensor_scalar(out=ohc[:], in0=ohc[:], scalar1=iota_sb[:, 0:1],
                                    scalar2=None, op0=ALU.is_equal)][-1])(),
                    waits=[("c1mm", p.now("c1mm"))], inc=("c1oh", 1))
                p.emit("tensor", (lambda: lambda e: e.matmul(
                    out=ps5[0:1, :], lhsT=csel[:], rhs=ohc[:], start=True, stop=True))(),
                    waits=[("c1oh", p.now("c1oh")), ("c1v", p.now("c1v"))], inc=("c1mm", 1))
                p.emit("vector", (lambda: lambda e: e.tensor_copy(
                    out=ohc[0:1, :], in_=ps5[0:1, :]))(),
                    waits=[("c1mm", p.now("c1mm"))], inc=("c1v", 1))
                p.emit("tensor", (lambda: lambda e: e.matmul(
                    out=ps6[:, :], lhsT=o_rf[:], rhs=ohc[0:1, :], start=True, stop=True))(),
                    waits=[("c1v", p.now("c1v")), ("p1v1", 16)], inc=("c1mm", 1))
                p.emit("vector", (lambda: lambda e: e.tensor_copy(
                    out=rm_blk[:], in_=ps6[:, :]))(),
                    waits=[("c1mm", p.now("c1mm")), ("c1rsp", 16 * j)], inc=("c1oh", 1))
                p.emit("gpsimd", (lambda j=j: lambda e: e.dma_start(
                    out=rm_d[:, j * 512:(j + 1) * 512], in_=rm_blk[:]))(),
                    waits=[("c1oh", p.now("c1oh"))], inc=("c1rsp", 16))

        xg = [arena_bf[:, i * 4096:(i + 1) * 4096] for i in range(G2)]
        ev_st = [arena_bf[:, 45056 + i * 512:45056 + (i + 1) * 512] for i in range(2)]
        wblk = [arena_w[:, 0:16384], arena_w[:, 16384:32768]]
        rp_t1 = arena_f[:, 0:256]
        rp_t2 = arena_f[:, 256:512]
        costl = arena_f[:, 512:576]
        sintl = arena_f[:, 576:640]
        cosP = [arena_bf[:, 46080 + i * 256:46080 + (i + 1) * 256] for i in range(G2)]
        sinP = [arena_bf[:, 48896 + i * 256:48896 + (i + 1) * 256] for i in range(G2)]

        state = {"evc": 0, "wlast": {0: 0, 1: 0}, "adlast": {0: 0, 1: 0}}

        CAST_CHUNKS = ([(wattn_bf, w_attn, 4096 + i * 512) for i in range(16)] +
                       [(wattn_bf, w_attn, i * 512) for i in range(8)] +
                       [(wffn_bf, w_ffn, i * 512) for i in range(8)] +
                       [(wproj_bf, w_proj, i * 512) for i in range(8)])

        def emit_cast_chunks(lo, hi):
            for ci in range(lo, min(hi, len(CAST_CHUNKS))):
                dst, srcw, r0 = CAST_CHUNKS[ci]
                p.emit("gpsimd", (lambda dst=dst, srcw=srcw, r0=r0: lambda e: e.dma_start(
                    out=dst[r0:r0 + 512, :], in_=srcw[r0:r0 + 512, :]))(),
                    inc=("wbc", 16))

        def w_src(kind, blk):
            if kind == "kv":
                return wattn_bf, 4096 + blk * 512, 16 * (blk + 1)
            if kind == "q":
                return wattn_bf, blk * 512, 16 * (17 + blk)
            if kind == "ffn":
                return wffn_bf, blk * 512, 16 * (25 + blk)
            return wproj_bf, blk * 512, 16 * (33 + blk)

        def load_wblk(kind, blk, par, mm_sem):
            src, ro, wneed = w_src(kind, blk)
            guard = [(mm_sem, state["wlast"][par]), ("p2ad", state["adlast"][par])]
            for cc in range(NCH):
                p.emit("sync", (lambda src=src, ro=ro, cc=cc, par=par: lambda e: e.dma_start(
                    out=wblk[par][:, cc * 512:(cc + 1) * 512],
                    in_=src[ro:ro + 512, cc * 128:(cc + 1) * 128], transpose=True))(),
                    waits=[("wbc", wneed)] + (guard if cc < 1 else []), inc=("w_ls", 16))
            return p.now("w_ls"), p.now("w_lc")

        def wbar_compute(par, wls, wlc, mm_sem, store_blk=None):
            p.emit("tensor", (lambda par=par: lambda e: [
                e.matmul(out=ps2[0:1, :], lhsT=o_cb[:],
                         rhs=wblk[par][:, cc * 512:(cc + 1) * 512],
                         start=(cc == 0), stop=(cc == NCH - 1)) for cc in range(NCH)][-1])(),
                waits=[("w_ls", wls), ("w_lc", wlc), ("p2wbv", p.now("p2wbv"))],
                inc=("p2wb", 1))
            p.emit("vector", lambda e: e.tensor_scalar(out=nwbar[:], in0=ps2[0:1, :],
                                                       scalar1=-1.0, scalar2=None,
                                                       op0=ALU.mult),
                   waits=[("p2wb", p.now("p2wb")), (mm_sem, p.now(mm_sem)),
                          ("p2wbsp", p.now("p2wbsp"))],
                   inc=("p2wbv", 1))
            if store_blk is not None:
                p.emit("gpsimd", (lambda b=store_blk: lambda e: e.dma_start(
                    out=wbar_d[b:b + 1, :], in_=nwbar[:]))(),
                    waits=[("p2wbv", p.now("p2wbv"))], inc=("p2wbsp", 16))

        def wbar_load(blk):
            p.emit("sync", (lambda b=blk: lambda e: e.dma_start(
                out=nwbar[:], in_=wbar_d[b:b + 1, :]))(),
                waits=[("p2wbsp", 16 * 16), ("p2mm", p.now("p2mm"))],
                inc=("p2wbl", 16))

        def slot_prologue(gslot, si):
            p.emit("sync", (lambda s=gslot: lambda e: e.dma_start(
                out=costl[:], in_=cos_tab[s * 128:(s + 1) * 128, :]))(),
                waits=[("p2rope", p.now("p2rope"))], inc=("p2cs", 16))
            p.emit("sync", (lambda s=gslot: lambda e: e.dma_start(
                out=sintl[:], in_=sin_tab[s * 128:(s + 1) * 128, :]))(),
                inc=("p2cs", 16))
            cs = p.now("p2cs")
            def mkrep(e, s=gslot, si=si):
                e.tensor_scalar(out=rp_t1[:, 0:64], in0=costl[:],
                                scalar1=rstd_all[:, s:s + 1], scalar2=None, op0=ALU.mult)
                e.tensor_scalar(out=rp_t2[:, 0:64], in0=sintl[:],
                                scalar1=rstd_all[:, s:s + 1], scalar2=None, op0=ALU.mult)
                for r in range(4):
                    e.tensor_copy(out=cosP[si][:, r * 64:(r + 1) * 64], in_=rp_t1[:, 0:64])
                r_ = None
                for r in range(4):
                    r_ = e.tensor_copy(out=sinP[si][:, r * 64:(r + 1) * 64],
                                       in_=rp_t2[:, 0:64])
                return r_
            p.emit("vector", mkrep, waits=[("p2cs", cs)], inc=("p2rope", 1))

        def slot_mms(kind, blk, par, xg_i, sw_i, psum, extra):
            def mm(e, kind=kind, blk=blk, par=par, xg_i=xg_i, sw_i=sw_i, psum=psum):
                for cc in range(NCH):
                    e.matmul(out=psum[:, :], lhsT=xg[xg_i][:, cc * 128:(cc + 1) * 128],
                             rhs=wblk[par][:, cc * 512:(cc + 1) * 512],
                             start=(cc == 0), stop=False)
                r = e.matmul(out=psum[:, :], lhsT=mrow_sw[:, sw_i * 128:(sw_i + 1) * 128],
                             rhs=nwbar[:], start=False, stop=(kind != "ffn"))
                if kind == "ffn":
                    r = e.matmul(out=psum[:, :],
                                 lhsT=sqv_sw[:, sw_i * 128:(sw_i + 1) * 128],
                                 rhs=bias_blk[:], start=False, stop=True)
                return r
            p.emit("tensor", mm, waits=extra, inc=("p2mm", 1))
            return p.now("p2mm")

        def spill_guard():
            return [("p2sp", 16 * max(0, state["evc"] - 1))]

        def evac_v(blk, gslot, psum, mmv):
            pr = state["evc"] % 2
            p.emit("vector", (lambda gslot=gslot, psum=psum, pr=pr: lambda e: e.tensor_scalar(
                out=ev_st[pr][:], in0=psum[:, :], scalar1=rstd_all[:, gslot:gslot + 1],
                scalar2=None, op0=ALU.mult))(),
                waits=[("p2mm", mmv)] + spill_guard(), inc=("p2ev", 1))
            p.emit("gpsimd", (lambda blk=blk, gslot=gslot, pr=pr: lambda e: e.dma_start(
                out=v_d[gslot * 128:(gslot + 1) * 128, (blk - 8) * 512:(blk - 7) * 512],
                in_=ev_st[pr][:]))(),
                waits=[("p2ev", p.now("p2ev"))], inc=("p2sp", 16))
            state["evc"] += 1

        def evac_rope(kind, blk, row, si, psum, mmv):
            pr = state["evc"] % 2
            dst = k_d if kind == "kv" else q_d

            def ev(e, psum=psum, pr=pr, si=si):
                pv = psum.rearrange("p (h two i) -> p h two i", two=2, i=64)
                ev_ = ev_st[pr].rearrange("p (h two i) -> p h two i", two=2, i=64)
                cp = cosP[si].rearrange("p (h i) -> p h i", i=64)
                sp_ = sinP[si].rearrange("p (h i) -> p h i", i=64)
                t1 = rp_t1.rearrange("p (h i) -> p h i", i=64)
                t2 = rp_t2.rearrange("p (h i) -> p h i", i=64)
                e.tensor_tensor(out=t1, in0=pv[:, :, 0, :], in1=cp, op=ALU.mult)
                e.tensor_tensor(out=t2, in0=pv[:, :, 1, :], in1=sp_, op=ALU.mult)
                e.tensor_tensor(out=ev_[:, :, 0, :], in0=t1, in1=t2, op=ALU.subtract)
                e.tensor_tensor(out=t1, in0=pv[:, :, 1, :], in1=cp, op=ALU.mult)
                e.tensor_tensor(out=t2, in0=pv[:, :, 0, :], in1=sp_, op=ALU.mult)
                return e.tensor_tensor(out=ev_[:, :, 1, :], in0=t1, in1=t2, op=ALU.add)
            p.emit("vector", ev,
                   waits=[("p2mm", mmv), ("p2rope", p.now("p2rope"))] + spill_guard(),
                   inc=("p2ev", 1))
            p.emit("gpsimd", (lambda dst=dst, row=row, blk=blk, pr=pr: lambda e: e.dma_start(
                out=dst[row * 128:(row + 1) * 128, blk * 512:(blk + 1) * 512],
                in_=ev_st[pr][:]))(),
                waits=[("p2ev", p.now("p2ev"))], inc=("p2sp", 16))
            state["evc"] += 1

        def evac_ffn(blk, gslot, jown, psum, mmv):
            pr = state["evc"] % 2
            p.emit("scalar", (lambda gslot=gslot, psum=psum, pr=pr: lambda e: e.activation(
                out=ev_st[pr][:], in_=psum[:, :], func=AF.Silu,
                scale=rstd_all[:, gslot:gslot + 1]))(),
                waits=[("p2mm", mmv)] + spill_guard(), inc=("p2ea", 1))
            p.emit("vector", (lambda gslot=gslot, pr=pr: lambda e: [
                e.tensor_tensor(out=ev_st[pr][:], in0=ev_st[pr][:], in1=rm_blk[:],
                                op=ALU.mult),
                e.tensor_scalar(out=ev_st[pr][:], in0=ev_st[pr][:],
                                scalar1=qsel_all[:, gslot:gslot + 1], scalar2=None,
                                op0=ALU.mult)][-1])(),
                waits=[("p2ea", p.now("p2ea")), ("p2rml", p.now("p2rml"))], inc=("p2ev", 1))
            p.emit("gpsimd", (lambda blk=blk, jown=jown, pr=pr: lambda e: e.dma_start(
                out=g_d[jown * 128:(jown + 1) * 128, blk * 512:(blk + 1) * 512],
                in_=ev_st[pr][:]))(),
                waits=[("p2ev", p.now("p2ev"))], inc=("p2sp", 16))
            state["evc"] += 1

        def adapter_mms(blk, par, wls, wlc):
            def amm(e, par=par):
                r = None
                for u in range(4):
                    for cc in range(NCH):
                        r = e.matmul(out=ps3[:, u * 10:(u + 1) * 10],
                                     lhsT=wblk[par][:, cc * 512 + u * 128:cc * 512 + (u + 1) * 128],
                                     rhs=adapt_fm[:, cc * 10:(cc + 1) * 10],
                                     start=(cc == 0), stop=(cc == NCH - 1),
                                     skip_group_check=True)
                return r
            p.emit("tensor", amm,
                   waits=[("ad_ev", 8), ("p2adv", p.now("p2adv")), ("w_ls", wls),
                          ("w_lc", wlc)],
                   inc=("p2ad", 1))
            state["adlast"][par] = p.now("p2ad")
            if blk < 8:
                p.emit("vector", (lambda blk=blk: lambda e: [
                    e.tensor_copy(out=ak_all[:, (blk * 4 + u) * 10:(blk * 4 + u + 1) * 10],
                                  in_=ps3[:, u * 10:(u + 1) * 10]) for u in range(4)][-1])(),
                    waits=[("p2ad", p.now("p2ad"))], inc=("p2adv", 1))
            else:
                p.emit("vector", lambda e: e.tensor_copy(out=adsc[:], in_=ps3[:, 0:40]),
                       waits=[("p2ad", p.now("p2ad"))], inc=("p2adv", 1))
                p.emit("tensor", (lambda: lambda e: [
                    e.transpose(out=ps7[0:10, u * 128:(u + 1) * 128],
                                in_=adsc[:, u * 10:(u + 1) * 10], identity=idb[:])
                    for u in range(4)][-1])(),
                    waits=[("p2adv", p.now("p2adv"))], inc=("p2ad", 1))
                p.emit("vector", (lambda blk=blk: lambda e: e.tensor_copy(
                    out=av_all[0:10, (blk - 8) * 512:(blk - 7) * 512],
                    in_=ps7[0:10, 0:512]))(),
                    waits=[("p2ad", p.now("p2ad"))], inc=("p2adv", 1))

        def load_sweep_rows(slots):
            runs = []
            st0 = slots[0]
            prev = st0
            for s in slots[1:] + [None]:
                if s is not None and s == prev + 1:
                    prev = s
                    continue
                runs.append((st0, prev))
                if s is not None:
                    st0 = prev = s
            ofs = 0
            for (a, b_) in runs:
                n = (b_ - a + 1) * 128
                p.emit("sync", (lambda a=a, n=n, ofs=ofs: lambda e: e.dma_start(
                    out=mrow_sw[:, ofs:ofs + n], in_=mrow_d[:, a * 128:a * 128 + n]))(),
                    waits=[("p2mm", p.now("p2mm")), ("p1rsp", 32 * 8)],
                    inc=("p2mr", 16))
                p.emit("sync", (lambda a=a, n=n, ofs=ofs: lambda e: e.dma_start(
                    out=sqv_sw[:, ofs:ofs + n], in_=sqv_d[:, a * 128:a * 128 + n]))(),
                    inc=("p2mr", 16))
                ofs += n
            return p.now("p2mr")

        def ph2():
            sweeps = [list(range(0, G2)), list(range(G2, 2 * G2)),
                      list(range(2 * G2, MYS))]
            for si, slots in enumerate(sweeps):
                for i, s in enumerate(slots):
                    p.emit("sync", (lambda i=i, s=s: lambda e: e.dma_start(
                        out=xg[i][:], in_=xfm_d[s, :, :]))(),
                        waits=[("p2mm", p.now("p2mm")), ("p1fsp", 16 * 512),
                               ("p1rsp", 16 * 16)],
                        inc=("p2xg", 16))
                mrn = load_sweep_rows(slots)
                for i, s in enumerate(slots):
                    slot_prologue(s, i)
                xgn = p.now("p2xg")
                ropen = p.now("p2rope")
                for bi in range(16):
                    par = bi % 2
                    wls, wlc = load_wblk("kv", bi, par, "p2mm")
                    if si == 0:
                        wbar_compute(par, wls, wlc, "p2mm", store_blk=bi)
                        adapter_mms(bi, par, wls, wlc)
                    else:
                        wbar_load(bi)
                    wbv, wbl = p.now("p2wbv"), p.now("p2wbl")
                    for s_i, s in enumerate(slots):
                        psum = ps0 if state["evc"] % 2 == 0 else ps1
                        waits = [("w_ls", wls), ("w_lc", wlc), ("p2xg", xgn),
                                 ("p2wbv", wbv), ("p2wbl", wbl), ("p2rope", ropen),
                                 ("p2mr", mrn),
                                 ("p2ev", max(0, p.now("p2ev") - 1)),
                                 ("p2ea", max(0, p.now("p2ea") - 1))]
                        mmv = slot_mms("kv", bi, par, s_i, s_i, psum, waits)
                        if bi < 8:
                            evac_rope("kv", bi, s, s_i, psum, mmv)
                        else:
                            evac_v(bi, s, psum, mmv)
                    state["wlast"][par] = p.now("p2mm")
            own = [4 * j + 3 for j in range(8)]
            for i, s in enumerate(own):
                p.emit("sync", (lambda i=i, s=s: lambda e: e.dma_start(
                    out=xg[i][:], in_=xfm_d[s, :, :]))(),
                    waits=[("p2mm", p.now("p2mm"))], inc=("p2xg", 16))
            mrn = load_sweep_rows(own)
            for i, s in enumerate(own):
                slot_prologue(s, i)
            xgn = p.now("p2xg")
            ropen = p.now("p2rope")
            for kind in ("q", "ffn"):
                for bi in range(8):
                    par = bi % 2
                    wls, wlc = load_wblk(kind, bi, par, "p2mm")
                    wbar_compute(par, wls, wlc, "p2mm")
                    wbv = p.now("p2wbv")
                    if kind == "ffn":
                        p.emit("sync", (lambda bi=bi: lambda e: e.dma_start(
                            out=bias_blk[:], in_=bias_d[:, bi * 512:(bi + 1) * 512]))(),
                            waits=[("wb_b", 16), ("p2mm", p.now("p2mm"))], inc=("p2bl", 16))
                        p.emit("sync", (lambda bi=bi: lambda e: e.dma_start(
                            out=rm_blk[:], in_=rm_d[:, bi * 512:(bi + 1) * 512]))(),
                            waits=[("c1rsp", 128), ("p2ev", p.now("p2ev"))],
                            inc=("p2rml", 16))
                    bln = p.now("p2bl")
                    rmn = p.now("p2rml")
                    for j in range(8):
                        s = own[j]
                        psum = ps0 if state["evc"] % 2 == 0 else ps1
                        waits = [("w_ls", wls), ("w_lc", wlc), ("p2xg", xgn),
                                 ("p2wbv", wbv), ("p2rope", ropen), ("p2mr", mrn),
                                 ("p2bl", bln), ("p2rml", rmn),
                                 ("p2ev", max(0, p.now("p2ev") - 1)),
                                 ("p2ea", max(0, p.now("p2ea") - 1))]
                        mmv = slot_mms(kind, bi, par, j, j, psum, waits)
                        if kind == "q":
                            evac_rope("q", bi, j, j, psum, mmv)
                        else:
                            evac_ffn(bi, s, j, psum, mmv)
                    state["wlast"][par] = p.now("p2mm")

        SP2_TOTAL = 16 * (16 * MYS + 16 * NOWN)
        kfm = [arena_bf[:, 0:4096], arena_bf[:, 4096:8192]]
        qfm = [arena_bf[:, 8192:9216], arena_bf[:, 9216:10240]]
        vsb = [arena_bf[:, 10240:14368], arena_bf[:, 14368:18496]]
        expS = arena_bf[:, 18496:36928]
        mks = arena_bf[:, 36928:41024]
        y_sb = arena_bf[:, 41216:42240]
        pa_all = arena_bf[:, 42240:42320]
        paT_all = arena_bf[0:10, 42320:43344]
        ev8 = arena_bf[:, 43344:44368]
        easc_in = arena_f[:, 0:80]
        easc = arena_f[:, 128:208]
        fac = arena_f[:, 256:264]
        sofs = []
        off = 0
        for s in range(MYS):
            w = (8 - s // 4) * 128
            sofs.append((off, w))
            off += w

        def ph3():
            p.emit("gpsimd", lambda e: e.dma_start(
                out=mks.rearrange("p (s q) -> p s q", s=32),
                in_=amask.rearrange("(s k) q -> k s q", k=128)), inc=("p3mk", 16))
            for h in range(H):
                pr = h % 2
                guard = [("p3m", MYS * max(0, h - 1)), ("p3av0", 8 * max(0, h - 1)),
                         ("p3as", 8 * max(0, h - 1))]
                p.emit("sync", (lambda h=h, pr=pr: lambda e: e.dma_start(
                    out=kfm[pr][:], in_=k_d[:, h * 128:(h + 1) * 128], transpose=True))(),
                    waits=guard + [("p2sp", SP2_TOTAL)], inc=("p3kl", 16))
                p.emit("sync", (lambda h=h, pr=pr: lambda e: e.dma_start(
                    out=qfm[pr][:], in_=q_d[:, h * 128:(h + 1) * 128], transpose=True))(),
                    inc=("p3kl", 16))
                p.emit("vector", (lambda pr=pr: lambda e: e.memset(
                    vsb[pr].rearrange("p (s w) -> p s w", w=129)[:, :, 128:129], 1.0))(),
                    waits=[("p3av0", 8 * max(0, h - 1))], inc=("p3vm", 1))
                p.emit("gpsimd", (lambda h=h, pr=pr: lambda e: e.dma_start(
                    out=vsb[pr].rearrange("p (s w) -> p s w", w=129)[:, :, 0:128],
                    in_=v_d.rearrange("(s t) c -> t s c", t=128)[:, :, h * 128:(h + 1) * 128]))(),
                    waits=[("p3vm", p.now("p3vm")), ("p2sp", SP2_TOTAL)],
                    inc=("p3vl", 16))
                kln, vln = p.now("p3kl"), p.now("p3vl")
                for s in range(MYS):
                    o0, w = sofs[s]
                    nmm = (w + 511) // 512
                    for m_i in range(nmm):
                        n0 = m_i * 512
                        n1 = min(w, n0 + 512)
                        bank = ps0 if p.now("p3s") % 2 == 0 else ps1
                        qc0 = (s // 4) * 128 + n0
                        p.emit("tensor", (lambda s=s, pr=pr, bank=bank, n0=n0, n1=n1,
                                          qc0=qc0: lambda e: e.matmul(
                                              out=bank[:, 0:n1 - n0],
                                              lhsT=kfm[pr][:, s * 128:(s + 1) * 128],
                                              rhs=qfm[pr][:, qc0:qc0 + (n1 - n0)],
                                              start=True, stop=True))(),
                               waits=[("p3kl", kln), ("p3x", max(0, p.now("p3x") - 1))],
                               inc=("p3s", 1))
                        p.emit("scalar", (lambda bank=bank, o0=o0, n0=n0, n1=n1:
                                          lambda e: e.activation(
                                              out=expS[:, o0 + n0:o0 + n1],
                                              in_=bank[:, 0:n1 - n0], func=AF.Exp,
                                              scale=INV_SQRT_HD))(),
                               waits=[("p3s", p.now("p3s"))], inc=("p3x", 1))
                    p.emit("vector", (lambda s=s, o0=o0: lambda e: e.tensor_tensor(
                        out=expS[:, o0:o0 + 128], in0=expS[:, o0:o0 + 128],
                        in1=mks[:, s * 128:(s + 1) * 128], op=ALU.mult))(),
                        waits=[("p3x", p.now("p3x")), ("p3mk", 16)], inc=("p3m", 1))
                mn = p.now("p3m")
                for j in range(8):
                    ybank = ps2 if j % 2 == 0 else ps4
                    nkt = 4 * j + 4
                    p.emit("tensor", (lambda j=j, pr=pr, ybank=ybank, nkt=nkt: lambda e: [
                        e.matmul(out=ybank[:, 0:129],
                                 lhsT=expS[:, sofs[s][0] + (j - s // 4) * 128:
                                           sofs[s][0] + (j - s // 4) * 128 + 128],
                                 rhs=vsb[pr][:, s * 129:(s + 1) * 129],
                                 start=(s == 0), stop=(s == nkt - 1),
                                 skip_group_check=True)
                        for s in range(nkt)][-1])(),
                        waits=[("p3m", mn), ("p3vl", vln),
                               ("p3ye", max(0, p.now("p3ye") - 1))],
                        inc=("p3av0", 1))
                    ywaits = [("p3av0", p.now("p3av0"))]
                    if j == 0:
                        ywaits.append(("p3ya", max(0, h - 1)))
                    p.emit("vector", (lambda j=j, ybank=ybank: lambda e: [
                        e.reciprocal(out=stat[:, 7:8], in_=ybank[:, 128:129]),
                        e.tensor_scalar(out=y_sb[:, j * 128:(j + 1) * 128],
                                        in0=ybank[:, 0:128],
                                        scalar1=stat[:, 7:8], scalar2=None,
                                        op0=ALU.mult)][-1])(),
                        waits=ywaits, inc=("p3ye", 1))
                for j in range(8):
                    p.emit("tensor", (lambda j=j, pr=pr, h=h: lambda e: e.matmul(
                        out=ps5[:, j * 10:(j + 1) * 10],
                        lhsT=qfm[pr][:, j * 128:(j + 1) * 128],
                        rhs=ak_all[:, h * 10:(h + 1) * 10], start=True, stop=True,
                        skip_group_check=True))(),
                        waits=[("p3ax", p.now("p3ax"))], inc=("p3as", 1))
                p.emit("vector", (lambda: lambda e: ([
                    e.reduce_max(out=stat[:, j:j + 1], in_=ps5[:, j * 10:(j + 1) * 10],
                                 axis=AXX, negate=True) for j in range(8)] + [
                    e.tensor_scalar(out=easc_in[:, j * 10:(j + 1) * 10],
                                    in0=ps5[:, j * 10:(j + 1) * 10],
                                    scalar1=stat[:, j:j + 1], scalar2=INV_64,
                                    op0=ALU.add, op1=ALU.mult) for j in range(8)])[-1])(),
                    waits=[("p3as", p.now("p3as"))], inc=("p3adv", 1))
                p.emit("scalar", (lambda: lambda e: e.activation(
                    out=easc[:], in_=easc_in[:], func=AF.Exp))(),
                    waits=[("p3adv", p.now("p3adv"))], inc=("p3ax", 1))
                p.emit("vector", (lambda: lambda e: ([
                    e.reduce_sum(out=stat[:, 8 + j:9 + j], in_=easc[:, j * 10:(j + 1) * 10],
                                 axis=AXX) for j in range(8)] + [
                    e.reciprocal(out=fac[:], in_=stat[:, 8:16]),
                    e.tensor_scalar(out=fac[:], in0=fac[:], scalar1=gat_bc[:, 0:1],
                                    scalar2=None, op0=ALU.mult)] + [
                    e.tensor_scalar(out=pa_all[:, j * 10:(j + 1) * 10],
                                    in0=easc[:, j * 10:(j + 1) * 10],
                                    scalar1=fac[:, j:j + 1], scalar2=None,
                                    op0=ALU.mult) for j in range(8)])[-1])(),
                    waits=[("p3ax", p.now("p3ax"))], inc=("p3adv", 1))
                p.emit("tensor", (lambda: lambda e: [
                    e.transpose(out=ps7[0:10, j * 128:(j + 1) * 128],
                                in_=pa_all[:, j * 10:(j + 1) * 10], identity=idb[:])
                    for j in range(8)][-1])(),
                    waits=[("p3adv", p.now("p3adv"))], inc=("p3at", 1))
                p.emit("vector", (lambda: lambda e: e.tensor_copy(
                    out=paT_all, in_=ps7[0:10, 0:1024]))(),
                    waits=[("p3at", p.now("p3at"))], inc=("p3adv", 1))
                p.emit("tensor", (lambda h=h: lambda e: [
                    e.matmul(out=(ps6 if j < 4 else ps3)[:, (j % 4) * 128:(j % 4 + 1) * 128],
                             lhsT=paT_all[0:10, j * 128:(j + 1) * 128],
                             rhs=av_all[0:10, h * 128:(h + 1) * 128], start=True, stop=True,
                             skip_group_check=True) for j in range(8)][-1])(),
                    waits=[("p3adv", p.now("p3adv")), ("p3ya", max(0, h - 1))],
                    inc=("p3avm", 1))
                p.emit("vector", (lambda: lambda e: [
                    e.tensor_tensor(out=ev8[:, j * 128:(j + 1) * 128],
                                    in0=y_sb[:, j * 128:(j + 1) * 128],
                                    in1=(ps6 if j < 4 else ps3)[:, (j % 4) * 128:(j % 4 + 1) * 128],
                                    op=ALU.add) for j in range(8)][-1])(),
                    waits=[("p3avm", p.now("p3avm")), ("p3ye", p.now("p3ye")),
                           ("p3ysp", 16 * 8 * h)],
                    inc=("p3ya", 1))
                yan = p.now("p3ya")
                for j in range(8):
                    p.emit("gpsimd", (lambda j=j, h=h: lambda e: e.dma_start(
                        out=y_d[j * 128:(j + 1) * 128, h * 128:(h + 1) * 128],
                        in_=ev8[:, j * 128:(j + 1) * 128]))(),
                        waits=[("p3ya", yan)], inc=("p3ysp", 16))

        yfm = arena_bf[:, 0:32768]
        gsl = [arena_bf[:, 32768 + i * 512:32768 + (i + 1) * 512] for i in range(2)]
        xsl = [arena_f[:, 0:512], arena_f[:, 512:1024]]
        osl = [arena_f[:, 1024:1536], arena_f[:, 1536:2048]]

        def ph4():
            for cc in range(NCH):
                p.emit("sync", (lambda cc=cc: lambda e: e.dma_start(
                    out=yfm[:, cc * 1024:(cc + 1) * 1024],
                    in_=y_d[:, cc * 128:(cc + 1) * 128], transpose=True))(),
                    waits=[("p3ysp", 16 * 256)], inc=("p4ys", 16))
            yl = (p.now("p4ys"), p.now("p4yc"))
            state["wlast"] = {0: 0, 1: 0}
            state["adlast"] = {0: p.now("p2ad"), 1: p.now("p2ad")}
            for bi in range(8):
                par = bi % 2
                wls, wlc = load_wblk("proj", bi, par, "p4mm")
                for j in range(8):
                    psum = ps0 if j % 2 == 0 else ps1
                    pr = j % 2
                    p.emit("sync", (lambda j=j, bi=bi, pr=pr: lambda e: e.dma_start(
                        out=gsl[pr][:],
                        in_=g_d[j * 128:(j + 1) * 128, bi * 512:(bi + 1) * 512]))(),
                        waits=[("p4o", max(0, p.now("p4o") - 1)), ("p2sp", SP2_TOTAL)],
                        inc=("p4g", 16))
                    p.emit("scalar", (lambda j=j, bi=bi, pr=pr: lambda e: e.dma_start(
                        out=xsl[pr][:], in_=xp[(4 * j + 3) * 128:(4 * j + 4) * 128,
                                               bi * 512:(bi + 1) * 512]))(),
                        waits=[("p4o", max(0, p.now("p4o") - 1))], inc=("p4x", 16))
                    gn, xn_ = p.now("p4g"), p.now("p4x")
                    p.emit("tensor", (lambda j=j, par=par, psum=psum: lambda e: [
                        e.matmul(out=psum[:, :],
                                 lhsT=yfm[:, cc * 1024 + j * 128:cc * 1024 + (j + 1) * 128],
                                 rhs=wblk[par][:, cc * 512:(cc + 1) * 512],
                                 start=(cc == 0), stop=(cc == NCH - 1))
                        for cc in range(NCH)][-1])(),
                        waits=[("p4ys", yl[0]), ("p4yc", yl[1]), ("w_ls", wls),
                               ("w_lc", wlc), ("p4o", max(0, p.now("p4o") - 1))],
                        inc=("p4mm", 1))
                    p.emit("vector", (lambda psum=psum, pr=pr: lambda e: [
                        e.tensor_tensor(out=osl[pr][:], in0=psum[:, :], in1=gsl[pr][:],
                                        op=ALU.mult),
                        e.tensor_tensor(out=osl[pr][:], in0=osl[pr][:], in1=xsl[pr][:],
                                        op=ALU.add)][-1])(),
                        waits=[("p4mm", p.now("p4mm")), ("p4g", gn), ("p4x", xn_),
                               ("p4st", 16 * max(0, p.now("p4o") - 1))],
                        inc=("p4o", 1))
                    p.emit("gpsimd", (lambda j=j, bi=bi, pr=pr: lambda e: e.dma_start(
                        out=out[j * 128:(j + 1) * 128, bi * 512:(bi + 1) * 512],
                        in_=osl[pr][:]))(),
                        waits=[("p4o", p.now("p4o"))], inc=("p4st", 16))
                state["wlast"][par] = p.now("p4mm")
            p.emit("gpsimd", None, waits=[("p4st", p.now("p4st"))])

        def phdbg():
            def d(dst, src):
                p.emit("gpsimd", (lambda dst=dst, src=src: lambda e: e.dma_start(
                    out=dst, in_=src))(), inc=("dbgs", 16))
            d(dbg_rstd, rstd_all[:])
            d(dbg_qsel, qsel_all[:])
            d(dbg_mrow, mrow_d)
            d(dbg_sqv, sqv_d)
            d(dbg_rm, rm_d)
            d(dbg_xfm, xfm_d[3, :, :])
            d(dbg_k, k_d)
            d(dbg_v, v_d)
            d(dbg_q, q_d)
            d(dbg_g, g_d)
            d(dbg_y, y_d)
            d(dbg_ak, ak_all[:])
            d(dbg_av, av_all[:])
            d(dbg_srow, srow_bc[:])
            d(dbg_xh, xhat_d[3 * 128:5 * 128, :])
            d(dbg_xl, xlo_d[3 * 128:5 * 128, :])
            p.emit("gpsimd", None, waits=[("dbgs", p.now("dbgs"))])

        def ph1():
            for tb in range(16):
                ph1a_block(tb)
                if tb >= 1:
                    ph1b_block(tb - 1)
                emit_cast_chunks((40 * tb) // 16, (40 * (tb + 1)) // 16)
            ph1a_flush()
            ph1b_block(15)
            emit_cast_chunks(40, 40)
            ph1c()

        phases = [ph0, ph1, ph2, ph3, ph4]
        for i, phf in enumerate(phases):
            if i >= upto:
                break
            phf()
            if i < len(phases) - 1:
                p.barrier()
        if dbg:
            p.barrier()
            phdbg()

        handles = {}
        guards = []
        for name in sorted(p.counts.keys()):
            g = nc.semaphore(name)
            handles[name] = g.__enter__()
            guards.append(g)
        chain_guards = []
        chain_sems = {}
        for en in CHAINED:
            g = nc.semaphore("__chain_" + en)
            chain_sems[en] = g.__enter__()
            chain_guards.append(g)
        handles.update({"__chain_" + en: chain_sems[en] for en in CHAINED})

        _CHAIN_OPS = {"tensor_copy", "tensor_tensor", "tensor_scalar",
                      "scalar_tensor_tensor", "tensor_single_scalar",
                      "tensor_reduce", "reduce_max", "reduce_sum", "reciprocal",
                      "memset", "memzero", "activation", "sqrt", "copy", "mul",
                      "add", "sign", "square", "select", "tensor_add",
                      "tensor_sub", "tensor_mul", "tensor_max"}

        class Chained:
            def __init__(self, eng, sem):
                self._eng = eng
                self._sem = sem
                self._n = 0

            def __getattr__(self, name):
                f = getattr(self._eng, name)
                if name in _CHAIN_OPS:
                    def wrapped(*a, **kw):
                        if self._n > 0:
                            self._eng.wait_ge(self._sem, self._n)
                        ins = f(*a, **kw)
                        ins.then_inc(self._sem, 1)
                        self._n += 1
                        return ins
                    return wrapped
                return f

        try:
            def run_items(eng_name):
                def body(eng):
                    if eng_name in chain_sems:
                        eng_w = Chained(eng, chain_sems[eng_name])
                    else:
                        eng_w = eng
                    for fn, waits, inc in p.items[eng_name]:
                        for s, v in waits:
                            eng.wait_ge(handles[s], v)
                        if fn is not None:
                            last = fn(eng_w)
                            if isinstance(last, list):
                                last = last[-1]
                            if inc is not None:
                                last.then_inc(handles[inc[0]], inc[1])
                        elif inc is not None:
                            eng.sem_inc(handles[inc[0]], inc[1])
                return body
            block.sync(run_items("sync"))
            block.tensor(run_items("tensor"))
            block.vector(run_items("vector"))
            block.scalar(run_items("scalar"))
            block.gpsimd(run_items("gpsimd"))
        finally:
            for g in reversed(chain_guards):
                g.__exit__(None, None, None)
            for g in reversed(guards):
                g.__exit__(None, None, None)
    _stack.close()
    return nc



def _rope_tables():
    d = np.arange(0, 128, 2, dtype=np.float64) / 128.0
    theta = 1.0 / (10000.0 ** d)
    pos = np.arange(T, dtype=np.float64)
    idx = np.outer(pos, theta)
    return np.cos(idx).astype(np.float32), np.sin(idx).astype(np.float32)


def _core_order(lane):
    order = []
    for g in range(8):
        others = [t for t in range(4 * g, 4 * g + 4) if t != 4 * g + lane]
        order.extend(others + [4 * g + lane])
    return order


def kernel(L0_attn_x, L0_attn_w_attn, L0_attn_w_proj, L0_attn_adapter_wte,
           L0_attn_gating, L0_ffn_weight, L0_ffn_bias, L0_ffn_centroids,
           L0_ffn_assignments):
    x = np.ascontiguousarray(np.asarray(L0_attn_x, np.float32))
    cosf, sinf = _rope_tables()
    tri = np.triu(np.ones((128, 128), np.float32)).astype(ml_dtypes.bfloat16)
    ones_m = np.ones((128, 128), ml_dtypes.bfloat16)
    zero_m = np.zeros((128, 128), ml_dtypes.bfloat16)

    shared = {
        "w_attn": np.ascontiguousarray(np.asarray(L0_attn_w_attn, np.float32)),
        "w_proj": np.ascontiguousarray(np.asarray(L0_attn_w_proj, np.float32)),
        "w_ffn": np.ascontiguousarray(np.asarray(L0_ffn_weight, np.float32)),
        "ffn_bias": np.ascontiguousarray(np.asarray(L0_ffn_bias, np.float32))[None, :],
        "cents": np.ascontiguousarray(np.asarray(L0_ffn_centroids, np.float32)),
        "asg": np.ascontiguousarray(np.asarray(L0_ffn_assignments, np.int32))[None, :],
        "adapter": np.ascontiguousarray(np.asarray(L0_attn_adapter_wte, np.float32)),
        "gating": np.asarray(L0_attn_gating, np.float32).reshape(1, 1),
        "ident_bf": np.eye(128, dtype=ml_dtypes.bfloat16),
        "ident_f": np.eye(128, dtype=np.float32),
        "iota64": np.arange(64, dtype=np.float32)[:, None],
        "onescol_b": np.ones((128, 1), ml_dtypes.bfloat16),
        "onescol_f": np.ones((128, 1), np.float32),
        "onesrow_f": np.ones((1, 128), np.float32),
    }

    in_maps = []
    metas = []
    for c in range(8):
        lane, seq = c % 4, c // 4
        order = _core_order(lane)
        xt = x[seq].reshape(32, 128, C)
        xo = x[1 - seq].reshape(32, 128, C)
        xp_rows = np.concatenate([xt[order], xo], axis=0).reshape(NSLOT * 128, C)
        ctab = cosf.reshape(32, 128, 64)[order].reshape(MYS * 128, 64)
        stab = sinf.reshape(32, 128, 64)[order].reshape(MYS * 128, 64)
        masks = np.empty((32, 128, 128), ml_dtypes.bfloat16)
        for s in range(32):
            ktrue = order[s]
            town = 4 * (s // 4) + lane
            if ktrue < town:
                masks[s] = ones_m
            elif ktrue == town:
                masks[s] = tri
            else:
                masks[s] = zero_m
        m = dict(shared)
        m["xp"] = np.ascontiguousarray(xp_rows)
        m["cos_tab"] = np.ascontiguousarray(ctab)
        m["sin_tab"] = np.ascontiguousarray(stab)
        m["amask"] = np.ascontiguousarray(masks.reshape(MYS * 128, 128))
        in_maps.append(m)
        metas.append((seq, lane))

    nc = build_program()
    res = run_bass_kernel_spmd(nc, in_maps, core_ids=list(range(8)))

    out_full = np.empty((B, T, C), np.float32)
    for c in range(8):
        seq, lane = metas[c]
        shard = res.results[c]["out"]
        for j in range(8):
            t0 = (4 * j + lane) * 128
            out_full[seq, t0:t0 + 128, :] = shard[j * 128:(j + 1) * 128, :]
    return out_full


if __name__ == "__main__":
    build_program()
    print("build ok")
